# version 07 (final)
# baseline (speedup 1.0000x reference)
import os
import numpy as np
import concourse.bass as bass
import concourse.mybir as mybir
from concourse.bass_utils import run_bass_kernel_spmd

F32 = mybir.dt.float32
BF16 = mybir.dt.bfloat16
I32 = mybir.dt.int32
AF = mybir.ActivationFunctionType
ALU = mybir.AluOpType
AX = mybir.AxisListType

NCORES = 8
B, T, D = 16, 2048, 1024
SEQ_PER_CORE = B // NCORES
NTOK = SEQ_PER_CORE * T
NT128 = NTOK // 128
NT512 = NTOK // 512
NCH = D // 128
HEADS = 16
HD = 64
FF = 4096
FFC = FF // 128
MOE_E, MOE_I = 8, 512
QI = 2816
QIC = QI // 128
EPS = 1e-5

DEBUG = os.environ.get("KDBG", "")


def build_consts():
    import ml_dtypes
    c = {}
    pos = np.arange(T, dtype=np.float32)[:, None]
    i = np.arange(0, HD, 2, dtype=np.float32)[None, :]
    theta = pos / np.power(10000.0, i / HD)
    cos64 = np.concatenate([np.cos(theta), np.cos(theta)], axis=1).astype(np.float32)
    sin64 = np.concatenate([np.sin(theta), np.sin(theta)], axis=1).astype(np.float32)
    c["c_cosrep"] = np.tile(cos64, (1, HEADS)).astype(ml_dtypes.bfloat16)
    c["c_sinrep"] = np.tile(sin64, (1, HEADS)).astype(ml_dtypes.bfloat16)
    tri = (np.arange(128)[None, :] >= np.arange(128)[:, None]).astype(np.float32)
    masks = np.zeros((4, 128, 512), dtype=np.float32)
    for o in range(4):
        for j in range(4):
            if j > o:
                masks[o][:, j * 128:(j + 1) * 128] = 1.0
            elif j == o:
                masks[o][:, j * 128:(j + 1) * 128] = tri
    c["c_masks"] = masks.reshape(512, 512).astype(ml_dtypes.bfloat16)
    c["c_iden_bf"] = np.eye(128).astype(ml_dtypes.bfloat16)
    c["c_iden_f32"] = np.eye(128).astype(np.float32)
    esel = np.zeros((16, 16 * 128), dtype=np.float32)
    for h in range(16):
        esel[h, h * 128:(h + 1) * 128] = 1.0
    c["c_esel"] = esel
    hs = np.zeros((128, NCH * 16), dtype=np.float32)
    for cc in range(NCH):
        hs[0:64, cc * 16 + 2 * cc] = 1.0
        hs[64:128, cc * 16 + 2 * cc + 1] = 1.0
    c["c_headsum"] = hs.astype(ml_dtypes.bfloat16)
    c["c_edelta"] = np.tile((np.arange(8, dtype=np.float32) * 1e-6)[None, :], (128, 1))
    return c


class Sem:
    def __init__(self, name):
        self.h = None
        self.name = name
        self.n = 0


class Arena:
    def __init__(self, tensor, n, name):
        self.t, self.n, self.name = tensor, n, name
        self.off = 0
        self.marks = []

    def alloc(self, cols):
        assert self.off + cols <= self.n, f"{self.name} arena overflow {self.off}+{cols}>{self.n}"
        ap = self.t[:, self.off:self.off + cols]
        self.off += cols
        return ap

    def push(self):
        self.marks.append(self.off)

    def pop(self):
        self.off = self.marks.pop()


class KB:
    def __init__(self, nc):
        self.nc = nc
        self.ops = []
        self.serial = False
        self.s_all = None

    def _ser_pre(self, eng):
        if self.serial and (self.serial is True or eng in self.serial):
            v = self.s_all.n
            self.ops.append((eng, lambda e, s=self.s_all, v=v: e.wait_ge(s.h, v)))

    def _ser_post(self, eng):
        if self.serial and (self.serial is True or eng in self.serial):
            self.s_all.n += 1
            self.ops.append((eng, lambda e: e.drain()))
            self.ops.append((eng, lambda e, s=self.s_all: e.nop().then_inc(s.h, 1)))

    def on(self, eng, fn):
        self._ser_pre(eng)
        self.ops.append((eng, fn))
        self._ser_post(eng)

    def on_raw(self, eng, fn):
        self.ops.append((eng, fn))

    def incop(self, eng, sem, fn, n=1):
        self._ser_pre(eng)
        self.ops.append((eng, fn))
        self.ops.append((eng, lambda e: e.drain()))
        sem.n += n
        self.ops.append((eng, lambda e, s=sem, n=n: e.nop().then_inc(s.h, n)))
        self._ser_post(eng)

    def mm(self, sem, fn):
        sem.n += 1
        self.on("tensor", lambda e, s=sem, fn=fn: fn(e).then_inc(s.h, 1))

    def mmq(self, fn):
        self.on("tensor", fn)

    def dma(self, eng, out, in_, sem):
        sem.n += 16
        self._ser_pre(eng)
        self.ops.append((eng, lambda e, out=out, in_=in_, s=sem: e.dma_start(out=out, in_=in_).then_inc(s.h, 16)))
        if self.serial and (self.serial is True or eng in self.serial):
            v = sem.n
            self.ops.append((eng, lambda e, s=sem, v=v: e.wait_ge(s.h, v)))
        self._ser_post(eng)

    def dma_tr(self, eng, out, in_, sem):
        sem.n += 16
        self._ser_pre(eng)
        self.ops.append((eng, lambda e, out=out, in_=in_, s=sem: e.dma_start_transpose(out, in_).then_inc(s.h, 16)))
        if self.serial and (self.serial is True or eng in self.serial):
            v = sem.n
            self.ops.append((eng, lambda e, s=sem, v=v: e.wait_ge(s.h, v)))
        self._ser_post(eng)

    def wait(self, eng, sem, val=None):
        v = sem.n if val is None else val
        self.on_raw(eng, lambda e, s=sem, v=v: e.wait_ge(s.h, v))

    def run(self, eng_name, eng):
        for name, fn in self.ops:
            if name == eng_name:
                fn(eng)


def ts(i, n):
    return slice(i * n, (i + 1) * n)


def build_nc(debug=""):
    nc = bass.Bass("TRN2", target_bir_lowering=False, debug=False)

    def EI(name, shape, dtype=F32):
        return nc.dram_tensor(name, shape, dtype, kind="ExternalInput").ap()

    x0 = EI("x0", [NTOK, D])
    w_qkv0 = EI("L0_attn_w_qkv", [D, 3 * D])
    w_out0 = EI("L0_attn_w_out", [D, D])
    gate_w = EI("L0_ffn_gate_w", [MOE_E, D])
    corr_b = EI("L0_ffn_correction_bias", [MOE_E])
    Wg = EI("L0_ffn_Wg", [MOE_E, D, MOE_I])
    Wu = EI("L0_ffn_Wu", [MOE_E, D, MOE_I])
    Wd = EI("L0_ffn_Wd", [MOE_E, MOE_I, D])
    Wg_s = EI("L0_ffn_Wg_s", [D, MOE_I])
    Wu_s = EI("L0_ffn_Wu_s", [D, MOE_I])
    Wd_s = EI("L0_ffn_Wd_s", [MOE_I, D])
    qkv_w1 = EI("L1_attn_qkv_w", [D, 3 * D])
    qkv_b1 = EI("L1_attn_qkv_b", [3 * D])
    out_w1 = EI("L1_attn_out_w", [D, D])
    out_b1 = EI("L1_attn_out_b", [D])
    g_attn = EI("L1_attn_g_attn", [D])
    b_attn = EI("L1_attn_b_attn", [D])
    g1v = EI("L1_attn_g1", [D])
    b1v = EI("L1_attn_b1", [D])
    g2v = EI("L1_attn_g2", [D])
    b2v = EI("L1_attn_b2", [D])
    w_ctx = EI("L1_attn_w_ctx", [D, D])
    w_x = EI("L1_attn_w_x", [D, D])
    b_inl = EI("L1_attn_b_inl", [D])
    ff_w1 = EI("L1_attn_ff_w1", [D, FF])
    ff_b1 = EI("L1_attn_ff_b1", [FF])
    ff_w2 = EI("L1_attn_ff_w2", [FF, D])
    ff_b2 = EI("L1_attn_ff_b2", [D])
    clip_x = EI("L1_ffn_clip_x", [1])
    clip_down = EI("L1_ffn_clip_down", [1])
    w_gate = EI("L1_ffn_w_gate", [QI, D])
    w_up = EI("L1_ffn_w_up", [QI, D])
    w_down = EI("L1_ffn_w_down", [D, QI])
    c_cos = EI("c_cosrep", [T, 1024], BF16)
    c_sin = EI("c_sinrep", [T, 1024], BF16)
    c_masks = EI("c_masks", [512, 512], BF16)
    c_iden_bf = EI("c_iden_bf", [128, 128], BF16)
    c_iden_f32 = EI("c_iden_f32", [128, 128], F32)
    c_esel = EI("c_esel", [16, 16 * 128], F32)
    c_headsum = EI("c_headsum", [128, NCH * 16], BF16)
    c_edelta = EI("c_edelta", [128, 8], F32)

    out = nc.dram_tensor("out", [NTOK, D], F32, kind="ExternalOutput").ap()

    def DS(name, shape, dtype):
        return nc.dram_tensor(name, shape, dtype).ap()

    XN0FM = DS("xn0fm", [NT128, 128, D], BF16)
    KFM = DS("kfm", [NT128, 128, D], BF16)
    VAUG = DS("vaug", [NT128, 128, 16 * 65], BF16)
    QFM = DS("qfm", [NT128, 128, D], BF16)
    S0 = DS("s0", [NTOK, D], F32)
    X1 = DS("x1", [NTOK, D], F32)
    XN1 = DS("xn1", [NTOK, D], F32)
    AFM = DS("afm", [NT128, 128, D], BF16)
    CTXFM = DS("ctxfm", [NT128, 128, D], BF16)
    Y1 = DS("y1", [NTOK, D], F32)
    Y2 = DS("y2", [NTOK, D], F32)
    Y2FM = DS("y2fm", [NT128, 128, D], BF16)
    HFM = DS("hfm", [NT512, 128, FFC * 512], BF16)
    S1 = DS("s1", [NTOK, D], F32)
    HPRE = DS("hpre", [NT512, 128, QIC * 512], F32)
    WQ1S = DS("wq1s", [NCH, 128, 3 * D], BF16)
    FFW1S = DS("ffw1s", [NCH, 128, FF], BF16)
    B1R_D = DS("b1r_d", [1, 3 * D], BF16)
    OUTB_D = DS("outb_d", [1, D], BF16)
    WQTG = DS("wqtg", [NCH, 128, QIC * 128], BF16)
    WQTU = DS("wqtu", [NCH, 128, QIC * 128], BF16)
    WDT = DS("wdt", [QIC, 128, NCH * 128], BF16)
    CCIN = nc.dram_tensor("ccin", [1, 8], F32)
    CCOUT = nc.dram_tensor("ccout", [1, 8], F32)
    CCIN2 = nc.dram_tensor("ccin2", [1, 8], F32)
    CCOUT2 = nc.dram_tensor("ccout2", [1, 8], F32)

    dbg = {}
    for name in [n for n in debug.split(",") if n]:
        key = "d_" + name
        dt_ = BF16 if name.endswith("fm") else F32
        dbg[key] = nc.dram_tensor(key, [NTOK, D], dt_, kind="ExternalOutput").ap()

    kb = KB(nc)
    import contextlib
    stack = contextlib.ExitStack()

    AB_N, AF_N, ASM_N = 69632, 10240, 1024
    AB = stack.enter_context(nc.sbuf_tensor("ab", [128, AB_N], BF16))
    AFp = stack.enter_context(nc.sbuf_tensor("af", [128, AF_N], F32))
    ASm = stack.enter_context(nc.sbuf_tensor("asm", [128, ASM_N], F32))
    AI = stack.enter_context(nc.sbuf_tensor("ai", [128, 1024], I32))
    arena_bf = Arena(AB, AB_N, "bf")
    arena_f = Arena(AFp, AF_N, "f32")
    arena_s = Arena(ASm, ASM_N, "sm")

    PS = [stack.enter_context(nc.psum_tensor(f"ps{i}", [128, 512], F32)) for i in range(7)]
    PTR = stack.enter_context(nc.psum_tensor("ptr", [128, 256], BF16))

    semnames = ["ldw", "xa", "xb2", "wa", "wb", "ha", "hb", "st", "mm", "ev",
                "dv", "gp", "xb", "cc", "all", "ga", "gb", "gc"]
    sems = {n: Sem(n) for n in semnames}
    for s in sems.values():
        s.h = stack.enter_context(nc.semaphore(s.name))
    kb.s_all = sems["all"]
    _ks = os.environ.get("KSERIAL", "0")
    kb.serial = True if _ks == "1" else (set(_ks.split(",")) if _ks not in ("", "0") else False)
    S_ldw = sems["ldw"]
    S_x2 = [sems["xa"], sems["xb2"]]
    S_w2 = [sems["wa"], sems["wb"]]
    S_h2 = [sems["ha"], sems["hb"]]
    S_st = sems["st"]; S_mm = sems["mm"]; S_ev = sems["ev"]
    S_ga = sems["ga"]; S_gb = sems["gb"]; S_gc = sems["gc"]
    S_dv = sems["dv"]; S_gp = sems["gp"]; S_xb = sems["xb"]; S_cc = sems["cc"]

    ALL_ENG = ["sync", "tensor", "scalar", "vector", "gpsimd"]
    BAR_SEMS = ["ldw", "xa", "xb2", "wa", "wb", "ha", "hb", "st", "mm", "ev",
                "dv", "gp", "xb", "ga", "gb", "gc", "all"]

    def phase_barrier():
        for eng in ALL_ENG:
            for sn in BAR_SEMS:
                s = sems[sn]
                if s.n > 0:
                    kb.wait(eng, s, s.n)

    ones = arena_s.alloc(512)
    iden_f32_sb = arena_s.alloc(128)
    ccst = arena_s.alloc(8)
    iden_bf = arena_bf.alloc(128)
    onesb = arena_bf.alloc(512)
    masks_sb = arena_bf.alloc(4 * 512)
    esel_sb = arena_f.alloc(2048)
    hsum_sb = arena_bf.alloc(NCH * 16)
    edelta_sb = arena_s.alloc(8)
    gcols = arena_f.alloc(48)
    b1acol = arena_f.alloc(32)
    binl_r = arena_bf.alloc(1024)
    ffb2_r = arena_bf.alloc(1024)
    qscal = arena_s.alloc(8)
    swg = arena_f.alloc(QIC)
    swu = arena_f.alloc(QIC)
    swd = arena_f.alloc(8)
    sgx = arena_f.alloc(QIC)
    sux = arena_f.alloc(QIC)
    sdx = arena_f.alloc(8)

    kb.on("gpsimd", lambda e: e.memset(ones[:, :], 1.0))
    kb.on("gpsimd", lambda e: e.memset(onesb[:, :], 1.0))
    kb.on("gpsimd", lambda e: e.drain())
    kb.on_raw("gpsimd", lambda e: e.nop().then_inc(S_gp.h, 1)); S_gp.n += 1
    kb.dma("sync", iden_bf, c_iden_bf, S_ldw)
    kb.dma("sync", iden_f32_sb, c_iden_f32, S_ldw)
    for o_ in range(4):
        kb.dma("sync", masks_sb[:, ts(o_, 512)], c_masks[ts(o_, 128), :], S_ldw)
    kb.dma("sync", esel_sb[0:16, :], c_esel, S_ldw)
    kb.dma("sync", hsum_sb, c_headsum, S_ldw)
    kb.dma("sync", edelta_sb, c_edelta, S_ldw)
    init_ldw = S_ldw.n
    for eng in ["tensor", "scalar", "vector"]:
        kb.wait(eng, S_ldw, init_ldw)
        kb.wait(eng, S_gp, S_gp.n)

    def emit_ln(xtile, stat, gate_scalar_all=True):
        st12, mv = stat[:, 0:12], stat[:, 12:14]
        veps, sq_, rs_, nmr = stat[:, 14:15], stat[:, 15:16], stat[:, 16:17], stat[:, 17:18]
        if gate_scalar_all:
            kb.wait("vector", S_ev)
        kb.on("vector", lambda e, o=st12[:, 0:6], i=xtile[:, 0:512]: e.bn_stats(o, i))
        kb.on("vector", lambda e, o=st12[:, 6:12], i=xtile[:, 512:1024]: e.bn_stats(o, i))
        kb.on("vector", lambda e: e.drain())
        kb.on("vector", lambda e, o=mv, i=st12: e.bn_aggr(o, i))
        kb.on("vector", lambda e: e.drain())
        kb.incop("vector", S_dv, lambda e, o=veps, i=mv:
                 e.tensor_scalar(o, i[:, 1:2], EPS, 0.0, ALU.add, ALU.add))
        kb.wait("scalar", S_dv)
        kb.incop("scalar", S_ev, lambda e, o=sq_, i=veps:
                 e.activation(o, i, AF.Sqrt, bias=0.0, scale=1.0))
        kb.wait("vector", S_ev)
        kb.on("vector", lambda e, o=rs_, i=sq_: e.reciprocal(o, i))
        kb.on("vector", lambda e: e.drain())
        kb.incop("vector", S_dv, lambda e, o=nmr, i=mv, r=rs_:
                 e.tensor_scalar(o, i[:, 0:1], r[:, 0:1], -1.0, ALU.mult, ALU.mult))
        kb.wait("scalar", S_dv)
        return rs_, nmr, mv

    def quant_chain(dst_bf, src_f32, rs_ap, width, tmpa, tmpsgn):
        kb.wait("scalar", S_dv)
        kb.incop("scalar", S_ev, lambda e, o=tmpa[:, 0:width], i=src_f32, r=rs_ap:
                 e.activation(o, i, AF.Abs, bias=0.0, scale=r))
        kb.incop("scalar", S_ev, lambda e, o=tmpsgn[:, 0:width], i=src_f32:
                 e.activation(o, i, AF.Sign, bias=0.0, scale=1.0))
        kb.wait("vector", S_ev)
        off = 0
        while off < width:
            w = min(1024, width - off)
            kb.on("vector", lambda e, o=AI[:, 0:w], i=tmpa[:, off:off + w]:
                  e.tensor_scalar(o, i, 128.49, 0.0, ALU.min, ALU.max))
            kb.on("vector", lambda e: e.drain())
            kb.on("vector", lambda e, o=tmpa[:, off:off + w], i=AI[:, 0:w]:
                  e.tensor_copy(o, i))
            kb.on("vector", lambda e: e.drain())
            off += w
        kb.on("vector", lambda e, o=tmpa[:, 0:width], a=tmpa[:, 0:width], b=tmpsgn[:, 0:width]:
              e.tensor_tensor(o, a, b, ALU.mult))
        kb.on("vector", lambda e: e.drain())
        kb.incop("vector", S_dv, lambda e, o=dst_bf, i=tmpa[:, 0:width]:
                 e.tensor_scalar(o, i, 127.0, -128.0, ALU.min, ALU.max))

    def phase_p0():
        ph_serial("p0")
        phase_barrier()
        arena_bf.push(); arena_f.push(); arena_s.push()
        xt = [arena_f.alloc(1024) for _ in range(2)]
        xnt = [arena_bf.alloc(1024) for _ in range(2)]
        xfm = [arena_bf.alloc(1024) for _ in range(2)]
        stat2 = [arena_f.alloc(18) for _ in range(2)]
        xb_done = []
        for t in range(NT128):
            bi = t % 2
            if t >= 2:
                kb.wait("sync", S_xb)
            kb.dma("sync", xt[bi], x0[ts(t, 128), :], S_x2[bi])
            kb.wait("vector", S_x2[bi])
            rs_, nmr, _ = emit_ln(xt[bi], stat2[bi])
            if t >= 2:
                kb.wait("scalar", S_xb, xb_done[t - 2])
            kb.incop("scalar", S_ev, lambda e, o=xnt[bi], i=xt[bi], r=rs_, b=nmr:
                     e.activation(o, i, AF.Identity, bias=b[:, 0:1], scale=r[:, 0:1]))
            kb.wait("sync", S_ev)
            for c in range(NCH):
                kb.dma_tr("sync", xfm[bi][:, ts(c, 128)], xnt[bi][:, ts(c, 128)], S_xb)
            xb_done.append(S_xb.n)
            kb.wait("sync", S_xb)
            kb.dma("sync", XN0FM[t, :, :], xfm[bi], S_xb)
        kb.wait("sync", S_xb)
        arena_bf.pop(); arena_f.pop(); arena_s.pop()

    KPH = int(os.environ.get("KPHASES", "99"))
    KSERP = set(filter(None, os.environ.get(
        "KSERP", "-").split(",")))
    _base_serial = kb.serial

    def ph_serial(name):
        kb.serial = True if (name in KSERP) else _base_serial

    def out_stub():
        arena_f.push()
        t0_ = arena_f.alloc(1024)
        phase_barrier()
        for t in range(NT128):
            kb.dma("sync", t0_, x0[ts(t, 128), :], S_x2[0])
            kb.wait("sync", S_x2[0])
            kb.dma("sync", out[ts(t, 128), :], t0_, S_st)
            kb.wait("sync", S_st)
        arena_f.pop()

    phase_p0()

    def phase_attn(layer):
        ph_serial("attn%d" % layer)
        phase_barrier()
        arena_bf.push(); arena_f.push(); arena_s.push()
        XIN = XN0FM if layer == 0 else AFM
        qn2 = arena_f.alloc(2048)
        kn2 = arena_f.alloc(2048)
        nstat = arena_f.alloc(16)
        mb_sb = arena_f.alloc(64)
        norm_fin = [0]
        b1r = arena_bf.alloc(3 * D) if layer == 1 else None
        outb_r = arena_bf.alloc(D) if layer == 1 else None
        wout_sb = arena_bf.alloc(NCH * D)
        WOUT = w_out0 if layer == 0 else out_w1
        kb.wait("gpsimd", S_mm)
        for c in range(NCH):
            kb.dma("gpsimd", wout_sb[:, ts(c, D)], WOUT[ts(c, 128), :], S_ga)
        if layer == 1:
            kb.dma("sync", b1r[0:1, :], B1R_D[0:1, :], S_ldw)
            kb.dma("sync", outb_r[0:1, :], OUTB_D[0:1, :], S_ldw)

        for s in range(SEQ_PER_CORE):
            arena_bf.push(); arena_f.push(); arena_s.push()
            wq = arena_bf.alloc(NCH * 3 * D)
            xnfm = [arena_bf.alloc(1024) for _ in range(2)]
            if layer == 0:
                costl = [arena_bf.alloc(1024) for _ in range(2)]
                sintl = [arena_bf.alloc(1024) for _ in range(2)]
            qkst = [arena_bf.alloc(2048) for _ in range(2)]
            if layer == 0:
                rotst = [arena_bf.alloc(2048) for _ in range(2)]
                rtmp = arena_bf.alloc(1024)
            vst = [arena_bf.alloc(1024) for _ in range(2)]
            fmst = [arena_bf.alloc(1024) for _ in range(2)]
            vaugst = [arena_bf.alloc(16 * 65) for _ in range(2)]
            sqst = arena_bf.alloc(1024)
            kb.wait("gpsimd", S_mm)
            for cch in range(NCH):
                if layer == 0:
                    kb.dma("gpsimd", wq[:, ts(cch, 3 * D)], w_qkv0[ts(cch, 128), :], S_ga)
                else:
                    kb.dma("gpsimd", wq[:, ts(cch, 3 * D)], WQ1S[cch, :, :], S_ga)
            kb.wait("tensor", S_ga)
            kb.wait("tensor", S_ldw)
            evac_done = []
            for t in range(16):
                g = s * 16 + t
                bi = t % 2
                if t >= 2:
                    kb.wait("sync", S_ev)
                    kb.wait("sync", S_dv)
                kb.dma("sync", xnfm[bi], XIN[g, :, :], S_x2[bi])
                if layer == 0:
                    kb.dma("sync", costl[bi], c_cos[ts(t, 128), :], S_x2[bi])
                    kb.dma("sync", sintl[bi], c_sin[ts(t, 128), :], S_x2[bi])
                kb.wait("tensor", S_x2[bi])
                for n in range(6):
                    pb = PS[n % 2]
                    if n >= 2:
                        kb.wait("tensor", S_ev, evac_done[-2])
                    for cch in range(NCH):
                        last = (cch == NCH - 1) and layer == 0
                        if last:
                            kb.mm(S_mm, lambda e, o=pb[:, 0:512], l=xnfm[bi][:, ts(cch, 128)],
                                  r=wq[:, cch * 3 * D + n * 512: cch * 3 * D + (n + 1) * 512],
                                  st=(cch == 0): e.matmul(o, l, r, start=st, stop=True))
                        else:
                            kb.mmq(lambda e, o=pb[:, 0:512], l=xnfm[bi][:, ts(cch, 128)],
                                   r=wq[:, cch * 3 * D + n * 512: cch * 3 * D + (n + 1) * 512],
                                   st=(cch == 0), sp=False: e.matmul(o, l, r, start=st, stop=sp))
                    if layer == 1:
                        kb.mm(S_mm, lambda e, o=pb[:, 0:512], l=onesb[0:1, 0:128],
                              r=b1r[0:1, ts(n, 512)]: e.matmul(o, l, r, start=False, stop=True))
                    kb.wait("scalar", S_mm)
                    if n < 4:
                        dst = qkst[bi][:, ts(n, 512)]
                    else:
                        dst = vst[bi][:, ts(n - 4, 512)]
                    kb.incop("scalar", S_ev, lambda e, o=dst, i=pb[:, 0:512]:
                             e.activation(o, i, AF.Identity, bias=0.0, scale=1.0))
                    evac_done.append(S_ev.n)
                if layer == 0:
                    kb.wait("vector", S_ev)
                    kb.wait("vector", S_x2[bi])
                    for half in range(2):
                        qf = qkst[bi][:, half * 1024: half * 1024 + 1024]
                        rf = rotst[bi][:, half * 1024: half * 1024 + 1024]
                        r_ = rf.rearrange("p (h x) -> p h x", x=64)
                        tv3 = rtmp[:, 0:1024].rearrange("p (h x) -> p h x", x=64)
                        kb.on("vector", lambda e, o=rf, a=qf, b=costl[bi]:
                              e.tensor_tensor(o, a, b, ALU.mult))
                        kb.on("vector", lambda e, o=rtmp[:, 0:1024], a=qf, b=sintl[bi]:
                              e.tensor_tensor(o, a, b, ALU.mult))
                        kb.on("vector", lambda e: e.drain())
                        kb.on("vector", lambda e, o=r_[:, :, 0:32], a=r_[:, :, 0:32], b=tv3[:, :, 32:64]:
                              e.tensor_tensor(o, a, b, ALU.subtract))
                        kb.on("vector", lambda e, o=r_[:, :, 32:64], a=r_[:, :, 32:64], b=tv3[:, :, 0:32]:
                              e.tensor_tensor(o, a, b, ALU.add))
                        kb.on("vector", lambda e: e.drain())
                    rot = rotst[bi]
                else:
                    rot = qkst[bi]
                kb.wait("vector", S_ev)
                kb.wait("vector", S_xb)
                vdst = vaugst[bi].rearrange("p (h x) -> p h x", x=65)[:, :, 0:64]
                vsrc = vst[bi].rearrange("p (h x) -> p h x", x=64)
                kb.on("vector", lambda e, o=vdst, i=vsrc: e.tensor_copy(o, i))
                vones = vaugst[bi].rearrange("p (h x) -> p h x", x=65)[:, :, 64:65]
                kb.incop("vector", S_dv, lambda e, o=vones: e.memset(o, 1.0))
                kb.wait("sync", S_dv)
                kb.dma("sync", VAUG[g, :, :], vaugst[bi], S_xb)
                for c in range(NCH):
                    kb.dma_tr("sync", fmst[bi][:, ts(c, 128)], rot[:, ts(c, 128)], S_xb)
                xb_q = S_xb.n
                kb.wait("sync", S_xb)
                kb.dma("sync", QFM[g, :, :], fmst[bi], S_xb)
                kb.wait("scalar", S_xb, xb_q)
                kb.incop("scalar", S_ev, lambda e, o=sqst, i=fmst[bi]:
                         e.activation(o, i, AF.Square, bias=0.0, scale=1.0))
                kb.wait("tensor", S_ev)
                for c in range(NCH):
                    if c == NCH - 1:
                        kb.mm(S_mm, lambda e, o=PS[2][0:16, 0:128], l=hsum_sb[:, ts(c, 16)],
                              r=sqst[:, ts(c, 128)], st=(c == 0): e.matmul(o, l, r, start=st, stop=True))
                    else:
                        kb.mmq(lambda e, o=PS[2][0:16, 0:128], l=hsum_sb[:, ts(c, 16)],
                               r=sqst[:, ts(c, 128)], st=(c == 0): e.matmul(o, l, r, start=st, stop=False))
                kb.wait("scalar", S_mm)
                if s > 0 and t == 0:
                    kb.wait("scalar", S_dv, norm_fin[0])
                kb.incop("scalar", S_ev, lambda e, o=qn2[0:16, ts(t, 128)], i=PS[2][0:16, 0:128]:
                         e.activation(o, i, AF.Identity, bias=0.0, scale=1.0))
                kb.wait("sync", S_ev)
                kb.wait("sync", S_xb)
                for c in range(NCH):
                    kb.dma_tr("sync", fmst[bi][:, ts(c, 128)],
                              rot[:, 1024 + c * 128: 1024 + (c + 1) * 128], S_xb)
                xb_k = S_xb.n
                kb.wait("sync", S_xb)
                kb.dma("sync", KFM[g, :, :], fmst[bi], S_xb)
                kb.wait("scalar", S_xb, xb_k)
                kb.incop("scalar", S_ev, lambda e, o=sqst, i=fmst[bi]:
                         e.activation(o, i, AF.Square, bias=0.0, scale=1.0))
                kb.wait("tensor", S_ev)
                for c in range(NCH):
                    if c == NCH - 1:
                        kb.mm(S_mm, lambda e, o=PS[2][0:16, 128:256], l=hsum_sb[:, ts(c, 16)],
                              r=sqst[:, ts(c, 128)], st=(c == 0): e.matmul(o, l, r, start=st, stop=True))
                    else:
                        kb.mmq(lambda e, o=PS[2][0:16, 128:256], l=hsum_sb[:, ts(c, 16)],
                               r=sqst[:, ts(c, 128)], st=(c == 0): e.matmul(o, l, r, start=st, stop=False))
                kb.wait("scalar", S_mm)
                kb.incop("scalar", S_ev, lambda e, o=kn2[0:16, ts(t, 128)], i=PS[2][0:16, 128:256]:
                         e.activation(o, i, AF.Identity, bias=0.0, scale=1.0))
            arena_bf.pop(); arena_f.pop(); arena_s.pop()

            kb.wait("vector", S_ev)
            for g4 in range(4):
                kb.on("vector", lambda e, o=nstat[0:16, g4:g4 + 1], i=qn2[0:16, ts(g4, 512)]:
                      e.tensor_reduce(o, i, AX.X, ALU.max))
            kb.on("vector", lambda e, o=nstat[0:16, 4:5], i=kn2[0:16, 0:2048]:
                  e.tensor_reduce(o, i, AX.X, ALU.max))
            kb.on("vector", lambda e: e.drain())
            kb.incop("vector", S_dv, lambda e: e.nop())
            kb.wait("scalar", S_dv)
            kb.incop("scalar", S_ev, lambda e, o=nstat[0:16, 5:10], i=nstat[0:16, 0:5]:
                     e.activation(o, i, AF.Sqrt, bias=0.0, scale=1.0))
            kb.wait("vector", S_ev)
            kb.incop("vector", S_dv, lambda e, o=nstat[0:16, 10:14], i=nstat[0:16, 5:9],
                     k=nstat[0:16, 9:10]:
                     e.tensor_scalar(o, i, k, -0.125, ALU.mult, ALU.mult))
            norm_fin[0] = S_dv.n
            kb.wait("tensor", S_dv)
            for h in range(16):
                if h == 15:
                    kb.mm(S_mm, lambda e, o=PS[2][:, ts(h, 4)], l=esel_sb[0:16, ts(h, 128)],
                          r=nstat[0:16, 10:14]: e.matmul(o, l, r, start=True, stop=True))
                else:
                    kb.mmq(lambda e, o=PS[2][:, ts(h, 4)], l=esel_sb[0:16, ts(h, 128)],
                           r=nstat[0:16, 10:14]: e.matmul(o, l, r, start=True, stop=True))
            kb.wait("scalar", S_mm)
            kb.incop("scalar", S_ev, lambda e, o=mb_sb[:, 0:64], i=PS[2][:, 0:64]:
                     e.activation(o, i, AF.Identity, bias=0.0, scale=1.0))

            arena_bf.push(); arena_f.push(); arena_s.push()
            Khalf = arena_bf.alloc(16 * 1024)
            Vhalf = arena_bf.alloc(16 * 1040)
            qsl = arena_bf.alloc(8 * 512)
            pt_b = [arena_bf.alloc(512) for _ in range(4)]
            o_half = arena_bf.alloc(512)
            o_tok = arena_bf.alloc(4096)
            o_fm = arena_bf.alloc(4096)
            rlb = arena_f.alloc(512)
            lnl = arena_f.alloc(512)
            xn1t = arena_f.alloc(1024) if layer == 1 else None
            s0t = arena_f.alloc(1024)
            ctxt = arena_bf.alloc(1024) if layer == 1 else None
            ctxfmst = arena_bf.alloc(1024) if layer == 1 else None
            kb.wait("sync", S_mm)
            kb.wait("sync", S_xb)
            kb.dma("sync", Khalf, KFM[s * 16: s * 16 + 16, :, :].rearrange("t p x -> p t x"),
                   S_w2[0])
            kb.dma("sync", Vhalf, VAUG[s * 16: s * 16 + 16, :, :].rearrange("t p x -> p t x"),
                   S_w2[0])
            kb.wait("tensor", S_w2[0])
            for g4 in range(4):
                kb.wait("sync", S_mm)
                kb.dma("sync", qsl,
                       QFM[s * 16 + g4 * 4: s * 16 + g4 * 4 + 4, :, :]
                       .rearrange("t p (c x) -> p c t x", x=128), S_x2[g4 % 2])
                kb.wait("tensor", S_x2[g4 % 2])
                nkt = 4 * g4 + 4
                if True:
                    for hp in range(8):
                        hA, hB = 2 * hp, 2 * hp + 1
                        poA, poB = PS[5], PS[6]
                        kb.wait("tensor", S_ev)
                        kb.wait("tensor", S_dv)
                        exp_done = []
                        for kt in range(0, nkt):
                            kl = kt
                            pa = PS[3] if kt % 2 == 0 else PS[0]
                            pb2 = PS[4] if kt % 2 == 0 else PS[1]
                            if len(exp_done) >= 2:
                                kb.wait("tensor", S_ev, exp_done[-2])
                            for (hh, pp) in ((hA, pa), (hB, pb2)):
                                base = (hh % 2) * 64
                                cch = hh // 2
                                kb.mm(S_mm, lambda e, o=pp[:, 0:512],
                                      l=Khalf[base:base + 64, kl * 1024 + cch * 128: kl * 1024 + (cch + 1) * 128],
                                      r=qsl[base:base + 64, cch * 512: (cch + 1) * 512]:
                                      e.matmul(o, l, r, start=True, stop=True))
                            kb.wait("scalar", S_mm)
                            for (idx, (hh, pp)) in enumerate(((hA, pa), (hB, pb2))):
                                pt = pt_b[(kt % 2) * 2 + idx]
                                kb.incop("scalar", S_ev, lambda e, o=pt, i=pp[:, 0:512],
                                         b=mb_sb[:, hh * 4 + g4: hh * 4 + g4 + 1]:
                                         e.activation(o, i, AF.Exp, bias=b, scale=0.125))
                            if kt >= 4 * g4:
                                o_ = kt - 4 * g4
                                kb.wait("vector", S_ev)
                                for idx in range(2):
                                    pt = pt_b[(kt % 2) * 2 + idx]
                                    kb.incop("vector", S_dv, lambda e, o=pt, a=pt,
                                             b=masks_sb[:, ts(o_, 512)]:
                                             e.tensor_tensor(o, a, b, ALU.mult))
                                kb.wait("tensor", S_dv)
                            else:
                                kb.wait("tensor", S_ev)
                            exp_done.append(S_ev.n)
                            for (idx, (hh, po)) in enumerate(((hA, poA), (hB, poB))):
                                pt = pt_b[(kt % 2) * 2 + idx]
                                kb.mm(S_mm, lambda e, o=po[0:65, 0:512],
                                      l=Vhalf[:, kl * 1040 + hh * 65: kl * 1040 + (hh + 1) * 65],
                                      r=pt, st=(kt == 0), sp=(kt == nkt - 1):
                                      e.matmul(o, l, r, start=st, stop=sp))
                        if True:
                            kb.wait("scalar", S_mm)
                            for (hh, po) in ((hA, poA), (hB, poB)):
                                kb.wait("scalar", S_dv)
                                kb.incop("scalar", S_ev, lambda e, o=lnl[64:65, 0:512], i=po[64:65, 0:512]:
                                         e.activation(o, i, AF.Ln, bias=0.0, scale=1.0))
                                kb.incop("scalar", S_ev, lambda e, o=lnl[64:65, 0:512], i=lnl[64:65, 0:512]:
                                         e.activation(o, i, AF.Exp, bias=0.0, scale=-1.0))
                                kb.wait("tensor", S_ev)
                                kb.mm(S_mm, lambda e, o=PS[2][0:64, 0:512], l=ones[64:65, 0:64],
                                      r=lnl[64:65, 0:512]: e.matmul(o, l, r, start=True, stop=True))
                                kb.wait("scalar", S_mm)
                                kb.incop("scalar", S_ev, lambda e, o=rlb[0:64, :], i=PS[2][0:64, 0:512]:
                                         e.activation(o, i, AF.Identity, bias=0.0, scale=1.0))
                                kb.wait("vector", S_ev)
                                kb.wait("vector", S_xb)
                                kb.incop("vector", S_dv, lambda e, o=o_half[0:64, 0:512],
                                         a=po[0:64, 0:512], b=rlb[0:64, :]:
                                         e.tensor_tensor(o, a, b, ALU.mult))
                                kb.wait("sync", S_dv)
                                for tt in range(4):
                                    kb.dma_tr("sync",
                                              o_tok[:, tt * 1024 + hh * 64: tt * 1024 + (hh + 1) * 64],
                                              o_half[0:64, ts(tt, 128)], S_xb)
                for tt in range(4):
                    g = s * 16 + g4 * 4 + tt
                    kb.wait("tensor", S_ev)
                    kb.wait("tensor", S_dv)
                    kb.wait("sync", S_xb)
                    for c in range(NCH):
                        kb.dma_tr("sync", o_fm[:, tt * 1024 + c * 128: tt * 1024 + (c + 1) * 128],
                                  o_tok[:, tt * 1024 + c * 128: tt * 1024 + (c + 1) * 128], S_xb)
                    kb.wait("tensor", S_xb)
                    for n in range(2):
                        pb2 = PS[3 + n]
                        for c in range(NCH):
                            last = (c == NCH - 1) and layer == 0
                            if last:
                                kb.mm(S_mm, lambda e, o=pb2[:, 0:512],
                                      l=o_fm[:, tt * 1024 + c * 128: tt * 1024 + (c + 1) * 128],
                                      r=wout_sb[:, c * 1024 + n * 512: c * 1024 + (n + 1) * 512],
                                      st=(c == 0): e.matmul(o, l, r, start=st, stop=True))
                            else:
                                kb.mmq(lambda e, o=pb2[:, 0:512],
                                       l=o_fm[:, tt * 1024 + c * 128: tt * 1024 + (c + 1) * 128],
                                       r=wout_sb[:, c * 1024 + n * 512: c * 1024 + (n + 1) * 512],
                                       st=(c == 0): e.matmul(o, l, r, start=st, stop=False))
                        if layer == 1:
                            kb.mm(S_mm, lambda e, o=pb2[:, 0:512], l=onesb[0:1, 0:128],
                                  r=outb_r[0:1, ts(n, 512)]: e.matmul(o, l, r, start=False, stop=True))
                    kb.wait("scalar", S_mm)
                    if layer == 0 or os.environ.get("KL1SIMPLE"):
                        kb.wait("scalar", S_st)
                        for n in range(2):
                            kb.incop("scalar", S_ev, lambda e, o=s0t[:, ts(n, 512)], i=PS[3 + n][:, 0:512]:
                                     e.activation(o, i, AF.Identity, bias=0.0, scale=1.0))
                        kb.wait("sync", S_ev)
                        kb.dma("sync", (S0 if layer == 0 else Y1)[ts(g, 128), :], s0t, S_st)
                    else:
                        kb.dma("sync", xn1t, XN1[ts(g, 128), :], S_h2[tt % 2])
                        kb.wait("vector", S_h2[tt % 2])
                        kb.wait("vector", S_mm)
                        kb.wait("vector", S_st)
                        for n in range(2):
                            kb.incop("vector", S_dv, lambda e, o=s0t[:, ts(n, 512)],
                                     a=PS[3 + n][:, 0:512], b=xn1t[:, ts(n, 512)]:
                                     e.tensor_tensor(o, a, b, ALU.add))
                        kb.wait("sync", S_dv)
                        kb.dma("sync", Y1[ts(g, 128), :], s0t, S_st)
                        kb.wait("scalar", S_mm)
                        kb.wait("scalar", S_xb)
                        kb.wait("scalar", S_dv)
                        for n in range(2):
                            kb.incop("scalar", S_ev, lambda e, o=ctxt[:, ts(n, 512)], i=PS[3 + n][:, 0:512]:
                                     e.activation(o, i, AF.Identity, bias=0.0, scale=1.0))
                        kb.wait("sync", S_ev)
                        for c in range(NCH):
                            kb.dma_tr("sync", ctxfmst[:, ts(c, 128)], ctxt[:, ts(c, 128)], S_xb)
                        kb.wait("sync", S_xb)
                        kb.dma("sync", CTXFM[g, :, :], ctxfmst, S_xb)
            arena_bf.pop(); arena_f.pop(); arena_s.pop()
        arena_bf.pop(); arena_f.pop(); arena_s.pop()

    if KPH >= 2:
        phase_attn(0)
    if "d_s0" in dbg:
        kb.wait("sync", S_st)
        for t_ in range(NT128):
            kb.dma("sync", dbg["d_s0"][ts(t_, 128), :], S0[ts(t_, 128), :], S_st)

    def phase_moe():
        ph_serial("moe")
        phase_barrier()
        arena_bf.push(); arena_f.push(); arena_s.push()
        gwT = arena_bf.alloc(64)
        gwt_tmp = arena_bf.alloc(1024)
        cbb = arena_f.alloc(8)
        cbrow = arena_f.alloc(8)
        xnfm4 = [arena_bf.alloc(4096) for _ in range(2)]
        wg_sb = arena_bf.alloc(4096)
        wu_sb = arena_bf.alloc(4096)
        wd_sb = [arena_bf.alloc(1024) for _ in range(2)]
        act_sb = arena_bf.alloc(36 * 512)
        sg_sb = arena_f.alloc(512)
        scrow = arena_f.alloc(512)
        cw_fm = arena_f.alloc(512)
        cwb_sb = arena_f.alloc(512)
        sctok = arena_f.alloc(8)
        raw = arena_f.alloc(8)
        gs4 = arena_f.alloc(4)
        gm = arena_f.alloc(8)
        m8 = arena_f.alloc(8)
        sfo = arena_f.alloc(8)
        sfw = arena_f.alloc(8)
        wsel = arena_f.alloc(8)
        rr = arena_f.alloc(8)
        t1_sb = arena_bf.alloc(512)
        g0_sb = arena_f.alloc(1024)
        x0t = arena_f.alloc(1024)
        s0t2 = arena_f.alloc(1024)
        x1t = arena_f.alloc(1024)
        stat = arena_f.alloc(18)
        satt = arena_f.alloc(8)
        xn1bf = arena_bf.alloc(1024)
        xfmst = arena_bf.alloc(1024)
        runmax = arena_s.alloc(4)
        kb.on("gpsimd", lambda e: e.memset(runmax[:, :], 0.0))
        kb.on_raw("gpsimd", lambda e: e.drain())
        kb.on_raw("gpsimd", lambda e: e.nop().then_inc(S_gp.h, 1)); S_gp.n += 1
        kb.wait("vector", S_gp)

        kb.dma("gpsimd", gwt_tmp[0:8, :], gate_w, S_ga)
        kb.dma("sync", cbrow[0:1, 0:8], corr_b.unsqueeze(0), S_ldw)
        kb.wait("tensor", S_ga)
        kb.wait("tensor", S_ldw)
        for c in range(NCH):
            kb.wait("tensor", S_ev)
            kb.mm(S_mm, lambda e, o=PTR[:, 0:8], l=gwt_tmp[0:8, ts(c, 128)], r=iden_bf[0:8, 0:8]:
                  e.transpose(o, l, r))
            kb.wait("scalar", S_mm)
            kb.incop("scalar", S_ev, lambda e, o=gwT[:, ts(c, 8)], i=PTR[:, 0:8]:
                     e.activation(o, i, AF.Identity, bias=0.0, scale=1.0))
        kb.wait("tensor", S_ev)
        kb.mm(S_mm, lambda e, o=PS[2][:, 0:8], l=ones[0:1, 0:128], r=cbrow[0:1, 0:8]:
              e.matmul(o, l, r, start=True, stop=True))
        kb.wait("scalar", S_mm)
        kb.incop("scalar", S_ev, lambda e, o=cbb, i=PS[2][:, 0:8]:
                 e.activation(o, i, AF.Identity, bias=0.0, scale=1.0))
        kb.wait("vector", S_ev)

        for g5 in range(NT512):
            bi = g5 % 2
            kb.wait("sync", S_mm)
            kb.dma("sync", xnfm4[bi],
                   XN0FM[g5 * 4: g5 * 4 + 4, :, :].rearrange("t p (c x) -> p c t x", x=128),
                   S_x2[bi])
            kb.wait("tensor", S_x2[bi])
            kb.wait("tensor", S_ev)
            for c in range(NCH):
                if c == NCH - 1:
                    kb.mm(S_mm, lambda e, o=PS[2][0:8, 0:512], l=gwT[:, ts(c, 8)],
                          r=xnfm4[bi][:, ts(c, 512)], st=(c == 0): e.matmul(o, l, r, start=st, stop=True))
                else:
                    kb.mmq(lambda e, o=PS[2][0:8, 0:512], l=gwT[:, ts(c, 8)],
                           r=xnfm4[bi][:, ts(c, 512)], st=(c == 0): e.matmul(o, l, r, start=st, stop=False))
            kb.wait("scalar", S_mm)
            kb.wait("scalar", S_dv)
            kb.incop("scalar", S_ev, lambda e, o=scrow[0:8, :], i=PS[2][0:8, 0:512]:
                     e.activation(o, i, AF.Sigmoid, bias=0.0, scale=1.0))
            for st_ in range(4):
                kb.wait("tensor", S_ev)
                kb.wait("tensor", S_dv)
                kb.mm(S_mm, lambda e, o=PS[5][:, 0:8], l=scrow[0:8, ts(st_, 128)],
                      r=iden_f32_sb[0:8, 0:8]: e.transpose(o, l, r))
                kb.wait("vector", S_mm)
                kb.on("vector", lambda e, o=raw, i=PS[5][:, 0:8]: e.tensor_copy(o, i))
                kb.on("vector", lambda e: e.drain())
                kb.on("vector", lambda e, o=sctok, a=raw, b=cbb: e.tensor_tensor(o, a, b, ALU.add))
                kb.on("vector", lambda e: e.drain())
                kb.on("vector", lambda e, o=gs4, i=sctok.rearrange("p (g x) -> p g x", x=2):
                      e.tensor_reduce(o, i, AX.X, ALU.add))
                kb.on("vector", lambda e: e.drain())
                kb.on("vector", lambda e, o=rr[:, 0:1], i=gs4: e.tensor_reduce(o, i, AX.X, ALU.max))
                kb.on("vector", lambda e: e.drain())
                kb.on("vector", lambda e, o=gm[:, 0:4], i=gs4, m=rr[:, 0:1]:
                      e.tensor_scalar(o, i, m, 0.0, ALU.is_equal, ALU.add))
                kb.on("vector", lambda e: e.drain())
                kb.on("vector", lambda e, o=gm[:, 4:8], a=gm[:, 0:4], b=gs4:
                      e.scalar_tensor_tensor(o, a, -1e30, b, ALU.mult, ALU.add))
                kb.on("vector", lambda e: e.drain())
                kb.on("vector", lambda e, o=rr[:, 1:2], i=gm[:, 4:8]: e.tensor_reduce(o, i, AX.X, ALU.max))
                kb.on("vector", lambda e: e.drain())
                kb.on("vector", lambda e, o=gm[:, 0:4], i=gs4, m=rr[:, 1:2]:
                      e.tensor_scalar(o, i, m, 0.0, ALU.is_ge, ALU.add))
                kb.on("vector", lambda e: e.drain())
                kb.on("vector", lambda e, o=m8.rearrange("p (g x) -> p g x", x=2)[:, :, 0:1],
                      i=gm[:, 0:4].rearrange("p (g x) -> p g x", x=1): e.tensor_copy(o, i))
                kb.on("vector", lambda e, o=m8.rearrange("p (g x) -> p g x", x=2)[:, :, 1:2],
                      i=gm[:, 0:4].rearrange("p (g x) -> p g x", x=1): e.tensor_copy(o, i))
                kb.on("vector", lambda e: e.drain())
                kb.on("vector", lambda e, o=sfo, a=sctok, b=edelta_sb: e.tensor_tensor(o, a, b, ALU.add))
                kb.on("vector", lambda e: e.drain())
                kb.on("vector", lambda e, o=sfo, a=sfo, b=m8: e.tensor_tensor(o, a, b, ALU.mult))
                kb.on("vector", lambda e: e.drain())
                kb.on("vector", lambda e, o=sfo, a=sfo, b=edelta_sb: e.tensor_tensor(o, a, b, ALU.subtract))
                kb.on("vector", lambda e: e.drain())
                kb.on("vector", lambda e, o=sfw, i=sfo: e.tensor_copy(o, i))
                kb.on("vector", lambda e: e.drain())
                for rep in range(3):
                    kb.on("vector", lambda e, o=rr[:, 2:3], i=sfw: e.tensor_reduce(o, i, AX.X, ALU.max))
                    kb.on("vector", lambda e: e.drain())
                    kb.on("vector", lambda e, o=wsel, i=sfw, m=rr[:, 2:3]:
                          e.tensor_scalar(o, i, m, 0.0, ALU.is_equal, ALU.add))
                    kb.on("vector", lambda e: e.drain())
                    kb.on("vector", lambda e, o=sfw, a=wsel, b=sfw:
                          e.scalar_tensor_tensor(o, a, -1e30, b, ALU.mult, ALU.add))
                    kb.on("vector", lambda e: e.drain())
                kb.on("vector", lambda e, o=rr[:, 3:4], i=sfw: e.tensor_reduce(o, i, AX.X, ALU.max))
                kb.on("vector", lambda e: e.drain())
                kb.on("vector", lambda e, o=wsel, i=sfo, m=rr[:, 3:4]:
                      e.tensor_scalar(o, i, m, 0.0, ALU.is_ge, ALU.add))
                kb.on("vector", lambda e: e.drain())
                kb.on("vector", lambda e, o=wsel, a=wsel, b=raw: e.tensor_tensor(o, a, b, ALU.mult))
                kb.on("vector", lambda e: e.drain())
                kb.on("vector", lambda e, o=rr[:, 4:5], i=wsel: e.tensor_reduce(o, i, AX.X, ALU.add))
                kb.on("vector", lambda e: e.drain())
                kb.incop("vector", S_dv, lambda e, o=rr[:, 5:6], i=rr[:, 4:5]:
                         e.tensor_scalar(o, i, 1e-20, 0.0, ALU.add, ALU.add))
                kb.wait("scalar", S_dv)
                kb.incop("scalar", S_ev, lambda e, o=rr[:, 6:7], i=rr[:, 5:6]:
                         e.activation(o, i, AF.Ln, bias=0.0, scale=1.0))
                kb.incop("scalar", S_ev, lambda e, o=rr[:, 6:7], i=rr[:, 6:7]:
                         e.activation(o, i, AF.Exp, bias=0.0, scale=-1.0))
                kb.wait("vector", S_ev)
                kb.incop("vector", S_dv, lambda e, o=wsel, i=wsel, m=rr[:, 6:7]:
                         e.tensor_scalar(o, i, m, 2.5, ALU.mult, ALU.mult))
                kb.wait("tensor", S_dv)
                kb.mm(S_mm, lambda e, o=PS[5][0:8, 0:128], l=wsel[:, 0:8], r=iden_f32_sb:
                      e.transpose(o, l, r))
                kb.wait("scalar", S_mm)
                kb.incop("scalar", S_ev, lambda e, o=cw_fm[0:8, ts(st_, 128)], i=PS[5][0:8, 0:128]:
                         e.activation(o, i, AF.Identity, bias=0.0, scale=1.0))
            for ee in range(9):
                if ee < 8:
                    srcg, srcu = Wg[ee], Wu[ee]
                else:
                    srcg, srcu = Wg_s, Wu_s
                kb.wait("gpsimd", S_mm)
                for c in range(NCH):
                    kb.dma("gpsimd", wg_sb[:, ts(c, 512)], srcg[ts(c, 128), :], S_ga)
                    kb.dma("gpsimd", wu_sb[:, ts(c, 512)], srcu[ts(c, 128), :], S_ga)
                if ee < 8:
                    kb.wait("tensor", S_ev)
                    kb.wait("tensor", S_dv)
                    kb.mm(S_mm, lambda e, o=PS[2][:, 0:512], l=esel_sb[0:8, ts(ee, 128)],
                          r=cw_fm[0:8, 0:512]: e.matmul(o, l, r, start=True, stop=True))
                    kb.wait("scalar", S_mm)
                    kb.incop("scalar", S_ev, lambda e, o=cwb_sb, i=PS[2][:, 0:512]:
                             e.activation(o, i, AF.Identity, bias=0.0, scale=1.0))
                kb.wait("tensor", S_ga)
                for ic in range(4):
                    kb.wait("tensor", S_ev)
                    kb.wait("tensor", S_dv)
                    for c in range(NCH):
                        kb.mmq(lambda e, o=PS[0][:, 0:512],
                               l=wg_sb[:, c * 512 + ic * 128: c * 512 + (ic + 1) * 128],
                               r=xnfm4[bi][:, ts(c, 512)], st=(c == 0), sp=(c == NCH - 1):
                               e.matmul(o, l, r, start=st, stop=sp))
                    for c in range(NCH):
                        if c == NCH - 1:
                            kb.mm(S_mm, lambda e, o=PS[1][:, 0:512],
                                  l=wu_sb[:, c * 512 + ic * 128: c * 512 + (ic + 1) * 128],
                                  r=xnfm4[bi][:, ts(c, 512)], st=(c == 0):
                                  e.matmul(o, l, r, start=st, stop=True))
                        else:
                            kb.mmq(lambda e, o=PS[1][:, 0:512],
                                   l=wu_sb[:, c * 512 + ic * 128: c * 512 + (ic + 1) * 128],
                                   r=xnfm4[bi][:, ts(c, 512)], st=(c == 0):
                                   e.matmul(o, l, r, start=st, stop=False))
                    kb.wait("scalar", S_mm)
                    kb.wait("scalar", S_dv)
                    kb.incop("scalar", S_ev, lambda e, o=sg_sb, i=PS[0][:, 0:512]:
                             e.activation(o, i, AF.Silu, bias=0.0, scale=1.0))
                    kb.wait("vector", S_ev)
                    kb.wait("vector", S_mm)
                    aslot = act_sb[:, ts(ee * 4 + ic, 512)]
                    if ee < 8:
                        kb.on("vector", lambda e, o=t1_sb, a=PS[1][:, 0:512], b=cwb_sb:
                              e.tensor_tensor(o, a, b, ALU.mult))
                        kb.on("vector", lambda e: e.drain())
                        kb.incop("vector", S_dv, lambda e, o=aslot, a=t1_sb, b=sg_sb:
                                 e.tensor_tensor(o, a, b, ALU.mult))
                    else:
                        kb.incop("vector", S_dv, lambda e, o=aslot, a=PS[1][:, 0:512], b=sg_sb:
                                 e.tensor_tensor(o, a, b, ALU.mult))
            dv_acts = S_dv.n
            wd_done = []
            for half in range(2):
                first = True
                for ee in range(9):
                    for ic in range(4):
                        widx = (ee * 4 + ic) % 2
                        src = Wd[ee, ts(ic, 128), :] if ee < 8 else Wd_s[ts(ic, 128), :]
                        if len(wd_done) >= 2:
                            kb.wait("gpsimd", S_mm, wd_done[-2])
                        gsem = S_gb if widx == 0 else S_gc
                        kb.dma("gpsimd", wd_sb[widx], src, gsem)
                        kb.wait("tensor", gsem)
                        if first:
                            kb.wait("tensor", S_dv, dv_acts)
                            kb.wait("tensor", S_ev)
                            first = False
                        lastgrp = (ee == 8 and ic == 3)
                        for tt in (2 * half, 2 * half + 1):
                            for n in range(2):
                                pb2 = PS[3 + 2 * (tt - 2 * half) + n]
                                if lastgrp and tt == 2 * half + 1 and n == 1:
                                    kb.mm(S_mm, lambda e, o=pb2[:, 0:512],
                                          l=act_sb[:, (ee * 4 + ic) * 512 + tt * 128: (ee * 4 + ic) * 512 + (tt + 1) * 128],
                                          r=wd_sb[widx][:, ts(n, 512)],
                                          st=(ee == 0 and ic == 0): e.matmul(o, l, r, start=st, stop=True))
                                elif tt == 2 * half + 1 and n == 1:
                                    kb.mm(S_mm, lambda e, o=pb2[:, 0:512],
                                          l=act_sb[:, (ee * 4 + ic) * 512 + tt * 128: (ee * 4 + ic) * 512 + (tt + 1) * 128],
                                          r=wd_sb[widx][:, ts(n, 512)],
                                          st=(ee == 0 and ic == 0), sp=lastgrp:
                                          e.matmul(o, l, r, start=st, stop=sp))
                                else:
                                    kb.mmq(lambda e, o=pb2[:, 0:512],
                                           l=act_sb[:, (ee * 4 + ic) * 512 + tt * 128: (ee * 4 + ic) * 512 + (tt + 1) * 128],
                                           r=wd_sb[widx][:, ts(n, 512)],
                                           st=(ee == 0 and ic == 0), sp=lastgrp:
                                           e.matmul(o, l, r, start=st, stop=sp))
                        wd_done.append(S_mm.n)
                kb.wait("scalar", S_mm)
                for tt in (2 * half, 2 * half + 1):
                    g = g5 * 4 + tt
                    kb.wait("scalar", S_dv)
                    for n in range(2):
                        pb2 = PS[3 + 2 * (tt - 2 * half) + n]
                        kb.incop("scalar", S_ev, lambda e, o=g0_sb[:, ts(n, 512)], i=pb2[:, 0:512]:
                                 e.activation(o, i, AF.Silu, bias=0.0, scale=1.0))
                    kb.wait("sync", S_dv)
                    kb.wait("sync", S_st)
                    kb.dma("sync", x0t, x0[ts(g, 128), :], S_h2[0])
                    kb.dma("sync", s0t2, S0[ts(g, 128), :], S_h2[0])
                    kb.wait("vector", S_h2[0])
                    kb.wait("vector", S_ev)
                    kb.on("vector", lambda e, o=s0t2, a=s0t2, b=g0_sb: e.tensor_tensor(o, a, b, ALU.mult))
                    kb.on("vector", lambda e: e.drain())
                    kb.incop("vector", S_dv, lambda e, o=x1t, a=x0t, b=s0t2:
                             e.tensor_tensor(o, a, b, ALU.add))
                    kb.wait("sync", S_dv)
                    kb.dma("sync", X1[ts(g, 128), :], x1t, S_st)
                    rs_, nmr, mv = emit_ln(x1t, stat)
                    kb.wait("scalar", S_st)
                    kb.incop("scalar", S_ev, lambda e, o=s0t2, i=x1t, r=rs_, b=nmr:
                             e.activation(o, i, AF.Identity, bias=b[:, 0:1], scale=r[:, 0:1]))
                    kb.wait("sync", S_ev)
                    kb.dma("sync", XN1[ts(g, 128), :], s0t2, S_st)
                    kb.wait("vector", S_ev)
                    kb.on("vector", lambda e, o=satt[:, 0:1], i=mv, r=rs_:
                          e.tensor_scalar(o, i[:, 1:2], r[:, 0:1], 0.0, ALU.mult, ALU.add))
                    kb.on("vector", lambda e: e.drain())
                    kb.incop("vector", S_dv, lambda e, o=satt[:, 1:2], i=satt[:, 0:1], r=rs_:
                             e.tensor_scalar(o, i, r[:, 0:1], EPS, ALU.mult, ALU.add))
                    kb.wait("scalar", S_dv)
                    kb.incop("scalar", S_ev, lambda e, o=satt[:, 2:3], i=satt[:, 1:2]:
                             e.activation(o, i, AF.Sqrt, bias=0.0, scale=1.0))
                    kb.wait("vector", S_ev)
                    kb.on("vector", lambda e, o=satt[:, 3:4], i=satt[:, 2:3]: e.reciprocal(o, i))
                    kb.on("vector", lambda e: e.drain())
                    kb.on("vector", lambda e, o=satt[:, 4:5], i=rs_, m=satt[:, 3:4]:
                          e.tensor_scalar(o, i, m, 0.0, ALU.mult, ALU.add))
                    kb.on("vector", lambda e: e.drain())
                    kb.incop("vector", S_dv, lambda e, o=satt[:, 5:6], i=nmr, m=satt[:, 3:4]:
                             e.tensor_scalar(o, i, m, 0.0, ALU.mult, ALU.add))
                    kb.wait("scalar", S_dv)
                    kb.incop("scalar", S_ev, lambda e, o=xn1bf, i=x1t, r=satt[:, 4:5], b=satt[:, 5:6]:
                             e.activation(o, i, AF.Identity, bias=b[:, 0:1], scale=r[:, 0:1]))
                    kb.wait("vector", S_dv)
                    kb.on("vector", lambda e, o=satt[:, 6:7], i=s0t2:
                          e.tensor_reduce(o, i, AX.X, ALU.max, apply_absolute_value=True))
                    kb.on("vector", lambda e: e.drain())
                    kb.incop("vector", S_dv, lambda e, o=runmax[:, 0:1], a=runmax[:, 0:1],
                             b=satt[:, 6:7]: e.tensor_tensor(o, a, b, ALU.max))
                    kb.wait("sync", S_ev)
                    kb.wait("sync", S_xb)
                    for c in range(NCH):
                        kb.dma_tr("sync", xfmst[:, ts(c, 128)], xn1bf[:, ts(c, 128)], S_xb)
                    kb.wait("sync", S_xb)
                    kb.dma("sync", AFM[g, :, :], xfmst, S_xb)
        kb.wait("tensor", S_dv)
        kb.mm(S_mm, lambda e, o=PS[2][0:1, 0:128], l=runmax[:, 0:1], r=iden_f32_sb:
              e.transpose(o, l, r))
        kb.wait("vector", S_mm)
        kb.on("vector", lambda e, o=runmax[0:1, 1:2], i=PS[2][0:1, 0:128]:
              e.tensor_reduce(o, i, AX.X, ALU.max))
        kb.on("vector", lambda e: e.drain())
        kb.on("vector", lambda e, o=ccst[0:1, 0:8]: e.memset(o, 0.0))
        kb.on("vector", lambda e: e.drain())
        kb.incop("vector", S_dv, lambda e: e.tensor_copy(ccst[0:1, 0:1], runmax[0:1, 1:2]))
        kb.wait("gpsimd", S_dv)
        kb.dma("gpsimd", CCIN.ap(), ccst[0:1, 0:8], S_gc)
        kb.wait("gpsimd", S_gc)
        S_cc.n += 1
        if os.environ.get("KNOCC"):
            kb.dma("gpsimd", CCOUT.ap(), CCIN.ap(), S_gc)
            kb.on_raw("gpsimd", lambda e: e.nop().then_inc(S_cc.h, 1))
        else:
            kb.on_raw("gpsimd", lambda e: e.collective_compute(
                "AllReduce", ALU.max, replica_groups=[list(range(NCORES))],
                ins=[CCIN.ap().opt()], outs=[CCOUT.ap().opt()]).then_inc(S_cc.h, 1))
        arena_bf.pop(); arena_f.pop(); arena_s.pop()

    if KPH >= 3:
        phase_moe()
    if "d_x1" in dbg:
        kb.wait("sync", S_st)
        for t_ in range(NT128):
            kb.dma("sync", dbg["d_x1"][ts(t_, 128), :], X1[ts(t_, 128), :], S_st)
    if "d_xn1" in dbg:
        kb.wait("sync", S_st)
        for t_ in range(NT128):
            kb.dma("sync", dbg["d_xn1"][ts(t_, 128), :], XN1[ts(t_, 128), :], S_st)

    def phase_l1prep():
        ph_serial("l1prep")
        phase_barrier()
        arena_bf.push(); arena_f.push(); arena_s.push()
        vt = arena_bf.alloc(128)
        bcol_bf = arena_bf.alloc(16)
        wtile = [arena_bf.alloc(3 * D) for _ in range(2)]
        fftile = [arena_bf.alloc(FF) for _ in range(2)]
        b1full = arena_f.alloc(3 * D)
        qkvbrow = arena_f.alloc(3 * D)
        obrow = arena_f.alloc(1024)
        rowtmp = arena_bf.alloc(3 * D)
        ffb1t = arena_bf.alloc(128)
        b2acc = arena_f.alloc(32)
        S_g2p = [S_ga, S_gb]
        vecs = [g_attn, b_attn, g1v, b1v, g2v, b2v]
        for vi, vv in enumerate(vecs):
            kb.wait("tensor", S_ev)
            kb.wait("gpsimd", S_mm)
            kb.dma("gpsimd", vt[0:8, 0:128], vv.rearrange("(c p) -> c p", p=128), S_ga)
            kb.wait("gpsimd", S_ga)
            kb.wait("tensor", S_ga)
            kb.mm(S_mm, lambda e, o=PTR[:, 0:8], l=vt[0:8, 0:128], r=iden_bf[0:8, 0:8]:
                  e.transpose(o, l, r))
            kb.wait("scalar", S_mm)
            kb.incop("scalar", S_ev, lambda e, o=gcols[:, ts(vi, 8)], i=PTR[:, 0:8]:
                     e.activation(o, i, AF.Identity, bias=0.0, scale=1.0))
        kb.wait("vector", S_ev)
        kb.incop("vector", S_dv, lambda e, o=gcols[:, 16:32], i=gcols[:, 16:32]:
                 e.tensor_scalar(o, i, 100.0, 0.0, ALU.mult, ALU.add))
        kb.incop("scalar", S_ev, lambda e, o=bcol_bf[:, 0:8], i=gcols[:, 8:16]:
                 e.activation(o, i, AF.Identity, bias=0.0, scale=1.0))
        kb.incop("scalar", S_ev, lambda e, o=bcol_bf[:, 8:16], i=gcols[:, 40:48]:
                 e.activation(o, i, AF.Identity, bias=0.0, scale=1.0))
        kb.dma("sync", qkvbrow[0:1, :], qkv_b1.unsqueeze(0), S_ldw)
        kb.dma("sync", obrow[0:1, :], out_b1.unsqueeze(0), S_ldw)
        kb.dma("gpsimd", binl_r[0:1, :], b_inl.unsqueeze(0), S_ga)
        kb.wait("gpsimd", S_ga)
        kb.dma("gpsimd", ffb2_r[0:1, :], ff_b2.unsqueeze(0), S_ga)
        kb.wait("gpsimd", S_ga)
        kb.dma("gpsimd", ffb1t[0:32, 0:128], ff_b1.rearrange("(c p) -> c p", p=128), S_ga)
        kb.wait("gpsimd", S_ga)
        ldm = S_ldw.n
        gam = S_ga.n
        kb.on("vector", lambda e, o=b1full[0:1, :]: e.memset(o, 0.0))
        kb.incop("vector", S_dv, lambda e: e.nop())
        wq_done = []
        for c in range(NCH):
            wb = wtile[c % 2]
            if c >= 2:
                kb.wait("gpsimd", S_xb, wq_done[c - 2][0])
                kb.wait("gpsimd", S_mm, wq_done[c - 2][1])
            kb.dma("gpsimd", wb, qkv_w1[ts(c, 128), :], S_g2p[c % 2])
            kb.wait("tensor", S_g2p[c % 2])
            kb.wait("tensor", S_ev)
            for n in range(6):
                kb.wait("tensor", S_dv)
                kb.mm(S_mm, lambda e, o=PS[2][0:1, 0:512], l=bcol_bf[:, c:c + 1],
                      r=wb[:, ts(n, 512)]: e.matmul(o, l, r, start=True, stop=True))
                kb.wait("vector", S_mm)
                kb.incop("vector", S_dv, lambda e, o=b1full[0:1, ts(n, 512)],
                         a=b1full[0:1, ts(n, 512)], b=PS[2][0:1, 0:512]:
                         e.tensor_tensor(o, a, b, ALU.add))
            kb.wait("scalar", S_g2p[c % 2])
            kb.wait("scalar", S_dv)
            kb.wait("scalar", S_mm)
            kb.incop("scalar", S_ev, lambda e, o=wb, i=wb, r=gcols[:, c:c + 1]:
                     e.activation(o, i, AF.Identity, bias=0.0, scale=r))
            kb.wait("sync", S_ev)
            kb.dma("sync", WQ1S[c, :, :], wb, S_xb)
            kb.wait("tensor", S_ev)
            wq_done.append((S_xb.n, S_mm.n))
        kb.wait("vector", S_ldw, ldm)
        kb.on("vector", lambda e, o=b1full[0:1, :], a=b1full[0:1, :], b=qkvbrow[0:1, :]:
              e.tensor_tensor(o, a, b, ALU.add))
        kb.incop("vector", S_dv, lambda e: e.nop())
        kb.wait("scalar", S_dv)
        kb.incop("scalar", S_ev, lambda e, o=rowtmp[0:1, :], i=b1full[0:1, :]:
                 e.activation(o, i, AF.Identity, bias=0.0, scale=1.0))
        kb.wait("sync", S_ev)
        kb.dma("sync", B1R_D[0:1, :], rowtmp[0:1, :], S_xb)
        kb.incop("scalar", S_ev, lambda e, o=rowtmp[0:1, 0:1024], i=obrow[0:1, :]:
                 e.activation(o, i, AF.Identity, bias=0.0, scale=1.0))
        kb.wait("sync", S_ev)
        kb.dma("sync", OUTB_D[0:1, :], rowtmp[0:1, 0:1024], S_xb)
        kb.on("vector", lambda e, o=b2acc[:, 0:32]: e.memset(o, 0.0))
        kb.incop("vector", S_dv, lambda e: e.nop())
        ff_done = []
        for c in range(NCH):
            fb = fftile[c % 2]
            if c >= 2:
                kb.wait("gpsimd", S_xb, ff_done[c - 2][0])
                kb.wait("gpsimd", S_mm, ff_done[c - 2][1])
            kb.dma("gpsimd", fb, ff_w1[ts(c, 128), :], S_g2p[c % 2])
            kb.wait("scalar", S_g2p[c % 2])
            kb.wait("scalar", S_dv)
            kb.incop("scalar", S_ev, lambda e, o=fb, i=fb, r=gcols[:, 32 + c:33 + c]:
                     e.activation(o, i, AF.Identity, bias=0.0, scale=r))
            kb.wait("sync", S_ev)
            kb.dma("sync", FFW1S[c, :, :], fb, S_xb)
            kb.wait("tensor", S_ev)
            for fcc in range(FFC):
                kb.wait("tensor", S_dv)
                kb.mm(S_mm, lambda e, o=PS[2][0:128, 0:1], l=fb[:, ts(fcc, 128)],
                      r=bcol_bf[:, 8 + c:9 + c]: e.matmul(o, l, r, start=True, stop=True))
                kb.wait("vector", S_mm)
                kb.incop("vector", S_dv, lambda e, o=b2acc[:, fcc:fcc + 1],
                         a=b2acc[:, fcc:fcc + 1], b=PS[2][0:128, 0:1]:
                         e.tensor_tensor(o, a, b, ALU.add))
            ff_done.append((S_xb.n, S_mm.n))
        kb.wait("tensor", S_ga, gam)
        kb.mm(S_mm, lambda e, o=PTR[:, 0:32], l=ffb1t[0:32, 0:128], r=iden_bf[0:32, 0:32]:
              e.transpose(o, l, r))
        kb.wait("vector", S_mm)
        kb.wait("vector", S_dv)
        kb.incop("vector", S_dv, lambda e, o=b1acol, a=b2acc[:, 0:32], b=PTR[:, 0:32]:
                 e.tensor_tensor(o, a, b, ALU.add))
        arena_bf.pop(); arena_f.pop(); arena_s.pop()

    if KPH >= 4:
        phase_l1prep()
    if KPH >= 5:
        phase_attn(1)
    if "d_y1" in dbg:
        kb.wait("sync", S_st)
        for t_ in range(NT128):
            kb.dma("sync", dbg["d_y1"][ts(t_, 128), :], Y1[ts(t_, 128), :], S_st)

    def phase_recur():
        ph_serial("recur")
        phase_barrier()
        arena_bf.push(); arena_f.push(); arena_s.push()
        wc_sb = arena_bf.alloc(8192)
        wx_sb = arena_bf.alloc(8192)
        ctxt = [arena_bf.alloc(4096) for _ in range(2)]
        Xa = arena_bf.alloc(4096)
        Xb = arena_bf.alloc(4096)
        ub = arena_bf.alloc(4096)
        tmp = arena_bf.alloc(512)
        xhst = arena_bf.alloc(1024)
        xhfm = arena_bf.alloc(4096)
        c_sb = arena_bf.alloc(4096)
        y1t = arena_f.alloc(4096)
        y2t = arena_f.alloc(1024)
        yh = arena_bf.alloc(1024)
        y2fmst = arena_bf.alloc(1024)
        stat = arena_f.alloc(18)
        for c in range(NCH):
            kb.dma("gpsimd", wc_sb[:, ts(c, 1024)], w_ctx[ts(c, 128), :], S_ga)
            kb.dma("gpsimd", wx_sb[:, ts(c, 1024)], w_x[ts(c, 128), :], S_ga)
        kb.wait("scalar", S_ga)
        for c in range(NCH):
            kb.incop("scalar", S_ev, lambda e, o=wx_sb[:, ts(c, 1024)], i=wx_sb[:, ts(c, 1024)]:
                     e.activation(o, i, AF.Identity, bias=0.0, scale=0.01))
        kb.wait("tensor", S_ev)
        for g5 in range(NT512):
            bi = g5 % 2
            kb.wait("sync", S_mm)
            kb.dma("sync", ctxt[bi],
                   CTXFM[g5 * 4: g5 * 4 + 4, :, :].rearrange("t p (c x) -> p c t x", x=128),
                   S_x2[bi])
            for tt in range(4):
                g = g5 * 4 + tt
                kb.wait("sync", S_dv)
                kb.dma("sync", y1t[:, ts(tt, 1024)], Y1[ts(g, 128), :], S_h2[tt % 2])
                kb.wait("vector", S_h2[tt % 2])
                rs_, nmr, _ = emit_ln(y1t[:, ts(tt, 1024)], stat)
                kb.wait("scalar", S_xb)
                kb.incop("scalar", S_ev, lambda e, o=xhst, i=y1t[:, ts(tt, 1024)], r=rs_, b=nmr:
                         e.activation(o, i, AF.Identity, bias=b[:, 0:1], scale=r[:, 0:1]))
                kb.wait("sync", S_ev)
                for c in range(NCH):
                    kb.dma_tr("sync", xhfm[:, c * 512 + tt * 128: c * 512 + (tt + 1) * 128],
                              xhst[:, ts(c, 128)], S_xb)
                kb.wait("sync", S_xb)
            kb.wait("scalar", S_xb)
            kb.wait("scalar", S_mm)
            kb.wait("scalar", S_dv)
            for c in range(NCH):
                kb.incop("scalar", S_ev, lambda e, o=Xa[:, ts(c, 512)], i=xhfm[:, ts(c, 512)],
                         r=gcols[:, 16 + c:17 + c], b=gcols[:, 24 + c:25 + c]:
                         e.activation(o, i, AF.Identity, bias=b[:, 0:1], scale=r[:, 0:1]))
            kb.wait("tensor", S_ev)
            kb.wait("tensor", S_x2[bi])
            for oc in range(NCH):
                pb2 = PS[oc % 2]
                kb.wait("tensor", S_ev)
                for c in range(NCH):
                    kb.mmq(lambda e, o=pb2[:, 0:512],
                           l=wc_sb[:, c * 1024 + oc * 128: c * 1024 + (oc + 1) * 128],
                           r=ctxt[bi][:, ts(c, 512)], st=(c == 0): e.matmul(o, l, r, start=st, stop=False))
                kb.mm(S_mm, lambda e, o=pb2[:, 0:512], l=binl_r[0:1, ts(oc, 128)],
                      r=onesb[0:1, 0:512]: e.matmul(o, l, r, start=False, stop=True))
                kb.wait("scalar", S_mm)
                kb.incop("scalar", S_ev, lambda e, o=c_sb[:, ts(oc, 512)], i=pb2[:, 0:512]:
                         e.activation(o, i, AF.Identity, bias=0.0, scale=1.0))
                kb.wait("tensor", S_ev)
                for c in range(NCH):
                    if c == NCH - 1:
                        kb.mm(S_mm, lambda e, o=pb2[:, 0:512],
                              l=wx_sb[:, c * 1024 + oc * 128: c * 1024 + (oc + 1) * 128],
                              r=Xa[:, ts(c, 512)]: e.matmul(o, l, r, start=False, stop=True))
                    else:
                        kb.mmq(lambda e, o=pb2[:, 0:512],
                               l=wx_sb[:, c * 1024 + oc * 128: c * 1024 + (oc + 1) * 128],
                               r=Xa[:, ts(c, 512)]: e.matmul(o, l, r, start=False, stop=False))
                kb.wait("scalar", S_mm)
                kb.incop("scalar", S_ev, lambda e, o=ub[:, ts(oc, 512)], i=pb2[:, 0:512]:
                         e.activation(o, i, AF.Tanh, bias=0.0, scale=1.0))
                kb.wait("vector", S_ev)
                kb.incop("vector", S_dv, lambda e, o=Xb[:, ts(oc, 512)], a=Xa[:, ts(oc, 512)],
                         b=ub[:, ts(oc, 512)]:
                         e.scalar_tensor_tensor(o, a, 0.99, b, ALU.mult, ALU.add))
            cur, prev = Xb, Xa
            for it in range(1, 8):
                itbar = S_dv.n
                kb.wait("tensor", S_dv, itbar)
                for oc in range(NCH):
                    pb2 = PS[oc % 2]
                    kb.wait("vector", S_ev)
                    kb.incop("vector", S_dv, lambda e, o=pb2[:, 0:512], i=c_sb[:, ts(oc, 512)]:
                             e.tensor_copy(o, i))
                    kb.wait("tensor", S_dv)
                    for c in range(NCH):
                        if c == NCH - 1:
                            kb.mm(S_mm, lambda e, o=pb2[:, 0:512],
                                  l=wx_sb[:, c * 1024 + oc * 128: c * 1024 + (oc + 1) * 128],
                                  r=cur[:, ts(c, 512)]: e.matmul(o, l, r, start=False, stop=True))
                        else:
                            kb.mmq(lambda e, o=pb2[:, 0:512],
                                   l=wx_sb[:, c * 1024 + oc * 128: c * 1024 + (oc + 1) * 128],
                                   r=cur[:, ts(c, 512)]: e.matmul(o, l, r, start=False, stop=False))
                    kb.wait("scalar", S_mm)
                    kb.incop("scalar", S_ev, lambda e, o=ub[:, ts(oc, 512)], i=pb2[:, 0:512]:
                             e.activation(o, i, AF.Tanh, bias=0.0, scale=1.0))
                    kb.wait("vector", S_ev)
                    kb.on("vector", lambda e, o=tmp, a=cur[:, ts(oc, 512)], b=prev[:, ts(oc, 512)]:
                          e.scalar_tensor_tensor(o, a, 2.1, b, ALU.mult, ALU.subtract))
                    kb.on("vector", lambda e: e.drain())
                    kb.incop("vector", S_dv, lambda e, o=prev[:, ts(oc, 512)], a=tmp,
                             b=ub[:, ts(oc, 512)]:
                             e.scalar_tensor_tensor(o, a, 0.9, b, ALU.mult, ALU.add))
                cur, prev = prev, cur
            kb.wait("sync", S_dv)
            for tt in range(4):
                g = g5 * 4 + tt
                for c in range(NCH):
                    kb.dma_tr("sync", xhst[:, ts(c, 128)],
                              cur[:, c * 512 + tt * 128: c * 512 + (tt + 1) * 128], S_xb)
                kb.wait("vector", S_xb)
                kb.incop("vector", S_dv, lambda e, o=y2t, a=xhst, b=y1t[:, ts(tt, 1024)]:
                         e.scalar_tensor_tensor(o, a, 0.01, b, ALU.mult, ALU.add))
                kb.wait("sync", S_dv)
                kb.dma("sync", Y2[ts(g, 128), :], y2t, S_st)
                rs_, nmr, _ = emit_ln(y2t, stat)
                kb.incop("scalar", S_ev, lambda e, o=yh, i=y2t, r=rs_, b=nmr:
                         e.activation(o, i, AF.Identity, bias=b[:, 0:1], scale=r[:, 0:1]))
                kb.wait("sync", S_ev)
                for c in range(NCH):
                    kb.dma_tr("sync", y2fmst[:, ts(c, 128)], yh[:, ts(c, 128)], S_xb)
                kb.wait("sync", S_xb)
                kb.dma("sync", Y2FM[g, :, :], y2fmst, S_xb)
        arena_bf.pop(); arena_f.pop(); arena_s.pop()

    if KPH >= 6:
        phase_recur()
    if "d_y2" in dbg:
        kb.wait("sync", S_st)
        kb.wait("sync", S_xb)
        for t_ in range(NT128):
            kb.dma("sync", dbg["d_y2"][ts(t_, 128), :], Y2[ts(t_, 128), :], S_st)

    def phase_ffn1():
        ph_serial("ffn1")
        phase_barrier()
        arena_bf.push(); arena_f.push(); arena_s.push()
        w1 = arena_bf.alloc(NCH * FF)
        y2fm = [arena_bf.alloc(4096) for _ in range(2)]
        hst = [arena_bf.alloc(512) for _ in range(2)]
        for c in range(NCH):
            kb.dma("gpsimd", w1[:, ts(c, FF)], FFW1S[c, :, :], S_ga)
        kb.wait("tensor", S_ga)
        for g5 in range(NT512):
            bi = g5 % 2
            kb.wait("sync", S_mm)
            kb.dma("sync", y2fm[bi],
                   Y2FM[g5 * 4: g5 * 4 + 4, :, :].rearrange("t p (c x) -> p c t x", x=128),
                   S_x2[bi])
            kb.wait("tensor", S_x2[bi])
            for fcc in range(FFC):
                pb2 = PS[fcc % 2]
                kb.wait("tensor", S_ev)
                for c in range(NCH):
                    if c == NCH - 1:
                        kb.mm(S_mm, lambda e, o=pb2[:, 0:512],
                              l=w1[:, c * FF + fcc * 128: c * FF + (fcc + 1) * 128],
                              r=y2fm[bi][:, ts(c, 512)], st=(c == 0): e.matmul(o, l, r, start=st, stop=True))
                    else:
                        kb.mmq(lambda e, o=pb2[:, 0:512],
                               l=w1[:, c * FF + fcc * 128: c * FF + (fcc + 1) * 128],
                               r=y2fm[bi][:, ts(c, 512)], st=(c == 0): e.matmul(o, l, r, start=st, stop=False))
                kb.wait("scalar", S_mm)
                hb = hst[fcc % 2]
                kb.wait("scalar", S_xb)
                kb.incop("scalar", S_ev, lambda e, o=hb, i=pb2[:, 0:512], b=b1acol[:, fcc:fcc + 1]:
                         e.activation(o, i, AF.Gelu, bias=b, scale=1.0))
                kb.wait("sync", S_ev)
                kb.dma("sync", HFM[g5, :, ts(fcc, 512)], hb, S_xb)
        kb.wait("sync", S_xb)
        arena_bf.pop(); arena_f.pop(); arena_s.pop()

    def phase_ffn2():
        ph_serial("ffn2")
        phase_barrier()
        arena_bf.push(); arena_f.push(); arena_s.push()
        w2 = arena_bf.alloc(FFC * D)
        hld = [arena_bf.alloc(512) for _ in range(2)]
        y2l = arena_f.alloc(1024)
        y3 = arena_f.alloc(1024)
        for fcc in range(FFC):
            kb.dma("gpsimd", w2[:, ts(fcc, D)], ff_w2[ts(fcc, 128), :], S_ga)
        kb.wait("tensor", S_ga)
        hf_done = []
        for g5 in range(NT512):
            for half in range(2):
                kb.wait("tensor", S_dv)
                for fcc in range(FFC):
                    hb = hld[fcc % 2]
                    if len(hf_done) >= 2:
                        kb.wait("sync", S_mm, hf_done[-2])
                    kb.dma("sync", hb, HFM[g5, :, ts(fcc, 512)], S_h2[fcc % 2])
                    kb.wait("tensor", S_h2[fcc % 2])
                    for tt in (2 * half, 2 * half + 1):
                        pi = 3 + 2 * (tt - 2 * half)
                        for n in range(2):
                            if tt == 2 * half + 1 and n == 1:
                                kb.mm(S_mm, lambda e, o=PS[pi + n][:, 0:512], l=hb[:, ts(tt, 128)],
                                      r=w2[:, fcc * D + n * 512: fcc * D + (n + 1) * 512],
                                      st=(fcc == 0): e.matmul(o, l, r, start=st, stop=False))
                            else:
                                kb.mmq(lambda e, o=PS[pi + n][:, 0:512], l=hb[:, ts(tt, 128)],
                                       r=w2[:, fcc * D + n * 512: fcc * D + (n + 1) * 512],
                                       st=(fcc == 0): e.matmul(o, l, r, start=st, stop=False))
                    hf_done.append(S_mm.n)
                for tt in (2 * half, 2 * half + 1):
                    g = g5 * 4 + tt
                    pi = 3 + 2 * (tt - 2 * half)
                    for n in range(2):
                        kb.mm(S_mm, lambda e, o=PS[pi + n][:, 0:512], l=onesb[0:1, 0:128],
                              r=ffb2_r[0:1, ts(n, 512)]: e.matmul(o, l, r, start=False, stop=True))
                    kb.wait("sync", S_dv)
                    kb.dma("sync", y2l, Y2[ts(g, 128), :], S_x2[tt % 2])
                    kb.wait("vector", S_x2[tt % 2])
                    kb.wait("vector", S_mm)
                    for n in range(2):
                        kb.incop("vector", S_dv, lambda e, o=y3[:, ts(n, 512)],
                                 a=PS[pi + n][:, 0:512], b=y2l[:, ts(n, 512)]:
                                 e.tensor_tensor(o, a, b, ALU.add))
                    kb.wait("sync", S_dv)
                    kb.dma("sync", S1[ts(g, 128), :], y3, S_st)
        arena_bf.pop(); arena_f.pop(); arena_s.pop()

    if KPH >= 7:
        phase_ffn1()
    if KPH >= 8:
        phase_ffn2()
    if "d_s1" in dbg:
        kb.wait("sync", S_st)
        for t_ in range(NT128):
            kb.dma("sync", dbg["d_s1"][ts(t_, 128), :], S1[ts(t_, 128), :], S_st)

    def phase_quantw():
        ph_serial("quantw")
        phase_barrier()
        arena_bf.push(); arena_f.push(); arena_s.push()
        wld = [arena_f.alloc(1024) for _ in range(2)]
        kwb = arena_bf.alloc(1024)
        trst = arena_bf.alloc(1024)
        trsd = arena_bf.alloc(2816)
        tmpa = arena_f.alloc(2816)
        tmpsgn = arena_bf.alloc(2816)
        rowm = arena_s.alloc(4)
        for (wsrc, wdst_d, ssb, nr) in ((w_gate, WQTG, swg, QIC), (w_up, WQTU, swu, QIC)):
            for ic in range(nr):
                bi2 = ic % 2
                kb.wait("sync", S_dv)
                kb.dma("sync", wld[bi2], wsrc[ts(ic, 128), :], S_w2[bi2])
                kb.wait("vector", S_w2[bi2])
                kb.wait("vector", S_ev)
                kb.on("vector", lambda e, o=rowm[:, 0:1], i=wld[bi2]:
                      e.tensor_reduce(o, i, AX.X, ALU.max, apply_absolute_value=True))
                kb.on("vector", lambda e: e.drain())
                kb.on("vector", lambda e, o=ssb[:, ic:ic + 1], i=rowm[:, 0:1]:
                      e.tensor_scalar(o, i, 1.0 / 127.0, 0.0, ALU.mult, ALU.add))
                kb.on("vector", lambda e: e.drain())
                kb.incop("vector", S_dv, lambda e, o=rowm[:, 1:2], i=ssb[:, ic:ic + 1]:
                         e.reciprocal(o, i))
                kb.wait("vector", S_xb)
                quant_chain(kwb, wld[bi2], rowm[:, 1:2], 1024, tmpa, tmpsgn)
                kb.wait("sync", S_dv)
                kb.wait("sync", S_st)
                for hc in range(NCH):
                    kb.dma_tr("sync", trst[:, ts(hc, 128)], kwb[:, ts(hc, 128)], S_xb)
                kb.wait("sync", S_xb)
                for hc in range(NCH):
                    kb.dma("sync", wdst_d[hc, :, ts(ic, 128)], trst[:, ts(hc, 128)], S_st)
        _wldd1 = arena_f.alloc(2816)
        wldd = [_wldd1, _wldd1]
        kwd = arena_bf.alloc(2816)
        for hc in range(NCH):
            bi2 = hc % 2
            kb.wait("sync", S_dv)
            kb.dma("sync", wldd[bi2], w_down[ts(hc, 128), :], S_w2[bi2])
            kb.wait("vector", S_w2[bi2])
            kb.wait("vector", S_ev)
            kb.on("vector", lambda e, o=rowm[:, 0:1], i=wldd[bi2]:
                  e.tensor_reduce(o, i, AX.X, ALU.max, apply_absolute_value=True))
            kb.on("vector", lambda e: e.drain())
            kb.on("vector", lambda e, o=swd[:, hc:hc + 1], i=rowm[:, 0:1]:
                  e.tensor_scalar(o, i, 1.0 / 127.0, 0.0, ALU.mult, ALU.add))
            kb.on("vector", lambda e: e.drain())
            kb.incop("vector", S_dv, lambda e, o=rowm[:, 1:2], i=swd[:, hc:hc + 1]:
                     e.reciprocal(o, i))
            kb.wait("vector", S_xb)
            quant_chain(kwd, wldd[bi2], rowm[:, 1:2], 2816, tmpa, tmpsgn)
            kb.wait("sync", S_dv)
            kb.wait("sync", S_st)
            for ic in range(QIC):
                kb.dma_tr("sync", trsd[:, ts(ic, 128)], kwd[:, ts(ic, 128)], S_xb)
            kb.wait("sync", S_xb)
            for ic in range(QIC):
                kb.dma("sync", WDT[ic, :, ts(hc, 128)], trsd[:, ts(ic, 128)], S_st)
        kb.wait("sync", S_st)
        kb.wait("sync", S_xb)
        arena_bf.pop(); arena_f.pop(); arena_s.pop()

    if KPH >= 9:
        phase_quantw()

    def phase_scale(ccout_t, clip_in, dst_off, sw_pairs):
        ph_serial("scale")
        phase_barrier()
        arena_f.push()
        ld = arena_f.alloc(16)
        kb.wait("sync", S_cc, S_cc.n)
        kb.dma("sync", ld[0:1, 0:8], ccout_t.ap(), S_ldw)
        kb.dma("sync", ld[0:1, 8:9], clip_in.unsqueeze(0), S_ldw)
        kb.wait("scalar", S_ldw)
        kb.incop("scalar", S_ev, lambda e, o=ld[0:1, 10:11], i=ld[0:1, 8:9]:
                 e.activation(o, i, AF.Sigmoid, bias=0.0, scale=1.0))
        kb.wait("vector", S_ev)
        kb.on("vector", lambda e, o=ld[0:1, 11:12], a=ld[0:1, 10:11], b=ld[0:1, 0:1]:
              e.tensor_tensor(o, a, b, ALU.mult))
        kb.on("vector", lambda e: e.drain())
        kb.incop("vector", S_dv, lambda e, o=ld[0:1, 12:13], i=ld[0:1, 11:12]:
                 e.tensor_scalar(o, i, 1.0 / 127.0, 0.0, ALU.mult, ALU.add))
        kb.wait("scalar", S_dv)
        kb.incop("scalar", S_ev, lambda e, o=ld[0:1, 13:14], i=ld[0:1, 12:13]:
                 e.activation(o, i, AF.Ln, bias=0.0, scale=1.0))
        kb.incop("scalar", S_ev, lambda e, o=ld[0:1, 13:14], i=ld[0:1, 13:14]:
                 e.activation(o, i, AF.Exp, bias=0.0, scale=-1.0))
        kb.wait("tensor", S_ev)
        kb.mm(S_mm, lambda e, o=PS[2][:, 0:2], l=ones[0:1, 0:128], r=ld[0:1, 12:14]:
              e.matmul(o, l, r, start=True, stop=True))
        kb.wait("scalar", S_mm)
        kb.incop("scalar", S_ev, lambda e, o=qscal[:, dst_off:dst_off + 2], i=PS[2][:, 0:2]:
                 e.activation(o, i, AF.Identity, bias=0.0, scale=1.0))
        kb.wait("vector", S_ev)
        for (dsts, srcs) in sw_pairs:
            kb.incop("vector", S_dv, lambda e, o=dsts, i=srcs, m=qscal[:, dst_off:dst_off + 1]:
                     e.tensor_scalar(o, i, m, 0.0, ALU.mult, ALU.add))
        arena_f.pop()

    if KPH >= 10:
        phase_scale(CCOUT, clip_x, 0, [(sgx, swg), (sux, swu)])

    def phase_gateup():
        ph_serial("gateup")
        phase_barrier()
        arena_bf.push(); arena_f.push(); arena_s.push()
        xl = arena_f.alloc(1024)
        kxt = arena_bf.alloc(1024)
        kxfm = arena_bf.alloc(4096)
        wstr = [arena_bf.alloc(2048) for _ in range(2)]
        tmpa = arena_f.alloc(1024)
        tmpsgn = arena_bf.alloc(1024)
        sgf = arena_f.alloc(512)
        hpf = arena_f.alloc(512)
        runm = arena_s.alloc(4)
        kb.on("gpsimd", lambda e: e.memset(runm[:, :], 0.0))
        kb.on_raw("gpsimd", lambda e: e.drain())
        kb.on_raw("gpsimd", lambda e: e.nop().then_inc(S_gp.h, 1)); S_gp.n += 1
        kb.wait("vector", S_gp)
        for g5 in range(NT512):
            kb.wait("sync", S_mm)
            for tt in range(4):
                g = g5 * 4 + tt
                kb.wait("sync", S_dv)
                kb.dma("sync", xl, XN1[ts(g, 128), :], S_x2[tt % 2])
                kb.wait("scalar", S_x2[tt % 2])
                kb.wait("vector", S_xb)
                quant_chain(kxt, xl, qscal[:, 1:2], 1024, tmpa, tmpsgn)
                kb.wait("sync", S_dv)
                for c in range(NCH):
                    kb.dma_tr("sync", kxfm[:, c * 512 + tt * 128: c * 512 + (tt + 1) * 128],
                              kxt[:, ts(c, 128)], S_xb)
                kb.wait("sync", S_xb)
            kb.wait("tensor", S_xb)
            for ic in range(QIC):
                bi2 = ic % 2
                if len(wstr_done) >= 2:
                    kb.wait("sync", S_mm, wstr_done[-2])
                kb.dma("sync", wstr[bi2][:, 0:1024],
                       WQTG[:, :, ts(ic, 128)].rearrange("c p x -> p c x"), S_w2[bi2])
                kb.dma("sync", wstr[bi2][:, 1024:2048],
                       WQTU[:, :, ts(ic, 128)].rearrange("c p x -> p c x"), S_w2[bi2])
                kb.wait("tensor", S_w2[bi2])
                kb.wait("tensor", S_ev)
                kb.wait("tensor", S_dv)
                for c in range(NCH):
                    kb.mmq(lambda e, o=PS[0][:, 0:512], l=wstr[bi2][:, ts(c, 128)],
                           r=kxfm[:, ts(c, 512)], st=(c == 0), sp=(c == NCH - 1):
                           e.matmul(o, l, r, start=st, stop=sp))
                for c in range(NCH):
                    if c == NCH - 1:
                        kb.mm(S_mm, lambda e, o=PS[1][:, 0:512],
                              l=wstr[bi2][:, 1024 + c * 128: 1024 + (c + 1) * 128],
                              r=kxfm[:, ts(c, 512)], st=(c == 0): e.matmul(o, l, r, start=st, stop=True))
                    else:
                        kb.mmq(lambda e, o=PS[1][:, 0:512],
                               l=wstr[bi2][:, 1024 + c * 128: 1024 + (c + 1) * 128],
                               r=kxfm[:, ts(c, 512)], st=(c == 0): e.matmul(o, l, r, start=st, stop=False))
                wstr_done.append(S_mm.n)
                kb.wait("scalar", S_mm)
                kb.wait("scalar", S_dv)
                kb.incop("scalar", S_ev, lambda e, o=sgf, i=PS[0][:, 0:512], r=sgx[:, ic:ic + 1]:
                         e.activation(o, i, AF.Silu, bias=0.0, scale=r))
                kb.wait("vector", S_ev)
                kb.wait("vector", S_st)
                kb.on("vector", lambda e, o=hpf, a=PS[1][:, 0:512], m=sux[:, ic:ic + 1], b=sgf:
                      e.scalar_tensor_tensor(o, a, m, b, ALU.mult, ALU.mult))
                kb.on("vector", lambda e: e.drain())
                kb.on("vector", lambda e, o=runm[:, 1:2], i=hpf:
                      e.tensor_reduce(o, i, AX.X, ALU.max, apply_absolute_value=True))
                kb.on("vector", lambda e: e.drain())
                kb.incop("vector", S_dv, lambda e, o=runm[:, 0:1], a=runm[:, 0:1], b=runm[:, 1:2]:
                         e.tensor_tensor(o, a, b, ALU.max))
                kb.wait("sync", S_dv)
                kb.dma("sync", HPRE[g5, :, ts(ic, 512)], hpf, S_st)
        kb.wait("tensor", S_dv)
        kb.mm(S_mm, lambda e, o=PS[2][0:1, 0:128], l=runm[:, 0:1], r=iden_f32_sb:
              e.transpose(o, l, r))
        kb.wait("vector", S_mm)
        kb.on("vector", lambda e, o=runm[0:1, 2:3], i=PS[2][0:1, 0:128]:
              e.tensor_reduce(o, i, AX.X, ALU.max))
        kb.on("vector", lambda e: e.drain())
        kb.on("vector", lambda e, o=ccst[0:1, 0:8]: e.memset(o, 0.0))
        kb.on("vector", lambda e: e.drain())
        kb.incop("vector", S_dv, lambda e: e.tensor_copy(ccst[0:1, 0:1], runm[0:1, 2:3]))
        kb.wait("gpsimd", S_dv)
        kb.dma("gpsimd", CCIN2.ap(), ccst[0:1, 0:8], S_gc)
        kb.wait("gpsimd", S_gc)
        S_cc.n += 1
        if os.environ.get("KNOCC"):
            kb.dma("gpsimd", CCOUT2.ap(), CCIN2.ap(), S_gc)
            kb.on_raw("gpsimd", lambda e: e.nop().then_inc(S_cc.h, 1))
        else:
            kb.on_raw("gpsimd", lambda e: e.collective_compute(
                "AllReduce", ALU.max, replica_groups=[list(range(NCORES))],
                ins=[CCIN2.ap().opt()], outs=[CCOUT2.ap().opt()]).then_inc(S_cc.h, 1))
        arena_bf.pop(); arena_f.pop(); arena_s.pop()

    wstr_done = []
    if KPH >= 11:
        phase_gateup()
    if KPH >= 12:
        phase_scale(CCOUT2, clip_down, 2, [(sdx, swd)])

    def phase_down():
        ph_serial("down")
        phase_barrier()
        arena_bf.push(); arena_f.push(); arena_s.push()
        wdt_sb = arena_bf.alloc(QIC * 1024)
        hql = [arena_f.alloc(512) for _ in range(2)]
        hq = arena_bf.alloc(QIC * 512)
        tmpa = arena_f.alloc(512)
        tmpsgn = arena_bf.alloc(512)
        g1fm = arena_bf.alloc(4096)
        g1tok = arena_bf.alloc(1024)
        x1l = arena_f.alloc(1024)
        s1l = arena_f.alloc(1024)
        outt = arena_f.alloc(1024)
        kb.dma("sync", wdt_sb, WDT[:, :, :].rearrange("i p x -> p i x"), S_ldw)
        kb.wait("tensor", S_ldw)
        for g5 in range(NT512):
            kb.wait("vector", S_mm)
            for ic in range(QIC):
                bi2 = ic % 2
                kb.wait("sync", S_dv)
                kb.dma("sync", hql[bi2], HPRE[g5, :, ts(ic, 512)], S_h2[bi2])
                kb.wait("scalar", S_h2[bi2])
                quant_chain(hq[:, ts(ic, 512)], hql[bi2], qscal[:, 3:4], 512, tmpa, tmpsgn)
            kb.wait("tensor", S_dv)
            for hc in range(NCH):
                pb2 = PS[hc % 2]
                kb.wait("tensor", S_ev)
                for ic in range(QIC):
                    if ic == QIC - 1:
                        kb.mm(S_mm, lambda e, o=pb2[:, 0:512],
                              l=wdt_sb[:, ic * 1024 + hc * 128: ic * 1024 + (hc + 1) * 128],
                              r=hq[:, ts(ic, 512)], st=(ic == 0): e.matmul(o, l, r, start=st, stop=True))
                    else:
                        kb.mmq(lambda e, o=pb2[:, 0:512],
                               l=wdt_sb[:, ic * 1024 + hc * 128: ic * 1024 + (hc + 1) * 128],
                               r=hq[:, ts(ic, 512)], st=(ic == 0): e.matmul(o, l, r, start=st, stop=False))
                kb.wait("scalar", S_mm)
                kb.wait("scalar", S_xb)
                kb.incop("scalar", S_ev, lambda e, o=g1fm[:, ts(hc, 512)], i=pb2[:, 0:512],
                         r=sdx[:, hc:hc + 1]:
                         e.activation(o, i, AF.Silu, bias=0.0, scale=r))
            kb.wait("sync", S_ev)
            for tt in range(4):
                g = g5 * 4 + tt
                kb.wait("sync", S_dv)
                for hc in range(NCH):
                    kb.dma_tr("sync", g1tok[:, ts(hc, 128)],
                              g1fm[:, hc * 512 + tt * 128: hc * 512 + (tt + 1) * 128], S_xb)
                kb.dma("sync", x1l, X1[ts(g, 128), :], S_x2[tt % 2])
                kb.dma("sync", s1l, S1[ts(g, 128), :], S_x2[tt % 2])
                kb.wait("vector", S_xb)
                kb.wait("vector", S_x2[tt % 2])
                kb.wait("vector", S_st)
                kb.on("vector", lambda e, o=s1l, a=s1l, b=g1tok: e.tensor_tensor(o, a, b, ALU.mult))
                kb.on("vector", lambda e: e.drain())
                kb.incop("vector", S_dv, lambda e, o=outt, a=x1l, b=s1l:
                         e.tensor_tensor(o, a, b, ALU.add))
                kb.wait("sync", S_dv)
                kb.dma("sync", out[ts(g, 128), :], outt, S_st)
        arena_bf.pop(); arena_f.pop(); arena_s.pop()

    if KPH >= 13:
        phase_down()
    else:
        out_stub()

    if "d_aux" in dbg:
        kb.wait("sync", S_st)
        aux = dbg["d_aux"]
        kb.dma("sync", aux[0:128, 0:8], qscal[:, 0:8], S_st)
        kb.dma("sync", aux[0:128, 32:54], swg[:, 0:22], S_st)
        kb.dma("sync", aux[0:128, 64:86], sgx[:, 0:22], S_st)
        kb.dma("sync", aux[0:128, 96:104], swd[:, 0:8], S_st)
        kb.dma("sync", aux[0:128, 104:112], sdx[:, 0:8], S_st)
        for j in range(8):
            kb.dma("sync", aux[256 + j * 128: 384 + j * 128, :], HPRE[0, :, ts(j, 1024)], S_st)

    kb.wait("gpsimd", S_st)
    kb.wait("sync", S_st)

    with nc.Block() as block:
        @block.sync
        def _(e):
            kb.run("sync", e)

        @block.tensor
        def _(e):
            kb.run("tensor", e)

        @block.scalar
        def _(e):
            kb.run("scalar", e)

        @block.vector
        def _(e):
            kb.run("vector", e)

        @block.gpsimd
        def _(e):
            kb.run("gpsimd", e)

    stack.close()
    return nc


def kernel(**inputs):
    consts = build_consts()
    nc = build_nc(DEBUG)
    x = np.ascontiguousarray(inputs["L0_attn_x"], dtype=np.float32)
    base = {}
    for k, v in inputs.items():
        if k == "L0_attn_x":
            continue
        base[k] = np.ascontiguousarray(v, dtype=np.float32)
    base.update(consts)
    in_maps = []
    for c in range(NCORES):
        m = dict(base)
        m["x0"] = np.ascontiguousarray(x[2 * c:2 * c + 2].reshape(NTOK, D))
        in_maps.append(m)
    res = run_bass_kernel_spmd(nc, in_maps, core_ids=list(range(NCORES)))
    outv = np.concatenate([r["out"].reshape(SEQ_PER_CORE, T, D) for r in res.results], axis=0)
    if DEBUG:
        kernel.dbg = [{k: v for k, v in r.items() if k.startswith("d_")} for r in res.results]
    return outv


# version 08
# speedup vs baseline: 1.0062x; 1.0062x over previous
import os
import numpy as np
import concourse.bass as bass
import concourse.mybir as mybir
from concourse.bass_utils import run_bass_kernel_spmd

F32 = mybir.dt.float32
BF16 = mybir.dt.bfloat16
I32 = mybir.dt.int32
AF = mybir.ActivationFunctionType
ALU = mybir.AluOpType
AX = mybir.AxisListType

NCORES = 8
B, T, D = 16, 2048, 1024
SEQ_PER_CORE = B // NCORES
NTOK = SEQ_PER_CORE * T
NT128 = NTOK // 128
NT512 = NTOK // 512
NCH = D // 128
HEADS = 16
HD = 64
FF = 4096
FFC = FF // 128
MOE_E, MOE_I = 8, 512
QI = 2816
QIC = QI // 128
EPS = 1e-5

DEBUG = os.environ.get("KDBG", "")


def build_consts():
    import ml_dtypes
    c = {}
    pos = np.arange(T, dtype=np.float32)[:, None]
    i = np.arange(0, HD, 2, dtype=np.float32)[None, :]
    theta = pos / np.power(10000.0, i / HD)
    cos64 = np.concatenate([np.cos(theta), np.cos(theta)], axis=1).astype(np.float32)
    sin64 = np.concatenate([np.sin(theta), np.sin(theta)], axis=1).astype(np.float32)
    c["c_cosrep"] = np.tile(cos64, (1, HEADS)).astype(ml_dtypes.bfloat16)
    c["c_sinrep"] = np.tile(sin64, (1, HEADS)).astype(ml_dtypes.bfloat16)
    tri = (np.arange(128)[None, :] >= np.arange(128)[:, None]).astype(np.float32)
    masks = np.zeros((4, 128, 512), dtype=np.float32)
    for o in range(4):
        for j in range(4):
            if j > o:
                masks[o][:, j * 128:(j + 1) * 128] = 1.0
            elif j == o:
                masks[o][:, j * 128:(j + 1) * 128] = tri
    c["c_masks"] = masks.reshape(512, 512).astype(ml_dtypes.bfloat16)
    c["c_iden_bf"] = np.eye(128).astype(ml_dtypes.bfloat16)
    c["c_iden_f32"] = np.eye(128).astype(np.float32)
    esel = np.zeros((16, 16 * 128), dtype=np.float32)
    for h in range(16):
        esel[h, h * 128:(h + 1) * 128] = 1.0
    c["c_esel"] = esel
    hs = np.zeros((128, NCH * 16), dtype=np.float32)
    for cc in range(NCH):
        hs[0:64, cc * 16 + 2 * cc] = 1.0
        hs[64:128, cc * 16 + 2 * cc + 1] = 1.0
    c["c_headsum"] = hs.astype(ml_dtypes.bfloat16)
    c["c_edelta"] = np.tile((np.arange(8, dtype=np.float32) * 1e-6)[None, :], (128, 1))
    return c


class Sem:
    def __init__(self, name):
        self.h = None
        self.name = name
        self.n = 0


class Arena:
    def __init__(self, tensor, n, name):
        self.t, self.n, self.name = tensor, n, name
        self.off = 0
        self.marks = []

    def alloc(self, cols):
        assert self.off + cols <= self.n, f"{self.name} arena overflow {self.off}+{cols}>{self.n}"
        ap = self.t[:, self.off:self.off + cols]
        self.off += cols
        return ap

    def push(self):
        self.marks.append(self.off)

    def pop(self):
        self.off = self.marks.pop()


class KB:
    def __init__(self, nc):
        self.nc = nc
        self.ops = []
        self.serial = False
        self.s_all = None

    def _ser_pre(self, eng):
        if self.serial and (self.serial is True or eng in self.serial):
            v = self.s_all.n
            self.ops.append((eng, lambda e, s=self.s_all, v=v: e.wait_ge(s.h, v)))

    def _ser_post(self, eng):
        if self.serial and (self.serial is True or eng in self.serial):
            self.s_all.n += 1
            self.ops.append((eng, lambda e: e.drain()))
            self.ops.append((eng, lambda e, s=self.s_all: e.nop().then_inc(s.h, 1)))

    def on(self, eng, fn):
        self._ser_pre(eng)
        self.ops.append((eng, fn))
        self._ser_post(eng)

    def on_raw(self, eng, fn):
        self.ops.append((eng, fn))

    def incop(self, eng, sem, fn, n=1):
        self._ser_pre(eng)
        self.ops.append((eng, fn))
        self.ops.append((eng, lambda e: e.drain()))
        sem.n += n
        self.ops.append((eng, lambda e, s=sem, n=n: e.nop().then_inc(s.h, n)))
        self._ser_post(eng)

    def mm(self, sem, fn):
        sem.n += 1
        self.on("tensor", lambda e, s=sem, fn=fn: fn(e).then_inc(s.h, 1))

    def mmq(self, fn):
        self.on("tensor", fn)

    def dma(self, eng, out, in_, sem):
        sem.n += 16
        self._ser_pre(eng)
        self.ops.append((eng, lambda e, out=out, in_=in_, s=sem: e.dma_start(out=out, in_=in_).then_inc(s.h, 16)))
        if self.serial and (self.serial is True or eng in self.serial):
            v = sem.n
            self.ops.append((eng, lambda e, s=sem, v=v: e.wait_ge(s.h, v)))
        self._ser_post(eng)

    def dma_tr(self, eng, out, in_, sem):
        sem.n += 16
        self._ser_pre(eng)
        self.ops.append((eng, lambda e, out=out, in_=in_, s=sem: e.dma_start_transpose(out, in_).then_inc(s.h, 16)))
        if self.serial and (self.serial is True or eng in self.serial):
            v = sem.n
            self.ops.append((eng, lambda e, s=sem, v=v: e.wait_ge(s.h, v)))
        self._ser_post(eng)

    def wait(self, eng, sem, val=None):
        v = sem.n if val is None else val
        self.on_raw(eng, lambda e, s=sem, v=v: e.wait_ge(s.h, v))

    def run(self, eng_name, eng):
        for name, fn in self.ops:
            if name == eng_name:
                fn(eng)


def ts(i, n):
    return slice(i * n, (i + 1) * n)


def build_nc(debug=""):
    nc = bass.Bass("TRN2", target_bir_lowering=False, debug=False)

    def EI(name, shape, dtype=F32):
        return nc.dram_tensor(name, shape, dtype, kind="ExternalInput").ap()

    x0 = EI("x0", [NTOK, D])
    w_qkv0 = EI("L0_attn_w_qkv", [D, 3 * D])
    w_out0 = EI("L0_attn_w_out", [D, D])
    gate_w = EI("L0_ffn_gate_w", [MOE_E, D])
    corr_b = EI("L0_ffn_correction_bias", [MOE_E])
    Wg = EI("L0_ffn_Wg", [MOE_E, D, MOE_I])
    Wu = EI("L0_ffn_Wu", [MOE_E, D, MOE_I])
    Wd = EI("L0_ffn_Wd", [MOE_E, MOE_I, D])
    Wg_s = EI("L0_ffn_Wg_s", [D, MOE_I])
    Wu_s = EI("L0_ffn_Wu_s", [D, MOE_I])
    Wd_s = EI("L0_ffn_Wd_s", [MOE_I, D])
    qkv_w1 = EI("L1_attn_qkv_w", [D, 3 * D])
    qkv_b1 = EI("L1_attn_qkv_b", [3 * D])
    out_w1 = EI("L1_attn_out_w", [D, D])
    out_b1 = EI("L1_attn_out_b", [D])
    g_attn = EI("L1_attn_g_attn", [D])
    b_attn = EI("L1_attn_b_attn", [D])
    g1v = EI("L1_attn_g1", [D])
    b1v = EI("L1_attn_b1", [D])
    g2v = EI("L1_attn_g2", [D])
    b2v = EI("L1_attn_b2", [D])
    w_ctx = EI("L1_attn_w_ctx", [D, D])
    w_x = EI("L1_attn_w_x", [D, D])
    b_inl = EI("L1_attn_b_inl", [D])
    ff_w1 = EI("L1_attn_ff_w1", [D, FF])
    ff_b1 = EI("L1_attn_ff_b1", [FF])
    ff_w2 = EI("L1_attn_ff_w2", [FF, D])
    ff_b2 = EI("L1_attn_ff_b2", [D])
    clip_x = EI("L1_ffn_clip_x", [1])
    clip_down = EI("L1_ffn_clip_down", [1])
    w_gate = EI("L1_ffn_w_gate", [QI, D])
    w_up = EI("L1_ffn_w_up", [QI, D])
    w_down = EI("L1_ffn_w_down", [D, QI])
    c_cos = EI("c_cosrep", [T, 1024], BF16)
    c_sin = EI("c_sinrep", [T, 1024], BF16)
    c_masks = EI("c_masks", [512, 512], BF16)
    c_iden_bf = EI("c_iden_bf", [128, 128], BF16)
    c_iden_f32 = EI("c_iden_f32", [128, 128], F32)
    c_esel = EI("c_esel", [16, 16 * 128], F32)
    c_headsum = EI("c_headsum", [128, NCH * 16], BF16)
    c_edelta = EI("c_edelta", [128, 8], F32)

    out = nc.dram_tensor("out", [NTOK, D], F32, kind="ExternalOutput").ap()

    def DS(name, shape, dtype):
        return nc.dram_tensor(name, shape, dtype).ap()

    XN0FM = DS("xn0fm", [NT128, 128, D], BF16)
    KFM = DS("kfm", [NT128, 128, D], BF16)
    VAUG = DS("vaug", [NT128, 128, 16 * 65], BF16)
    QFM = DS("qfm", [NT128, 128, D], BF16)
    S0 = DS("s0", [NTOK, D], F32)
    X1 = DS("x1", [NTOK, D], F32)
    XN1 = DS("xn1", [NTOK, D], F32)
    AFM = DS("afm", [NT128, 128, D], BF16)
    CTXFM = DS("ctxfm", [NT128, 128, D], BF16)
    Y1 = DS("y1", [NTOK, D], F32)
    Y2 = DS("y2", [NTOK, D], F32)
    Y2FM = DS("y2fm", [NT128, 128, D], BF16)
    HFM = DS("hfm", [NT512, 128, FFC * 512], BF16)
    S1 = DS("s1", [NTOK, D], F32)
    HPRE = DS("hpre", [NT512, 128, QIC * 512], F32)
    WQ1S = DS("wq1s", [NCH, 128, 3 * D], BF16)
    FFW1S = DS("ffw1s", [NCH, 128, FF], BF16)
    B1R_D = DS("b1r_d", [1, 3 * D], BF16)
    OUTB_D = DS("outb_d", [1, D], BF16)
    WQTG = DS("wqtg", [NCH, 128, QIC * 128], BF16)
    WQTU = DS("wqtu", [NCH, 128, QIC * 128], BF16)
    WDT = DS("wdt", [QIC, 128, NCH * 128], BF16)
    CCIN = nc.dram_tensor("ccin", [1, 8], F32)
    CCOUT = nc.dram_tensor("ccout", [1, 8], F32)
    CCIN2 = nc.dram_tensor("ccin2", [1, 8], F32)
    CCOUT2 = nc.dram_tensor("ccout2", [1, 8], F32)

    dbg = {}
    for name in [n for n in debug.split(",") if n]:
        key = "d_" + name
        dt_ = BF16 if name.endswith("fm") else F32
        dbg[key] = nc.dram_tensor(key, [NTOK, D], dt_, kind="ExternalOutput").ap()

    kb = KB(nc)
    import contextlib
    stack = contextlib.ExitStack()

    AB_N, AF_N, ASM_N = 69632, 10240, 1024
    AB = stack.enter_context(nc.sbuf_tensor("ab", [128, AB_N], BF16))
    AFp = stack.enter_context(nc.sbuf_tensor("af", [128, AF_N], F32))
    ASm = stack.enter_context(nc.sbuf_tensor("asm", [128, ASM_N], F32))
    AI = stack.enter_context(nc.sbuf_tensor("ai", [128, 1024], I32))
    arena_bf = Arena(AB, AB_N, "bf")
    arena_f = Arena(AFp, AF_N, "f32")
    arena_s = Arena(ASm, ASM_N, "sm")

    PS = [stack.enter_context(nc.psum_tensor(f"ps{i}", [128, 512], F32)) for i in range(7)]
    PTR = stack.enter_context(nc.psum_tensor("ptr", [128, 256], BF16))

    semnames = ["ldw", "xa", "xb2", "wa", "wb", "ha", "hb", "st", "mm", "ev",
                "dv", "gp", "xb", "cc", "all", "ga", "gb", "gc"]
    sems = {n: Sem(n) for n in semnames}
    for s in sems.values():
        s.h = stack.enter_context(nc.semaphore(s.name))
    kb.s_all = sems["all"]
    _ks = os.environ.get("KSERIAL", "0")
    kb.serial = True if _ks == "1" else (set(_ks.split(",")) if _ks not in ("", "0") else False)
    S_ldw = sems["ldw"]
    S_x2 = [sems["xa"], sems["xb2"]]
    S_w2 = [sems["wa"], sems["wb"]]
    S_h2 = [sems["ha"], sems["hb"]]
    S_st = sems["st"]; S_mm = sems["mm"]; S_ev = sems["ev"]
    S_ga = sems["ga"]; S_gb = sems["gb"]; S_gc = sems["gc"]
    S_dv = sems["dv"]; S_gp = sems["gp"]; S_xb = sems["xb"]; S_cc = sems["cc"]

    ALL_ENG = ["sync", "tensor", "scalar", "vector", "gpsimd"]
    BAR_SEMS = ["ldw", "xa", "xb2", "wa", "wb", "ha", "hb", "st", "mm", "ev",
                "dv", "gp", "xb", "ga", "gb", "gc", "all"]

    def phase_barrier():
        for eng in ALL_ENG:
            for sn in BAR_SEMS:
                s = sems[sn]
                if s.n > 0:
                    kb.wait(eng, s, s.n)

    ones = arena_s.alloc(512)
    iden_f32_sb = arena_s.alloc(128)
    ccst = arena_s.alloc(8)
    iden_bf = arena_bf.alloc(128)
    onesb = arena_bf.alloc(512)
    masks_sb = arena_bf.alloc(4 * 512)
    esel_sb = arena_f.alloc(2048)
    hsum_sb = arena_bf.alloc(NCH * 16)
    edelta_sb = arena_s.alloc(8)
    gcols = arena_f.alloc(48)
    b1acol = arena_f.alloc(32)
    binl_r = arena_bf.alloc(1024)
    ffb2_r = arena_bf.alloc(1024)
    qscal = arena_s.alloc(8)
    swg = arena_f.alloc(QIC)
    swu = arena_f.alloc(QIC)
    swd = arena_f.alloc(8)
    sgx = arena_f.alloc(QIC)
    sux = arena_f.alloc(QIC)
    sdx = arena_f.alloc(8)

    kb.on("gpsimd", lambda e: e.memset(ones[:, :], 1.0))
    kb.on("gpsimd", lambda e: e.memset(onesb[:, :], 1.0))
    kb.on("gpsimd", lambda e: e.drain())
    kb.on_raw("gpsimd", lambda e: e.nop().then_inc(S_gp.h, 1)); S_gp.n += 1
    kb.dma("sync", iden_bf, c_iden_bf, S_ldw)
    kb.dma("sync", iden_f32_sb, c_iden_f32, S_ldw)
    for o_ in range(4):
        kb.dma("sync", masks_sb[:, ts(o_, 512)], c_masks[ts(o_, 128), :], S_ldw)
    kb.dma("sync", esel_sb[0:16, :], c_esel, S_ldw)
    kb.dma("sync", hsum_sb, c_headsum, S_ldw)
    kb.dma("sync", edelta_sb, c_edelta, S_ldw)
    init_ldw = S_ldw.n
    for eng in ["tensor", "scalar", "vector"]:
        kb.wait(eng, S_ldw, init_ldw)
        kb.wait(eng, S_gp, S_gp.n)

    def emit_ln(xtile, stat, gate_scalar_all=True):
        st12, mv = stat[:, 0:12], stat[:, 12:14]
        veps, sq_, rs_, nmr = stat[:, 14:15], stat[:, 15:16], stat[:, 16:17], stat[:, 17:18]
        if gate_scalar_all:
            kb.wait("vector", S_ev)
        kb.on("vector", lambda e, o=st12[:, 0:6], i=xtile[:, 0:512]: e.bn_stats(o, i))
        kb.on("vector", lambda e, o=st12[:, 6:12], i=xtile[:, 512:1024]: e.bn_stats(o, i))
        kb.on("vector", lambda e: e.drain())
        kb.on("vector", lambda e, o=mv, i=st12: e.bn_aggr(o, i))
        kb.on("vector", lambda e: e.drain())
        kb.incop("vector", S_dv, lambda e, o=veps, i=mv:
                 e.tensor_scalar(o, i[:, 1:2], EPS, 0.0, ALU.add, ALU.add))
        kb.wait("scalar", S_dv)
        kb.incop("scalar", S_ev, lambda e, o=sq_, i=veps:
                 e.activation(o, i, AF.Sqrt, bias=0.0, scale=1.0))
        kb.wait("vector", S_ev)
        kb.on("vector", lambda e, o=rs_, i=sq_: e.reciprocal(o, i))
        kb.on("vector", lambda e: e.drain())
        kb.incop("vector", S_dv, lambda e, o=nmr, i=mv, r=rs_:
                 e.tensor_scalar(o, i[:, 0:1], r[:, 0:1], -1.0, ALU.mult, ALU.mult))
        kb.wait("scalar", S_dv)
        return rs_, nmr, mv

    def quant_chain(dst_bf, src_f32, rs_ap, width, tmpa, tmpsgn):
        kb.wait("scalar", S_dv)
        kb.incop("scalar", S_ev, lambda e, o=tmpa[:, 0:width], i=src_f32, r=rs_ap:
                 e.activation(o, i, AF.Abs, bias=0.0, scale=r))
        kb.incop("scalar", S_ev, lambda e, o=tmpsgn[:, 0:width], i=src_f32:
                 e.activation(o, i, AF.Sign, bias=0.0, scale=1.0))
        kb.wait("vector", S_ev)
        off = 0
        while off < width:
            w = min(1024, width - off)
            kb.on("vector", lambda e, o=AI[:, 0:w], i=tmpa[:, off:off + w]:
                  e.tensor_scalar(o, i, 128.49, 0.0, ALU.min, ALU.max))
            kb.on("vector", lambda e: e.drain())
            kb.on("vector", lambda e, o=tmpa[:, off:off + w], i=AI[:, 0:w]:
                  e.tensor_copy(o, i))
            kb.on("vector", lambda e: e.drain())
            off += w
        kb.on("vector", lambda e, o=tmpa[:, 0:width], a=tmpa[:, 0:width], b=tmpsgn[:, 0:width]:
              e.tensor_tensor(o, a, b, ALU.mult))
        kb.on("vector", lambda e: e.drain())
        kb.incop("vector", S_dv, lambda e, o=dst_bf, i=tmpa[:, 0:width]:
                 e.tensor_scalar(o, i, 127.0, -128.0, ALU.min, ALU.max))

    def phase_p0():
        ph_serial("p0")
        phase_barrier()
        arena_bf.push(); arena_f.push(); arena_s.push()
        xt = [arena_f.alloc(1024) for _ in range(2)]
        xnt = [arena_bf.alloc(1024) for _ in range(2)]
        xfm = [arena_bf.alloc(1024) for _ in range(2)]
        stat2 = [arena_f.alloc(18) for _ in range(2)]
        xb_done = []
        for t in range(NT128):
            bi = t % 2
            if t >= 2:
                kb.wait("sync", S_xb)
            kb.dma("sync", xt[bi], x0[ts(t, 128), :], S_x2[bi])
            kb.wait("vector", S_x2[bi])
            rs_, nmr, _ = emit_ln(xt[bi], stat2[bi])
            if t >= 2:
                kb.wait("scalar", S_xb, xb_done[t - 2])
            kb.incop("scalar", S_ev, lambda e, o=xnt[bi], i=xt[bi], r=rs_, b=nmr:
                     e.activation(o, i, AF.Identity, bias=b[:, 0:1], scale=r[:, 0:1]))
            kb.wait("sync", S_ev)
            for c in range(NCH):
                kb.dma_tr("sync", xfm[bi][:, ts(c, 128)], xnt[bi][:, ts(c, 128)], S_xb)
            xb_done.append(S_xb.n)
            kb.wait("sync", S_xb)
            kb.dma("sync", XN0FM[t, :, :], xfm[bi], S_xb)
        kb.wait("sync", S_xb)
        arena_bf.pop(); arena_f.pop(); arena_s.pop()

    KPH = int(os.environ.get("KPHASES", "99"))
    KSERP = set(filter(None, os.environ.get(
        "KSERP", "-").split(",")))
    _base_serial = kb.serial

    def ph_serial(name):
        kb.serial = True if (name in KSERP) else _base_serial

    def out_stub():
        arena_f.push()
        t0_ = arena_f.alloc(1024)
        phase_barrier()
        for t in range(NT128):
            kb.dma("sync", t0_, x0[ts(t, 128), :], S_x2[0])
            kb.wait("sync", S_x2[0])
            kb.dma("sync", out[ts(t, 128), :], t0_, S_st)
            kb.wait("sync", S_st)
        arena_f.pop()

    phase_p0()

    def phase_attn(layer):
        ph_serial("attn%d" % layer)
        phase_barrier()
        arena_bf.push(); arena_f.push(); arena_s.push()
        XIN = XN0FM if layer == 0 else AFM
        qn2 = arena_f.alloc(2048)
        kn2 = arena_f.alloc(2048)
        nstat = arena_f.alloc(16)
        mb_sb = arena_f.alloc(64)
        norm_fin = [0]
        b1r = arena_bf.alloc(3 * D) if layer == 1 else None
        outb_r = arena_bf.alloc(D) if layer == 1 else None
        wout_sb = arena_bf.alloc(NCH * D)
        WOUT = w_out0 if layer == 0 else out_w1
        kb.wait("gpsimd", S_mm)
        for c in range(NCH):
            kb.dma("gpsimd", wout_sb[:, ts(c, D)], WOUT[ts(c, 128), :], S_ga)
        if layer == 1:
            kb.dma("sync", b1r[0:1, :], B1R_D[0:1, :], S_ldw)
            kb.dma("sync", outb_r[0:1, :], OUTB_D[0:1, :], S_ldw)

        for s in range(SEQ_PER_CORE):
            arena_bf.push(); arena_f.push(); arena_s.push()
            wq = arena_bf.alloc(NCH * 3 * D)
            xnfm = [arena_bf.alloc(1024) for _ in range(2)]
            if layer == 0:
                costl = [arena_bf.alloc(1024) for _ in range(2)]
                sintl = [arena_bf.alloc(1024) for _ in range(2)]
            qkst = [arena_bf.alloc(2048) for _ in range(2)]
            if layer == 0:
                rotst = [arena_bf.alloc(2048) for _ in range(2)]
                rtmp = arena_bf.alloc(1024)
            vst = [arena_bf.alloc(1024) for _ in range(2)]
            fmst = [arena_bf.alloc(1024) for _ in range(2)]
            vaugst = [arena_bf.alloc(16 * 65) for _ in range(2)]
            sqst = arena_bf.alloc(1024)
            kb.wait("gpsimd", S_mm)
            for cch in range(NCH):
                if layer == 0:
                    kb.dma("gpsimd", wq[:, ts(cch, 3 * D)], w_qkv0[ts(cch, 128), :], S_ga)
                else:
                    kb.dma("gpsimd", wq[:, ts(cch, 3 * D)], WQ1S[cch, :, :], S_ga)
            kb.wait("tensor", S_ga)
            kb.wait("tensor", S_ldw)
            evac_done = []
            for t in range(16):
                g = s * 16 + t
                bi = t % 2
                if t >= 2:
                    kb.wait("sync", S_ev)
                    kb.wait("sync", S_dv)
                kb.dma("sync", xnfm[bi], XIN[g, :, :], S_x2[bi])
                if layer == 0:
                    kb.dma("sync", costl[bi], c_cos[ts(t, 128), :], S_x2[bi])
                    kb.dma("sync", sintl[bi], c_sin[ts(t, 128), :], S_x2[bi])
                kb.wait("tensor", S_x2[bi])
                for n in range(6):
                    pb = PS[n % 2]
                    if n >= 2:
                        kb.wait("tensor", S_ev, evac_done[-2])
                    for cch in range(NCH):
                        last = (cch == NCH - 1) and layer == 0
                        if last:
                            kb.mm(S_mm, lambda e, o=pb[:, 0:512], l=xnfm[bi][:, ts(cch, 128)],
                                  r=wq[:, cch * 3 * D + n * 512: cch * 3 * D + (n + 1) * 512],
                                  st=(cch == 0): e.matmul(o, l, r, start=st, stop=True))
                        else:
                            kb.mmq(lambda e, o=pb[:, 0:512], l=xnfm[bi][:, ts(cch, 128)],
                                   r=wq[:, cch * 3 * D + n * 512: cch * 3 * D + (n + 1) * 512],
                                   st=(cch == 0), sp=False: e.matmul(o, l, r, start=st, stop=sp))
                    if layer == 1:
                        kb.mm(S_mm, lambda e, o=pb[:, 0:512], l=onesb[0:1, 0:128],
                              r=b1r[0:1, ts(n, 512)]: e.matmul(o, l, r, start=False, stop=True))
                    kb.wait("scalar", S_mm)
                    if n < 4:
                        dst = qkst[bi][:, ts(n, 512)]
                    else:
                        dst = vst[bi][:, ts(n - 4, 512)]
                    kb.incop("scalar", S_ev, lambda e, o=dst, i=pb[:, 0:512]:
                             e.activation(o, i, AF.Identity, bias=0.0, scale=1.0))
                    evac_done.append(S_ev.n)
                if layer == 0:
                    kb.wait("vector", S_ev)
                    kb.wait("vector", S_x2[bi])
                    for half in range(2):
                        qf = qkst[bi][:, half * 1024: half * 1024 + 1024]
                        rf = rotst[bi][:, half * 1024: half * 1024 + 1024]
                        r_ = rf.rearrange("p (h x) -> p h x", x=64)
                        tv3 = rtmp[:, 0:1024].rearrange("p (h x) -> p h x", x=64)
                        kb.on("vector", lambda e, o=rf, a=qf, b=costl[bi]:
                              e.tensor_tensor(o, a, b, ALU.mult))
                        kb.on("vector", lambda e, o=rtmp[:, 0:1024], a=qf, b=sintl[bi]:
                              e.tensor_tensor(o, a, b, ALU.mult))
                        kb.on("vector", lambda e: e.drain())
                        kb.on("vector", lambda e, o=r_[:, :, 0:32], a=r_[:, :, 0:32], b=tv3[:, :, 32:64]:
                              e.tensor_tensor(o, a, b, ALU.subtract))
                        kb.on("vector", lambda e, o=r_[:, :, 32:64], a=r_[:, :, 32:64], b=tv3[:, :, 0:32]:
                              e.tensor_tensor(o, a, b, ALU.add))
                        kb.on("vector", lambda e: e.drain())
                    rot = rotst[bi]
                else:
                    rot = qkst[bi]
                kb.wait("vector", S_ev)
                kb.wait("vector", S_xb)
                vdst = vaugst[bi].rearrange("p (h x) -> p h x", x=65)[:, :, 0:64]
                vsrc = vst[bi].rearrange("p (h x) -> p h x", x=64)
                kb.on("vector", lambda e, o=vdst, i=vsrc: e.tensor_copy(o, i))
                vones = vaugst[bi].rearrange("p (h x) -> p h x", x=65)[:, :, 64:65]
                kb.incop("vector", S_dv, lambda e, o=vones: e.memset(o, 1.0))
                kb.wait("sync", S_dv)
                kb.dma("sync", VAUG[g, :, :], vaugst[bi], S_xb)
                for c in range(NCH):
                    kb.dma_tr("sync", fmst[bi][:, ts(c, 128)], rot[:, ts(c, 128)], S_xb)
                xb_q = S_xb.n
                kb.wait("sync", S_xb)
                kb.dma("sync", QFM[g, :, :], fmst[bi], S_xb)
                kb.wait("scalar", S_xb, xb_q)
                kb.incop("scalar", S_ev, lambda e, o=sqst, i=fmst[bi]:
                         e.activation(o, i, AF.Square, bias=0.0, scale=1.0))
                kb.wait("tensor", S_ev)
                for c in range(NCH):
                    if c == NCH - 1:
                        kb.mm(S_mm, lambda e, o=PS[2][0:16, 0:128], l=hsum_sb[:, ts(c, 16)],
                              r=sqst[:, ts(c, 128)], st=(c == 0): e.matmul(o, l, r, start=st, stop=True))
                    else:
                        kb.mmq(lambda e, o=PS[2][0:16, 0:128], l=hsum_sb[:, ts(c, 16)],
                               r=sqst[:, ts(c, 128)], st=(c == 0): e.matmul(o, l, r, start=st, stop=False))
                kb.wait("scalar", S_mm)
                if s > 0 and t == 0:
                    kb.wait("scalar", S_dv, norm_fin[0])
                kb.incop("scalar", S_ev, lambda e, o=qn2[0:16, ts(t, 128)], i=PS[2][0:16, 0:128]:
                         e.activation(o, i, AF.Identity, bias=0.0, scale=1.0))
                kb.wait("sync", S_ev)
                kb.wait("sync", S_xb)
                for c in range(NCH):
                    kb.dma_tr("sync", fmst[bi][:, ts(c, 128)],
                              rot[:, 1024 + c * 128: 1024 + (c + 1) * 128], S_xb)
                xb_k = S_xb.n
                kb.wait("sync", S_xb)
                kb.dma("sync", KFM[g, :, :], fmst[bi], S_xb)
                kb.wait("scalar", S_xb, xb_k)
                kb.incop("scalar", S_ev, lambda e, o=sqst, i=fmst[bi]:
                         e.activation(o, i, AF.Square, bias=0.0, scale=1.0))
                kb.wait("tensor", S_ev)
                for c in range(NCH):
                    if c == NCH - 1:
                        kb.mm(S_mm, lambda e, o=PS[2][0:16, 128:256], l=hsum_sb[:, ts(c, 16)],
                              r=sqst[:, ts(c, 128)], st=(c == 0): e.matmul(o, l, r, start=st, stop=True))
                    else:
                        kb.mmq(lambda e, o=PS[2][0:16, 128:256], l=hsum_sb[:, ts(c, 16)],
                               r=sqst[:, ts(c, 128)], st=(c == 0): e.matmul(o, l, r, start=st, stop=False))
                kb.wait("scalar", S_mm)
                kb.incop("scalar", S_ev, lambda e, o=kn2[0:16, ts(t, 128)], i=PS[2][0:16, 128:256]:
                         e.activation(o, i, AF.Identity, bias=0.0, scale=1.0))
            arena_bf.pop(); arena_f.pop(); arena_s.pop()

            kb.wait("vector", S_ev)
            for g4 in range(4):
                kb.on("vector", lambda e, o=nstat[0:16, g4:g4 + 1], i=qn2[0:16, ts(g4, 512)]:
                      e.tensor_reduce(o, i, AX.X, ALU.max))
            kb.on("vector", lambda e, o=nstat[0:16, 4:5], i=kn2[0:16, 0:2048]:
                  e.tensor_reduce(o, i, AX.X, ALU.max))
            kb.on("vector", lambda e: e.drain())
            kb.incop("vector", S_dv, lambda e: e.nop())
            kb.wait("scalar", S_dv)
            kb.incop("scalar", S_ev, lambda e, o=nstat[0:16, 5:10], i=nstat[0:16, 0:5]:
                     e.activation(o, i, AF.Sqrt, bias=0.0, scale=1.0))
            kb.wait("vector", S_ev)
            kb.incop("vector", S_dv, lambda e, o=nstat[0:16, 10:14], i=nstat[0:16, 5:9],
                     k=nstat[0:16, 9:10]:
                     e.tensor_scalar(o, i, k, -0.125, ALU.mult, ALU.mult))
            norm_fin[0] = S_dv.n
            kb.wait("tensor", S_dv)
            for h in range(16):
                if h == 15:
                    kb.mm(S_mm, lambda e, o=PS[2][:, ts(h, 4)], l=esel_sb[0:16, ts(h, 128)],
                          r=nstat[0:16, 10:14]: e.matmul(o, l, r, start=True, stop=True))
                else:
                    kb.mmq(lambda e, o=PS[2][:, ts(h, 4)], l=esel_sb[0:16, ts(h, 128)],
                           r=nstat[0:16, 10:14]: e.matmul(o, l, r, start=True, stop=True))
            kb.wait("scalar", S_mm)
            kb.incop("scalar", S_ev, lambda e, o=mb_sb[:, 0:64], i=PS[2][:, 0:64]:
                     e.activation(o, i, AF.Identity, bias=0.0, scale=1.0))

            arena_bf.push(); arena_f.push(); arena_s.push()
            Khalf = arena_bf.alloc(16 * 1024)
            Vhalf = arena_bf.alloc(16 * 1040)
            qsl = arena_bf.alloc(8 * 512)
            pt_b = [arena_bf.alloc(512) for _ in range(4)]
            o_half = arena_bf.alloc(512)
            o_tok = arena_bf.alloc(4096)
            o_fm = arena_bf.alloc(4096)
            rlb = arena_f.alloc(512)
            lnl = arena_f.alloc(512)
            xn1t = arena_f.alloc(1024) if layer == 1 else None
            s0t = arena_f.alloc(1024)
            ctxt = arena_bf.alloc(1024) if layer == 1 else None
            ctxfmst = arena_bf.alloc(1024) if layer == 1 else None
            kb.wait("sync", S_mm)
            kb.wait("sync", S_xb)
            kb.dma("sync", Khalf, KFM[s * 16: s * 16 + 16, :, :].rearrange("t p x -> p t x"),
                   S_w2[0])
            kb.dma("sync", Vhalf, VAUG[s * 16: s * 16 + 16, :, :].rearrange("t p x -> p t x"),
                   S_w2[0])
            kb.wait("tensor", S_w2[0])
            for g4 in range(4):
                kb.wait("sync", S_mm)
                kb.dma("sync", qsl,
                       QFM[s * 16 + g4 * 4: s * 16 + g4 * 4 + 4, :, :]
                       .rearrange("t p (c x) -> p c t x", x=128), S_x2[g4 % 2])
                kb.wait("tensor", S_x2[g4 % 2])
                nkt = 4 * g4 + 4
                if True:
                    for hp in range(8):
                        hA, hB = 2 * hp, 2 * hp + 1
                        poA, poB = PS[5], PS[6]
                        kb.wait("tensor", S_ev)
                        kb.wait("tensor", S_dv)
                        exp_done = []
                        for kt in range(0, nkt):
                            kl = kt
                            pa = PS[3] if kt % 2 == 0 else PS[0]
                            pb2 = PS[4] if kt % 2 == 0 else PS[1]
                            if len(exp_done) >= 2:
                                kb.wait("tensor", S_ev, exp_done[-2])
                            for (hh, pp) in ((hA, pa), (hB, pb2)):
                                base = (hh % 2) * 64
                                cch = hh // 2
                                kb.mm(S_mm, lambda e, o=pp[:, 0:512],
                                      l=Khalf[base:base + 64, kl * 1024 + cch * 128: kl * 1024 + (cch + 1) * 128],
                                      r=qsl[base:base + 64, cch * 512: (cch + 1) * 512]:
                                      e.matmul(o, l, r, start=True, stop=True))
                            kb.wait("scalar", S_mm)
                            for (idx, (hh, pp)) in enumerate(((hA, pa), (hB, pb2))):
                                pt = pt_b[(kt % 2) * 2 + idx]
                                kb.incop("scalar", S_ev, lambda e, o=pt, i=pp[:, 0:512],
                                         b=mb_sb[:, hh * 4 + g4: hh * 4 + g4 + 1]:
                                         e.activation(o, i, AF.Exp, bias=b, scale=0.125))
                            if kt >= 4 * g4:
                                o_ = kt - 4 * g4
                                kb.wait("vector", S_ev)
                                for idx in range(2):
                                    pt = pt_b[(kt % 2) * 2 + idx]
                                    kb.incop("vector", S_dv, lambda e, o=pt, a=pt,
                                             b=masks_sb[:, ts(o_, 512)]:
                                             e.tensor_tensor(o, a, b, ALU.mult))
                                kb.wait("tensor", S_dv)
                            else:
                                kb.wait("tensor", S_ev)
                            exp_done.append(S_ev.n)
                            for (idx, (hh, po)) in enumerate(((hA, poA), (hB, poB))):
                                pt = pt_b[(kt % 2) * 2 + idx]
                                kb.mm(S_mm, lambda e, o=po[0:65, 0:512],
                                      l=Vhalf[:, kl * 1040 + hh * 65: kl * 1040 + (hh + 1) * 65],
                                      r=pt, st=(kt == 0), sp=(kt == nkt - 1):
                                      e.matmul(o, l, r, start=st, stop=sp))
                        if True:
                            kb.wait("scalar", S_mm)
                            for (hh, po) in ((hA, poA), (hB, poB)):
                                kb.wait("scalar", S_dv)
                                kb.incop("scalar", S_ev, lambda e, o=lnl[64:65, 0:512], i=po[64:65, 0:512]:
                                         e.activation(o, i, AF.Ln, bias=0.0, scale=1.0))
                                kb.incop("scalar", S_ev, lambda e, o=lnl[64:65, 0:512], i=lnl[64:65, 0:512]:
                                         e.activation(o, i, AF.Exp, bias=0.0, scale=-1.0))
                                kb.wait("tensor", S_ev)
                                kb.mm(S_mm, lambda e, o=PS[2][0:64, 0:512], l=ones[64:65, 0:64],
                                      r=lnl[64:65, 0:512]: e.matmul(o, l, r, start=True, stop=True))
                                kb.wait("scalar", S_mm)
                                kb.incop("scalar", S_ev, lambda e, o=rlb[0:64, :], i=PS[2][0:64, 0:512]:
                                         e.activation(o, i, AF.Identity, bias=0.0, scale=1.0))
                                kb.wait("vector", S_ev)
                                kb.wait("vector", S_xb)
                                kb.incop("vector", S_dv, lambda e, o=o_half[0:64, 0:512],
                                         a=po[0:64, 0:512], b=rlb[0:64, :]:
                                         e.tensor_tensor(o, a, b, ALU.mult))
                                kb.wait("sync", S_dv)
                                for tt in range(4):
                                    kb.dma_tr("sync",
                                              o_tok[:, tt * 1024 + hh * 64: tt * 1024 + (hh + 1) * 64],
                                              o_half[0:64, ts(tt, 128)], S_xb)
                for tt in range(4):
                    g = s * 16 + g4 * 4 + tt
                    kb.wait("tensor", S_ev)
                    kb.wait("tensor", S_dv)
                    kb.wait("sync", S_xb)
                    for c in range(NCH):
                        kb.dma_tr("sync", o_fm[:, tt * 1024 + c * 128: tt * 1024 + (c + 1) * 128],
                                  o_tok[:, tt * 1024 + c * 128: tt * 1024 + (c + 1) * 128], S_xb)
                    kb.wait("tensor", S_xb)
                    for n in range(2):
                        pb2 = PS[3 + n]
                        for c in range(NCH):
                            last = (c == NCH - 1) and layer == 0
                            if last:
                                kb.mm(S_mm, lambda e, o=pb2[:, 0:512],
                                      l=o_fm[:, tt * 1024 + c * 128: tt * 1024 + (c + 1) * 128],
                                      r=wout_sb[:, c * 1024 + n * 512: c * 1024 + (n + 1) * 512],
                                      st=(c == 0): e.matmul(o, l, r, start=st, stop=True))
                            else:
                                kb.mmq(lambda e, o=pb2[:, 0:512],
                                       l=o_fm[:, tt * 1024 + c * 128: tt * 1024 + (c + 1) * 128],
                                       r=wout_sb[:, c * 1024 + n * 512: c * 1024 + (n + 1) * 512],
                                       st=(c == 0): e.matmul(o, l, r, start=st, stop=False))
                        if layer == 1:
                            kb.mm(S_mm, lambda e, o=pb2[:, 0:512], l=onesb[0:1, 0:128],
                                  r=outb_r[0:1, ts(n, 512)]: e.matmul(o, l, r, start=False, stop=True))
                    kb.wait("scalar", S_mm)
                    if layer == 0 or os.environ.get("KL1SIMPLE"):
                        kb.wait("scalar", S_st)
                        for n in range(2):
                            kb.incop("scalar", S_ev, lambda e, o=s0t[:, ts(n, 512)], i=PS[3 + n][:, 0:512]:
                                     e.activation(o, i, AF.Identity, bias=0.0, scale=1.0))
                        kb.wait("sync", S_ev)
                        kb.dma("sync", (S0 if layer == 0 else Y1)[ts(g, 128), :], s0t, S_st)
                    else:
                        kb.dma("sync", xn1t, XN1[ts(g, 128), :], S_h2[tt % 2])
                        kb.wait("vector", S_h2[tt % 2])
                        kb.wait("vector", S_mm)
                        kb.wait("vector", S_st)
                        for n in range(2):
                            kb.incop("vector", S_dv, lambda e, o=s0t[:, ts(n, 512)],
                                     a=PS[3 + n][:, 0:512], b=xn1t[:, ts(n, 512)]:
                                     e.tensor_tensor(o, a, b, ALU.add))
                        kb.wait("sync", S_dv)
                        kb.dma("sync", Y1[ts(g, 128), :], s0t, S_st)
                        kb.wait("scalar", S_mm)
                        kb.wait("scalar", S_xb)
                        kb.wait("scalar", S_dv)
                        for n in range(2):
                            kb.incop("scalar", S_ev, lambda e, o=ctxt[:, ts(n, 512)], i=PS[3 + n][:, 0:512]:
                                     e.activation(o, i, AF.Identity, bias=0.0, scale=1.0))
                        kb.wait("sync", S_ev)
                        for c in range(NCH):
                            kb.dma_tr("sync", ctxfmst[:, ts(c, 128)], ctxt[:, ts(c, 128)], S_xb)
                        kb.wait("sync", S_xb)
                        kb.dma("sync", CTXFM[g, :, :], ctxfmst, S_xb)
            arena_bf.pop(); arena_f.pop(); arena_s.pop()
        arena_bf.pop(); arena_f.pop(); arena_s.pop()

    if KPH >= 2:
        phase_attn(0)
    if "d_s0" in dbg:
        kb.wait("sync", S_st)
        for t_ in range(NT128):
            kb.dma("sync", dbg["d_s0"][ts(t_, 128), :], S0[ts(t_, 128), :], S_st)

    def phase_moe():
        ph_serial("moe")
        phase_barrier()
        arena_bf.push(); arena_f.push(); arena_s.push()
        gwT = arena_bf.alloc(64)
        gwt_tmp = arena_bf.alloc(1024)
        cbb = arena_f.alloc(8)
        cbrow = arena_f.alloc(8)
        xnfm4 = [arena_bf.alloc(4096) for _ in range(2)]
        wg_sb = arena_bf.alloc(4096)
        wu_sb = arena_bf.alloc(4096)
        wd_sb = [arena_bf.alloc(1024) for _ in range(2)]
        act_sb = arena_bf.alloc(36 * 512)
        sg_sb = arena_f.alloc(512)
        scrow = arena_f.alloc(512)
        cw_fm = arena_f.alloc(512)
        cwb_sb = arena_f.alloc(512)
        sctok = arena_f.alloc(8)
        raw = arena_f.alloc(8)
        gs4 = arena_f.alloc(4)
        gm = arena_f.alloc(8)
        m8 = arena_f.alloc(8)
        sfo = arena_f.alloc(8)
        sfw = arena_f.alloc(8)
        wsel = arena_f.alloc(8)
        rr = arena_f.alloc(8)
        t1_sb = arena_bf.alloc(512)
        g0_sb = arena_f.alloc(1024)
        x0t = arena_f.alloc(1024)
        s0t2 = arena_f.alloc(1024)
        x1t = arena_f.alloc(1024)
        stat = arena_f.alloc(18)
        satt = arena_f.alloc(8)
        xn1bf = arena_bf.alloc(1024)
        xfmst = arena_bf.alloc(1024)
        runmax = arena_s.alloc(4)
        kb.on("gpsimd", lambda e: e.memset(runmax[:, :], 0.0))
        kb.on_raw("gpsimd", lambda e: e.drain())
        kb.on_raw("gpsimd", lambda e: e.nop().then_inc(S_gp.h, 1)); S_gp.n += 1
        kb.wait("vector", S_gp)

        kb.dma("gpsimd", gwt_tmp[0:8, :], gate_w, S_ga)
        kb.dma("sync", cbrow[0:1, 0:8], corr_b.unsqueeze(0), S_ldw)
        kb.wait("tensor", S_ga)
        kb.wait("tensor", S_ldw)
        for c in range(NCH):
            kb.wait("tensor", S_ev)
            kb.mm(S_mm, lambda e, o=PTR[:, 0:8], l=gwt_tmp[0:8, ts(c, 128)], r=iden_bf[0:8, 0:8]:
                  e.transpose(o, l, r))
            kb.wait("scalar", S_mm)
            kb.incop("scalar", S_ev, lambda e, o=gwT[:, ts(c, 8)], i=PTR[:, 0:8]:
                     e.activation(o, i, AF.Identity, bias=0.0, scale=1.0))
        kb.wait("tensor", S_ev)
        kb.mm(S_mm, lambda e, o=PS[2][:, 0:8], l=ones[0:1, 0:128], r=cbrow[0:1, 0:8]:
              e.matmul(o, l, r, start=True, stop=True))
        kb.wait("scalar", S_mm)
        kb.incop("scalar", S_ev, lambda e, o=cbb, i=PS[2][:, 0:8]:
                 e.activation(o, i, AF.Identity, bias=0.0, scale=1.0))
        kb.wait("vector", S_ev)

        for g5 in range(NT512):
            bi = g5 % 2
            kb.wait("sync", S_mm)
            kb.dma("sync", xnfm4[bi],
                   XN0FM[g5 * 4: g5 * 4 + 4, :, :].rearrange("t p (c x) -> p c t x", x=128),
                   S_x2[bi])
            kb.wait("tensor", S_x2[bi])
            kb.wait("tensor", S_ev)
            for c in range(NCH):
                if c == NCH - 1:
                    kb.mm(S_mm, lambda e, o=PS[2][0:8, 0:512], l=gwT[:, ts(c, 8)],
                          r=xnfm4[bi][:, ts(c, 512)], st=(c == 0): e.matmul(o, l, r, start=st, stop=True))
                else:
                    kb.mmq(lambda e, o=PS[2][0:8, 0:512], l=gwT[:, ts(c, 8)],
                           r=xnfm4[bi][:, ts(c, 512)], st=(c == 0): e.matmul(o, l, r, start=st, stop=False))
            kb.wait("scalar", S_mm)
            kb.wait("scalar", S_dv)
            kb.incop("scalar", S_ev, lambda e, o=scrow[0:8, :], i=PS[2][0:8, 0:512]:
                     e.activation(o, i, AF.Sigmoid, bias=0.0, scale=1.0))
            for st_ in range(4):
                kb.wait("tensor", S_ev)
                kb.wait("tensor", S_dv)
                kb.mm(S_mm, lambda e, o=PS[5][:, 0:8], l=scrow[0:8, ts(st_, 128)],
                      r=iden_f32_sb[0:8, 0:8]: e.transpose(o, l, r))
                kb.wait("vector", S_mm)
                kb.on("vector", lambda e, o=raw, i=PS[5][:, 0:8]: e.tensor_copy(o, i))
                kb.on("vector", lambda e: e.drain())
                kb.on("vector", lambda e, o=sctok, a=raw, b=cbb: e.tensor_tensor(o, a, b, ALU.add))
                kb.on("vector", lambda e: e.drain())
                kb.on("vector", lambda e, o=gs4, i=sctok.rearrange("p (g x) -> p g x", x=2):
                      e.tensor_reduce(o, i, AX.X, ALU.add))
                kb.on("vector", lambda e: e.drain())
                kb.on("vector", lambda e, o=rr[:, 0:1], i=gs4: e.tensor_reduce(o, i, AX.X, ALU.max))
                kb.on("vector", lambda e: e.drain())
                kb.on("vector", lambda e, o=gm[:, 0:4], i=gs4, m=rr[:, 0:1]:
                      e.tensor_scalar(o, i, m, 0.0, ALU.is_equal, ALU.add))
                kb.on("vector", lambda e: e.drain())
                kb.on("vector", lambda e, o=gm[:, 4:8], a=gm[:, 0:4], b=gs4:
                      e.scalar_tensor_tensor(o, a, -1e30, b, ALU.mult, ALU.add))
                kb.on("vector", lambda e: e.drain())
                kb.on("vector", lambda e, o=rr[:, 1:2], i=gm[:, 4:8]: e.tensor_reduce(o, i, AX.X, ALU.max))
                kb.on("vector", lambda e: e.drain())
                kb.on("vector", lambda e, o=gm[:, 0:4], i=gs4, m=rr[:, 1:2]:
                      e.tensor_scalar(o, i, m, 0.0, ALU.is_ge, ALU.add))
                kb.on("vector", lambda e: e.drain())
                kb.on("vector", lambda e, o=m8.rearrange("p (g x) -> p g x", x=2)[:, :, 0:1],
                      i=gm[:, 0:4].rearrange("p (g x) -> p g x", x=1): e.tensor_copy(o, i))
                kb.on("vector", lambda e, o=m8.rearrange("p (g x) -> p g x", x=2)[:, :, 1:2],
                      i=gm[:, 0:4].rearrange("p (g x) -> p g x", x=1): e.tensor_copy(o, i))
                kb.on("vector", lambda e: e.drain())
                kb.on("vector", lambda e, o=sfo, a=sctok, b=edelta_sb: e.tensor_tensor(o, a, b, ALU.add))
                kb.on("vector", lambda e: e.drain())
                kb.on("vector", lambda e, o=sfo, a=sfo, b=m8: e.tensor_tensor(o, a, b, ALU.mult))
                kb.on("vector", lambda e: e.drain())
                kb.on("vector", lambda e, o=sfo, a=sfo, b=edelta_sb: e.tensor_tensor(o, a, b, ALU.subtract))
                kb.on("vector", lambda e: e.drain())
                kb.on("vector", lambda e, o=sfw, i=sfo: e.tensor_copy(o, i))
                kb.on("vector", lambda e: e.drain())
                for rep in range(3):
                    kb.on("vector", lambda e, o=rr[:, 2:3], i=sfw: e.tensor_reduce(o, i, AX.X, ALU.max))
                    kb.on("vector", lambda e: e.drain())
                    kb.on("vector", lambda e, o=wsel, i=sfw, m=rr[:, 2:3]:
                          e.tensor_scalar(o, i, m, 0.0, ALU.is_equal, ALU.add))
                    kb.on("vector", lambda e: e.drain())
                    kb.on("vector", lambda e, o=sfw, a=wsel, b=sfw:
                          e.scalar_tensor_tensor(o, a, -1e30, b, ALU.mult, ALU.add))
                    kb.on("vector", lambda e: e.drain())
                kb.on("vector", lambda e, o=rr[:, 3:4], i=sfw: e.tensor_reduce(o, i, AX.X, ALU.max))
                kb.on("vector", lambda e: e.drain())
                kb.on("vector", lambda e, o=wsel, i=sfo, m=rr[:, 3:4]:
                      e.tensor_scalar(o, i, m, 0.0, ALU.is_ge, ALU.add))
                kb.on("vector", lambda e: e.drain())
                kb.on("vector", lambda e, o=wsel, a=wsel, b=raw: e.tensor_tensor(o, a, b, ALU.mult))
                kb.on("vector", lambda e: e.drain())
                kb.on("vector", lambda e, o=rr[:, 4:5], i=wsel: e.tensor_reduce(o, i, AX.X, ALU.add))
                kb.on("vector", lambda e: e.drain())
                kb.incop("vector", S_dv, lambda e, o=rr[:, 5:6], i=rr[:, 4:5]:
                         e.tensor_scalar(o, i, 1e-20, 0.0, ALU.add, ALU.add))
                kb.wait("scalar", S_dv)
                kb.incop("scalar", S_ev, lambda e, o=rr[:, 6:7], i=rr[:, 5:6]:
                         e.activation(o, i, AF.Ln, bias=0.0, scale=1.0))
                kb.incop("scalar", S_ev, lambda e, o=rr[:, 6:7], i=rr[:, 6:7]:
                         e.activation(o, i, AF.Exp, bias=0.0, scale=-1.0))
                kb.wait("vector", S_ev)
                kb.incop("vector", S_dv, lambda e, o=wsel, i=wsel, m=rr[:, 6:7]:
                         e.tensor_scalar(o, i, m, 2.5, ALU.mult, ALU.mult))
                kb.wait("tensor", S_dv)
                kb.mm(S_mm, lambda e, o=PS[5][0:8, 0:128], l=wsel[:, 0:8], r=iden_f32_sb:
                      e.transpose(o, l, r))
                kb.wait("scalar", S_mm)
                kb.incop("scalar", S_ev, lambda e, o=cw_fm[0:8, ts(st_, 128)], i=PS[5][0:8, 0:128]:
                         e.activation(o, i, AF.Identity, bias=0.0, scale=1.0))
            for ee in range(9):
                if ee < 8:
                    srcg, srcu = Wg[ee], Wu[ee]
                else:
                    srcg, srcu = Wg_s, Wu_s
                kb.wait("gpsimd", S_mm)
                for c in range(NCH):
                    kb.dma("gpsimd", wg_sb[:, ts(c, 512)], srcg[ts(c, 128), :], S_ga)
                    kb.dma("gpsimd", wu_sb[:, ts(c, 512)], srcu[ts(c, 128), :], S_ga)
                if ee < 8:
                    kb.wait("tensor", S_ev)
                    kb.wait("tensor", S_dv)
                    kb.mm(S_mm, lambda e, o=PS[2][:, 0:512], l=esel_sb[0:8, ts(ee, 128)],
                          r=cw_fm[0:8, 0:512]: e.matmul(o, l, r, start=True, stop=True))
                    kb.wait("scalar", S_mm)
                    kb.incop("scalar", S_ev, lambda e, o=cwb_sb, i=PS[2][:, 0:512]:
                             e.activation(o, i, AF.Identity, bias=0.0, scale=1.0))
                kb.wait("tensor", S_ga)
                for ic in range(4):
                    kb.wait("tensor", S_ev)
                    kb.wait("tensor", S_dv)
                    for c in range(NCH):
                        kb.mmq(lambda e, o=PS[0][:, 0:512],
                               l=wg_sb[:, c * 512 + ic * 128: c * 512 + (ic + 1) * 128],
                               r=xnfm4[bi][:, ts(c, 512)], st=(c == 0), sp=(c == NCH - 1):
                               e.matmul(o, l, r, start=st, stop=sp))
                    for c in range(NCH):
                        if c == NCH - 1:
                            kb.mm(S_mm, lambda e, o=PS[1][:, 0:512],
                                  l=wu_sb[:, c * 512 + ic * 128: c * 512 + (ic + 1) * 128],
                                  r=xnfm4[bi][:, ts(c, 512)], st=(c == 0):
                                  e.matmul(o, l, r, start=st, stop=True))
                        else:
                            kb.mmq(lambda e, o=PS[1][:, 0:512],
                                   l=wu_sb[:, c * 512 + ic * 128: c * 512 + (ic + 1) * 128],
                                   r=xnfm4[bi][:, ts(c, 512)], st=(c == 0):
                                   e.matmul(o, l, r, start=st, stop=False))
                    kb.wait("scalar", S_mm)
                    kb.wait("scalar", S_dv)
                    kb.incop("scalar", S_ev, lambda e, o=sg_sb, i=PS[0][:, 0:512]:
                             e.activation(o, i, AF.Silu, bias=0.0, scale=1.0))
                    kb.wait("vector", S_ev)
                    kb.wait("vector", S_mm)
                    aslot = act_sb[:, ts(ee * 4 + ic, 512)]
                    if ee < 8:
                        kb.on("vector", lambda e, o=t1_sb, a=PS[1][:, 0:512], b=cwb_sb:
                              e.tensor_tensor(o, a, b, ALU.mult))
                        kb.on("vector", lambda e: e.drain())
                        kb.incop("vector", S_dv, lambda e, o=aslot, a=t1_sb, b=sg_sb:
                                 e.tensor_tensor(o, a, b, ALU.mult))
                    else:
                        kb.incop("vector", S_dv, lambda e, o=aslot, a=PS[1][:, 0:512], b=sg_sb:
                                 e.tensor_tensor(o, a, b, ALU.mult))
            dv_acts = S_dv.n
            wd_done = []
            for half in range(2):
                first = True
                for ee in range(9):
                    for ic in range(4):
                        widx = (ee * 4 + ic) % 2
                        src = Wd[ee, ts(ic, 128), :] if ee < 8 else Wd_s[ts(ic, 128), :]
                        if len(wd_done) >= 2:
                            kb.wait("gpsimd", S_mm, wd_done[-2])
                        gsem = S_gb if widx == 0 else S_gc
                        kb.dma("gpsimd", wd_sb[widx], src, gsem)
                        kb.wait("tensor", gsem)
                        if first:
                            kb.wait("tensor", S_dv, dv_acts)
                            kb.wait("tensor", S_ev)
                            first = False
                        lastgrp = (ee == 8 and ic == 3)
                        for tt in (2 * half, 2 * half + 1):
                            for n in range(2):
                                pb2 = PS[3 + 2 * (tt - 2 * half) + n]
                                if lastgrp and tt == 2 * half + 1 and n == 1:
                                    kb.mm(S_mm, lambda e, o=pb2[:, 0:512],
                                          l=act_sb[:, (ee * 4 + ic) * 512 + tt * 128: (ee * 4 + ic) * 512 + (tt + 1) * 128],
                                          r=wd_sb[widx][:, ts(n, 512)],
                                          st=(ee == 0 and ic == 0): e.matmul(o, l, r, start=st, stop=True))
                                elif tt == 2 * half + 1 and n == 1:
                                    kb.mm(S_mm, lambda e, o=pb2[:, 0:512],
                                          l=act_sb[:, (ee * 4 + ic) * 512 + tt * 128: (ee * 4 + ic) * 512 + (tt + 1) * 128],
                                          r=wd_sb[widx][:, ts(n, 512)],
                                          st=(ee == 0 and ic == 0), sp=lastgrp:
                                          e.matmul(o, l, r, start=st, stop=sp))
                                else:
                                    kb.mmq(lambda e, o=pb2[:, 0:512],
                                           l=act_sb[:, (ee * 4 + ic) * 512 + tt * 128: (ee * 4 + ic) * 512 + (tt + 1) * 128],
                                           r=wd_sb[widx][:, ts(n, 512)],
                                           st=(ee == 0 and ic == 0), sp=lastgrp:
                                           e.matmul(o, l, r, start=st, stop=sp))
                        wd_done.append(S_mm.n)
                kb.wait("scalar", S_mm)
                for tt in (2 * half, 2 * half + 1):
                    g = g5 * 4 + tt
                    kb.wait("scalar", S_dv)
                    for n in range(2):
                        pb2 = PS[3 + 2 * (tt - 2 * half) + n]
                        kb.incop("scalar", S_ev, lambda e, o=g0_sb[:, ts(n, 512)], i=pb2[:, 0:512]:
                                 e.activation(o, i, AF.Silu, bias=0.0, scale=1.0))
                    kb.wait("sync", S_dv)
                    kb.wait("sync", S_st)
                    kb.dma("sync", x0t, x0[ts(g, 128), :], S_h2[0])
                    kb.dma("sync", s0t2, S0[ts(g, 128), :], S_h2[0])
                    kb.wait("vector", S_h2[0])
                    kb.wait("vector", S_ev)
                    kb.on("vector", lambda e, o=s0t2, a=s0t2, b=g0_sb: e.tensor_tensor(o, a, b, ALU.mult))
                    kb.on("vector", lambda e: e.drain())
                    kb.incop("vector", S_dv, lambda e, o=x1t, a=x0t, b=s0t2:
                             e.tensor_tensor(o, a, b, ALU.add))
                    kb.wait("sync", S_dv)
                    kb.dma("sync", X1[ts(g, 128), :], x1t, S_st)
                    rs_, nmr, mv = emit_ln(x1t, stat)
                    kb.wait("scalar", S_st)
                    kb.incop("scalar", S_ev, lambda e, o=s0t2, i=x1t, r=rs_, b=nmr:
                             e.activation(o, i, AF.Identity, bias=b[:, 0:1], scale=r[:, 0:1]))
                    kb.wait("sync", S_ev)
                    kb.dma("sync", XN1[ts(g, 128), :], s0t2, S_st)
                    kb.wait("vector", S_ev)
                    kb.on("vector", lambda e, o=satt[:, 0:1], i=mv, r=rs_:
                          e.tensor_scalar(o, i[:, 1:2], r[:, 0:1], 0.0, ALU.mult, ALU.add))
                    kb.on("vector", lambda e: e.drain())
                    kb.incop("vector", S_dv, lambda e, o=satt[:, 1:2], i=satt[:, 0:1], r=rs_:
                             e.tensor_scalar(o, i, r[:, 0:1], EPS, ALU.mult, ALU.add))
                    kb.wait("scalar", S_dv)
                    kb.incop("scalar", S_ev, lambda e, o=satt[:, 2:3], i=satt[:, 1:2]:
                             e.activation(o, i, AF.Sqrt, bias=0.0, scale=1.0))
                    kb.wait("vector", S_ev)
                    kb.on("vector", lambda e, o=satt[:, 3:4], i=satt[:, 2:3]: e.reciprocal(o, i))
                    kb.on("vector", lambda e: e.drain())
                    kb.on("vector", lambda e, o=satt[:, 4:5], i=rs_, m=satt[:, 3:4]:
                          e.tensor_scalar(o, i, m, 0.0, ALU.mult, ALU.add))
                    kb.on("vector", lambda e: e.drain())
                    kb.incop("vector", S_dv, lambda e, o=satt[:, 5:6], i=nmr, m=satt[:, 3:4]:
                             e.tensor_scalar(o, i, m, 0.0, ALU.mult, ALU.add))
                    kb.wait("scalar", S_dv)
                    kb.incop("scalar", S_ev, lambda e, o=xn1bf, i=x1t, r=satt[:, 4:5], b=satt[:, 5:6]:
                             e.activation(o, i, AF.Identity, bias=b[:, 0:1], scale=r[:, 0:1]))
                    kb.wait("vector", S_dv)
                    kb.on("vector", lambda e, o=satt[:, 6:7], i=s0t2:
                          e.tensor_reduce(o, i, AX.X, ALU.max, apply_absolute_value=True))
                    kb.on("vector", lambda e: e.drain())
                    kb.incop("vector", S_dv, lambda e, o=runmax[:, 0:1], a=runmax[:, 0:1],
                             b=satt[:, 6:7]: e.tensor_tensor(o, a, b, ALU.max))
                    kb.wait("sync", S_ev)
                    kb.wait("sync", S_xb)
                    for c in range(NCH):
                        kb.dma_tr("sync", xfmst[:, ts(c, 128)], xn1bf[:, ts(c, 128)], S_xb)
                    kb.wait("sync", S_xb)
                    kb.dma("sync", AFM[g, :, :], xfmst, S_xb)
        kb.wait("tensor", S_dv)
        kb.mm(S_mm, lambda e, o=PS[2][0:1, 0:128], l=runmax[:, 0:1], r=iden_f32_sb:
              e.transpose(o, l, r))
        kb.wait("vector", S_mm)
        kb.on("vector", lambda e, o=runmax[0:1, 1:2], i=PS[2][0:1, 0:128]:
              e.tensor_reduce(o, i, AX.X, ALU.max))
        kb.on("vector", lambda e: e.drain())
        kb.on("vector", lambda e, o=ccst[0:1, 0:8]: e.memset(o, 0.0))
        kb.on("vector", lambda e: e.drain())
        kb.incop("vector", S_dv, lambda e: e.tensor_copy(ccst[0:1, 0:1], runmax[0:1, 1:2]))
        kb.wait("gpsimd", S_dv)
        kb.dma("gpsimd", CCIN.ap(), ccst[0:1, 0:8], S_gc)
        kb.wait("gpsimd", S_gc)
        S_cc.n += 1
        if os.environ.get("KNOCC"):
            kb.dma("gpsimd", CCOUT.ap(), CCIN.ap(), S_gc)
            kb.on_raw("gpsimd", lambda e: e.nop().then_inc(S_cc.h, 1))
        else:
            kb.on_raw("gpsimd", lambda e: e.collective_compute(
                "AllReduce", ALU.max, replica_groups=[list(range(NCORES))],
                ins=[CCIN.ap().opt()], outs=[CCOUT.ap().opt()]).then_inc(S_cc.h, 1))
        arena_bf.pop(); arena_f.pop(); arena_s.pop()

    if KPH >= 3:
        phase_moe()
    if "d_x1" in dbg:
        kb.wait("sync", S_st)
        for t_ in range(NT128):
            kb.dma("sync", dbg["d_x1"][ts(t_, 128), :], X1[ts(t_, 128), :], S_st)
    if "d_xn1" in dbg:
        kb.wait("sync", S_st)
        for t_ in range(NT128):
            kb.dma("sync", dbg["d_xn1"][ts(t_, 128), :], XN1[ts(t_, 128), :], S_st)

    def phase_l1prep():
        ph_serial("l1prep")
        phase_barrier()
        arena_bf.push(); arena_f.push(); arena_s.push()
        vt = arena_bf.alloc(128)
        bcol_bf = arena_bf.alloc(16)
        wtile = [arena_bf.alloc(3 * D) for _ in range(2)]
        fftile = [arena_bf.alloc(FF) for _ in range(2)]
        b1full = arena_f.alloc(3 * D)
        qkvbrow = arena_f.alloc(3 * D)
        obrow = arena_f.alloc(1024)
        rowtmp = arena_bf.alloc(3 * D)
        ffb1t = arena_bf.alloc(128)
        b2acc = arena_f.alloc(32)
        S_g2p = [S_ga, S_gb]
        vecs = [g_attn, b_attn, g1v, b1v, g2v, b2v]
        for vi, vv in enumerate(vecs):
            kb.wait("tensor", S_ev)
            kb.wait("gpsimd", S_mm)
            kb.dma("gpsimd", vt[0:8, 0:128], vv.rearrange("(c p) -> c p", p=128), S_ga)
            kb.wait("gpsimd", S_ga)
            kb.wait("tensor", S_ga)
            kb.mm(S_mm, lambda e, o=PTR[:, 0:8], l=vt[0:8, 0:128], r=iden_bf[0:8, 0:8]:
                  e.transpose(o, l, r))
            kb.wait("scalar", S_mm)
            kb.incop("scalar", S_ev, lambda e, o=gcols[:, ts(vi, 8)], i=PTR[:, 0:8]:
                     e.activation(o, i, AF.Identity, bias=0.0, scale=1.0))
        kb.wait("vector", S_ev)
        kb.incop("vector", S_dv, lambda e, o=gcols[:, 16:32], i=gcols[:, 16:32]:
                 e.tensor_scalar(o, i, 100.0, 0.0, ALU.mult, ALU.add))
        kb.incop("scalar", S_ev, lambda e, o=bcol_bf[:, 0:8], i=gcols[:, 8:16]:
                 e.activation(o, i, AF.Identity, bias=0.0, scale=1.0))
        kb.incop("scalar", S_ev, lambda e, o=bcol_bf[:, 8:16], i=gcols[:, 40:48]:
                 e.activation(o, i, AF.Identity, bias=0.0, scale=1.0))
        kb.dma("sync", qkvbrow[0:1, :], qkv_b1.unsqueeze(0), S_ldw)
        kb.dma("sync", obrow[0:1, :], out_b1.unsqueeze(0), S_ldw)
        kb.dma("gpsimd", binl_r[0:1, :], b_inl.unsqueeze(0), S_ga)
        kb.wait("gpsimd", S_ga)
        kb.dma("gpsimd", ffb2_r[0:1, :], ff_b2.unsqueeze(0), S_ga)
        kb.wait("gpsimd", S_ga)
        kb.dma("gpsimd", ffb1t[0:32, 0:128], ff_b1.rearrange("(c p) -> c p", p=128), S_ga)
        kb.wait("gpsimd", S_ga)
        ldm = S_ldw.n
        gam = S_ga.n
        kb.on("vector", lambda e, o=b1full[0:1, :]: e.memset(o, 0.0))
        kb.incop("vector", S_dv, lambda e: e.nop())
        wq_done = []
        for c in range(NCH):
            wb = wtile[c % 2]
            if c >= 2:
                kb.wait("gpsimd", S_xb, wq_done[c - 2][0])
                kb.wait("gpsimd", S_mm, wq_done[c - 2][1])
            kb.dma("gpsimd", wb, qkv_w1[ts(c, 128), :], S_g2p[c % 2])
            kb.wait("tensor", S_g2p[c % 2])
            kb.wait("tensor", S_ev)
            for n in range(6):
                kb.wait("tensor", S_dv)
                kb.mm(S_mm, lambda e, o=PS[2][0:1, 0:512], l=bcol_bf[:, c:c + 1],
                      r=wb[:, ts(n, 512)]: e.matmul(o, l, r, start=True, stop=True))
                kb.wait("vector", S_mm)
                kb.incop("vector", S_dv, lambda e, o=b1full[0:1, ts(n, 512)],
                         a=b1full[0:1, ts(n, 512)], b=PS[2][0:1, 0:512]:
                         e.tensor_tensor(o, a, b, ALU.add))
            kb.wait("scalar", S_g2p[c % 2])
            kb.wait("scalar", S_dv)
            kb.wait("scalar", S_mm)
            kb.incop("scalar", S_ev, lambda e, o=wb, i=wb, r=gcols[:, c:c + 1]:
                     e.activation(o, i, AF.Identity, bias=0.0, scale=r))
            kb.wait("sync", S_ev)
            kb.dma("sync", WQ1S[c, :, :], wb, S_xb)
            kb.wait("tensor", S_ev)
            wq_done.append((S_xb.n, S_mm.n))
        kb.wait("vector", S_ldw, ldm)
        kb.on("vector", lambda e, o=b1full[0:1, :], a=b1full[0:1, :], b=qkvbrow[0:1, :]:
              e.tensor_tensor(o, a, b, ALU.add))
        kb.incop("vector", S_dv, lambda e: e.nop())
        kb.wait("scalar", S_dv)
        kb.incop("scalar", S_ev, lambda e, o=rowtmp[0:1, :], i=b1full[0:1, :]:
                 e.activation(o, i, AF.Identity, bias=0.0, scale=1.0))
        kb.wait("sync", S_ev)
        kb.dma("sync", B1R_D[0:1, :], rowtmp[0:1, :], S_xb)
        kb.incop("scalar", S_ev, lambda e, o=rowtmp[0:1, 0:1024], i=obrow[0:1, :]:
                 e.activation(o, i, AF.Identity, bias=0.0, scale=1.0))
        kb.wait("sync", S_ev)
        kb.dma("sync", OUTB_D[0:1, :], rowtmp[0:1, 0:1024], S_xb)
        kb.on("vector", lambda e, o=b2acc[:, 0:32]: e.memset(o, 0.0))
        kb.incop("vector", S_dv, lambda e: e.nop())
        ff_done = []
        for c in range(NCH):
            fb = fftile[c % 2]
            if c >= 2:
                kb.wait("gpsimd", S_xb, ff_done[c - 2][0])
                kb.wait("gpsimd", S_mm, ff_done[c - 2][1])
            kb.dma("gpsimd", fb, ff_w1[ts(c, 128), :], S_g2p[c % 2])
            kb.wait("scalar", S_g2p[c % 2])
            kb.wait("scalar", S_dv)
            kb.incop("scalar", S_ev, lambda e, o=fb, i=fb, r=gcols[:, 32 + c:33 + c]:
                     e.activation(o, i, AF.Identity, bias=0.0, scale=r))
            kb.wait("sync", S_ev)
            kb.dma("sync", FFW1S[c, :, :], fb, S_xb)
            kb.wait("tensor", S_ev)
            for fcc in range(FFC):
                kb.wait("tensor", S_dv)
                kb.mm(S_mm, lambda e, o=PS[2][0:128, 0:1], l=fb[:, ts(fcc, 128)],
                      r=bcol_bf[:, 8 + c:9 + c]: e.matmul(o, l, r, start=True, stop=True))
                kb.wait("vector", S_mm)
                kb.incop("vector", S_dv, lambda e, o=b2acc[:, fcc:fcc + 1],
                         a=b2acc[:, fcc:fcc + 1], b=PS[2][0:128, 0:1]:
                         e.tensor_tensor(o, a, b, ALU.add))
            ff_done.append((S_xb.n, S_mm.n))
        kb.wait("tensor", S_ga, gam)
        kb.mm(S_mm, lambda e, o=PTR[:, 0:32], l=ffb1t[0:32, 0:128], r=iden_bf[0:32, 0:32]:
              e.transpose(o, l, r))
        kb.wait("vector", S_mm)
        kb.wait("vector", S_dv)
        kb.incop("vector", S_dv, lambda e, o=b1acol, a=b2acc[:, 0:32], b=PTR[:, 0:32]:
                 e.tensor_tensor(o, a, b, ALU.add))
        arena_bf.pop(); arena_f.pop(); arena_s.pop()

    if KPH >= 4:
        phase_l1prep()
    if KPH >= 5:
        phase_attn(1)
    if "d_y1" in dbg:
        kb.wait("sync", S_st)
        for t_ in range(NT128):
            kb.dma("sync", dbg["d_y1"][ts(t_, 128), :], Y1[ts(t_, 128), :], S_st)

    def phase_recur():
        ph_serial("recur")
        phase_barrier()
        arena_bf.push(); arena_f.push(); arena_s.push()
        wc_sb = arena_bf.alloc(8192)
        wx_sb = arena_bf.alloc(8192)
        ctxt = [arena_bf.alloc(4096) for _ in range(2)]
        Xa = arena_bf.alloc(4096)
        Xb = arena_bf.alloc(4096)
        ub = arena_bf.alloc(4096)
        tmp = arena_bf.alloc(512)
        xhst = arena_bf.alloc(1024)
        xhfm = arena_bf.alloc(4096)
        c_sb = arena_bf.alloc(4096)
        y1t = arena_f.alloc(4096)
        y2t = arena_f.alloc(1024)
        yh = arena_bf.alloc(1024)
        y2fmst = arena_bf.alloc(1024)
        stat = arena_f.alloc(18)
        for c in range(NCH):
            kb.dma("gpsimd", wc_sb[:, ts(c, 1024)], w_ctx[ts(c, 128), :], S_ga)
            kb.dma("gpsimd", wx_sb[:, ts(c, 1024)], w_x[ts(c, 128), :], S_ga)
        kb.wait("scalar", S_ga)
        for c in range(NCH):
            kb.incop("scalar", S_ev, lambda e, o=wx_sb[:, ts(c, 1024)], i=wx_sb[:, ts(c, 1024)]:
                     e.activation(o, i, AF.Identity, bias=0.0, scale=0.01))
        kb.wait("tensor", S_ev)
        for g5 in range(NT512):
            bi = g5 % 2
            kb.wait("sync", S_mm)
            kb.dma("sync", ctxt[bi],
                   CTXFM[g5 * 4: g5 * 4 + 4, :, :].rearrange("t p (c x) -> p c t x", x=128),
                   S_x2[bi])
            for tt in range(4):
                g = g5 * 4 + tt
                kb.wait("sync", S_dv)
                kb.dma("sync", y1t[:, ts(tt, 1024)], Y1[ts(g, 128), :], S_h2[tt % 2])
                kb.wait("vector", S_h2[tt % 2])
                rs_, nmr, _ = emit_ln(y1t[:, ts(tt, 1024)], stat)
                kb.wait("scalar", S_xb)
                kb.incop("scalar", S_ev, lambda e, o=xhst, i=y1t[:, ts(tt, 1024)], r=rs_, b=nmr:
                         e.activation(o, i, AF.Identity, bias=b[:, 0:1], scale=r[:, 0:1]))
                kb.wait("sync", S_ev)
                for c in range(NCH):
                    kb.dma_tr("sync", xhfm[:, c * 512 + tt * 128: c * 512 + (tt + 1) * 128],
                              xhst[:, ts(c, 128)], S_xb)
                kb.wait("sync", S_xb)
            kb.wait("scalar", S_xb)
            kb.wait("scalar", S_mm)
            kb.wait("scalar", S_dv)
            for c in range(NCH):
                kb.incop("scalar", S_ev, lambda e, o=Xa[:, ts(c, 512)], i=xhfm[:, ts(c, 512)],
                         r=gcols[:, 16 + c:17 + c], b=gcols[:, 24 + c:25 + c]:
                         e.activation(o, i, AF.Identity, bias=b[:, 0:1], scale=r[:, 0:1]))
            kb.wait("tensor", S_ev)
            kb.wait("tensor", S_x2[bi])
            for oc in range(NCH):
                pb2 = PS[oc % 2]
                kb.wait("tensor", S_ev)
                for c in range(NCH):
                    kb.mmq(lambda e, o=pb2[:, 0:512],
                           l=wc_sb[:, c * 1024 + oc * 128: c * 1024 + (oc + 1) * 128],
                           r=ctxt[bi][:, ts(c, 512)], st=(c == 0): e.matmul(o, l, r, start=st, stop=False))
                kb.mm(S_mm, lambda e, o=pb2[:, 0:512], l=binl_r[0:1, ts(oc, 128)],
                      r=onesb[0:1, 0:512]: e.matmul(o, l, r, start=False, stop=True))
                kb.wait("scalar", S_mm)
                kb.incop("scalar", S_ev, lambda e, o=c_sb[:, ts(oc, 512)], i=pb2[:, 0:512]:
                         e.activation(o, i, AF.Identity, bias=0.0, scale=1.0))
                kb.wait("tensor", S_ev)
                for c in range(NCH):
                    if c == NCH - 1:
                        kb.mm(S_mm, lambda e, o=pb2[:, 0:512],
                              l=wx_sb[:, c * 1024 + oc * 128: c * 1024 + (oc + 1) * 128],
                              r=Xa[:, ts(c, 512)]: e.matmul(o, l, r, start=False, stop=True))
                    else:
                        kb.mmq(lambda e, o=pb2[:, 0:512],
                               l=wx_sb[:, c * 1024 + oc * 128: c * 1024 + (oc + 1) * 128],
                               r=Xa[:, ts(c, 512)]: e.matmul(o, l, r, start=False, stop=False))
                kb.wait("scalar", S_mm)
                kb.incop("scalar", S_ev, lambda e, o=ub[:, ts(oc, 512)], i=pb2[:, 0:512]:
                         e.activation(o, i, AF.Tanh, bias=0.0, scale=1.0))
                kb.wait("vector", S_ev)
                kb.incop("vector", S_dv, lambda e, o=Xb[:, ts(oc, 512)], a=Xa[:, ts(oc, 512)],
                         b=ub[:, ts(oc, 512)]:
                         e.scalar_tensor_tensor(o, a, 0.99, b, ALU.mult, ALU.add))
            cur, prev = Xb, Xa
            for it in range(1, 8):
                itbar = S_dv.n
                kb.wait("tensor", S_dv, itbar)
                for oc in range(NCH):
                    pb2 = PS[oc % 2]
                    kb.wait("vector", S_ev)
                    kb.incop("vector", S_dv, lambda e, o=pb2[:, 0:512], i=c_sb[:, ts(oc, 512)]:
                             e.tensor_copy(o, i))
                    kb.wait("tensor", S_dv)
                    for c in range(NCH):
                        if c == NCH - 1:
                            kb.mm(S_mm, lambda e, o=pb2[:, 0:512],
                                  l=wx_sb[:, c * 1024 + oc * 128: c * 1024 + (oc + 1) * 128],
                                  r=cur[:, ts(c, 512)]: e.matmul(o, l, r, start=False, stop=True))
                        else:
                            kb.mmq(lambda e, o=pb2[:, 0:512],
                                   l=wx_sb[:, c * 1024 + oc * 128: c * 1024 + (oc + 1) * 128],
                                   r=cur[:, ts(c, 512)]: e.matmul(o, l, r, start=False, stop=False))
                    kb.wait("scalar", S_mm)
                    kb.incop("scalar", S_ev, lambda e, o=ub[:, ts(oc, 512)], i=pb2[:, 0:512]:
                             e.activation(o, i, AF.Tanh, bias=0.0, scale=1.0))
                    kb.wait("vector", S_ev)
                    kb.on("vector", lambda e, o=tmp, a=cur[:, ts(oc, 512)], b=prev[:, ts(oc, 512)]:
                          e.scalar_tensor_tensor(o, a, 2.1, b, ALU.mult, ALU.subtract))
                    kb.on("vector", lambda e: e.drain())
                    kb.incop("vector", S_dv, lambda e, o=prev[:, ts(oc, 512)], a=tmp,
                             b=ub[:, ts(oc, 512)]:
                             e.scalar_tensor_tensor(o, a, 0.9, b, ALU.mult, ALU.add))
                cur, prev = prev, cur
            kb.wait("sync", S_dv)
            for tt in range(4):
                g = g5 * 4 + tt
                for c in range(NCH):
                    kb.dma_tr("sync", xhst[:, ts(c, 128)],
                              cur[:, c * 512 + tt * 128: c * 512 + (tt + 1) * 128], S_xb)
                kb.wait("vector", S_xb)
                kb.incop("vector", S_dv, lambda e, o=y2t, a=xhst, b=y1t[:, ts(tt, 1024)]:
                         e.scalar_tensor_tensor(o, a, 0.01, b, ALU.mult, ALU.add))
                kb.wait("sync", S_dv)
                kb.dma("sync", Y2[ts(g, 128), :], y2t, S_st)
                rs_, nmr, _ = emit_ln(y2t, stat)
                kb.incop("scalar", S_ev, lambda e, o=yh, i=y2t, r=rs_, b=nmr:
                         e.activation(o, i, AF.Identity, bias=b[:, 0:1], scale=r[:, 0:1]))
                kb.wait("sync", S_ev)
                for c in range(NCH):
                    kb.dma_tr("sync", y2fmst[:, ts(c, 128)], yh[:, ts(c, 128)], S_xb)
                kb.wait("sync", S_xb)
                kb.dma("sync", Y2FM[g, :, :], y2fmst, S_xb)
        arena_bf.pop(); arena_f.pop(); arena_s.pop()

    if KPH >= 6:
        phase_recur()
    if "d_y2" in dbg:
        kb.wait("sync", S_st)
        kb.wait("sync", S_xb)
        for t_ in range(NT128):
            kb.dma("sync", dbg["d_y2"][ts(t_, 128), :], Y2[ts(t_, 128), :], S_st)

    def phase_ffn1():
        ph_serial("ffn1")
        phase_barrier()
        arena_bf.push(); arena_f.push(); arena_s.push()
        w1 = arena_bf.alloc(NCH * FF)
        y2fm = [arena_bf.alloc(4096) for _ in range(2)]
        hst = [arena_bf.alloc(512) for _ in range(2)]
        for c in range(NCH):
            kb.dma("gpsimd", w1[:, ts(c, FF)], FFW1S[c, :, :], S_ga)
        kb.wait("tensor", S_ga)
        for g5 in range(NT512):
            bi = g5 % 2
            kb.wait("sync", S_mm)
            kb.dma("sync", y2fm[bi],
                   Y2FM[g5 * 4: g5 * 4 + 4, :, :].rearrange("t p (c x) -> p c t x", x=128),
                   S_x2[bi])
            kb.wait("tensor", S_x2[bi])
            for fcc in range(FFC):
                pb2 = PS[fcc % 2]
                kb.wait("tensor", S_ev)
                for c in range(NCH):
                    if c == NCH - 1:
                        kb.mm(S_mm, lambda e, o=pb2[:, 0:512],
                              l=w1[:, c * FF + fcc * 128: c * FF + (fcc + 1) * 128],
                              r=y2fm[bi][:, ts(c, 512)], st=(c == 0): e.matmul(o, l, r, start=st, stop=True))
                    else:
                        kb.mmq(lambda e, o=pb2[:, 0:512],
                               l=w1[:, c * FF + fcc * 128: c * FF + (fcc + 1) * 128],
                               r=y2fm[bi][:, ts(c, 512)], st=(c == 0): e.matmul(o, l, r, start=st, stop=False))
                kb.wait("scalar", S_mm)
                hb = hst[fcc % 2]
                kb.wait("scalar", S_xb)
                kb.incop("scalar", S_ev, lambda e, o=hb, i=pb2[:, 0:512], b=b1acol[:, fcc:fcc + 1]:
                         e.activation(o, i, AF.Gelu, bias=b, scale=1.0))
                kb.wait("sync", S_ev)
                kb.dma("sync", HFM[g5, :, ts(fcc, 512)], hb, S_xb)
        kb.wait("sync", S_xb)
        arena_bf.pop(); arena_f.pop(); arena_s.pop()

    def phase_ffn2():
        ph_serial("ffn2")
        phase_barrier()
        arena_bf.push(); arena_f.push(); arena_s.push()
        w2 = arena_bf.alloc(FFC * D)
        hld = [arena_bf.alloc(512) for _ in range(2)]
        y2l = arena_f.alloc(1024)
        y3 = arena_f.alloc(1024)
        for fcc in range(FFC):
            kb.dma("gpsimd", w2[:, ts(fcc, D)], ff_w2[ts(fcc, 128), :], S_ga)
        kb.wait("tensor", S_ga)
        hf_done = []
        for g5 in range(NT512):
            for half in range(2):
                kb.wait("tensor", S_dv)
                for fcc in range(FFC):
                    hb = hld[fcc % 2]
                    if len(hf_done) >= 2:
                        kb.wait("sync", S_mm, hf_done[-2])
                    kb.dma("sync", hb, HFM[g5, :, ts(fcc, 512)], S_h2[fcc % 2])
                    kb.wait("tensor", S_h2[fcc % 2])
                    for tt in (2 * half, 2 * half + 1):
                        pi = 3 + 2 * (tt - 2 * half)
                        for n in range(2):
                            if tt == 2 * half + 1 and n == 1:
                                kb.mm(S_mm, lambda e, o=PS[pi + n][:, 0:512], l=hb[:, ts(tt, 128)],
                                      r=w2[:, fcc * D + n * 512: fcc * D + (n + 1) * 512],
                                      st=(fcc == 0): e.matmul(o, l, r, start=st, stop=False))
                            else:
                                kb.mmq(lambda e, o=PS[pi + n][:, 0:512], l=hb[:, ts(tt, 128)],
                                       r=w2[:, fcc * D + n * 512: fcc * D + (n + 1) * 512],
                                       st=(fcc == 0): e.matmul(o, l, r, start=st, stop=False))
                    hf_done.append(S_mm.n)
                for tt in (2 * half, 2 * half + 1):
                    g = g5 * 4 + tt
                    pi = 3 + 2 * (tt - 2 * half)
                    for n in range(2):
                        kb.mm(S_mm, lambda e, o=PS[pi + n][:, 0:512], l=onesb[0:1, 0:128],
                              r=ffb2_r[0:1, ts(n, 512)]: e.matmul(o, l, r, start=False, stop=True))
                    kb.wait("sync", S_dv)
                    kb.dma("sync", y2l, Y2[ts(g, 128), :], S_x2[tt % 2])
                    kb.wait("vector", S_x2[tt % 2])
                    kb.wait("vector", S_mm)
                    for n in range(2):
                        kb.incop("vector", S_dv, lambda e, o=y3[:, ts(n, 512)],
                                 a=PS[pi + n][:, 0:512], b=y2l[:, ts(n, 512)]:
                                 e.tensor_tensor(o, a, b, ALU.add))
                    kb.wait("sync", S_dv)
                    kb.dma("sync", S1[ts(g, 128), :], y3, S_st)
        arena_bf.pop(); arena_f.pop(); arena_s.pop()

    if KPH >= 7:
        phase_ffn1()
    if KPH >= 8:
        phase_ffn2()
    if "d_s1" in dbg:
        kb.wait("sync", S_st)
        for t_ in range(NT128):
            kb.dma("sync", dbg["d_s1"][ts(t_, 128), :], S1[ts(t_, 128), :], S_st)

    def phase_quantw():
        ph_serial("quantw")
        phase_barrier()
        arena_bf.push(); arena_f.push(); arena_s.push()
        wld = [arena_f.alloc(1024) for _ in range(2)]
        kwb = arena_bf.alloc(1024)
        trst = arena_bf.alloc(1024)
        trsd = arena_bf.alloc(2816)
        tmpa = arena_f.alloc(2816)
        tmpsgn = arena_bf.alloc(2816)
        rowm = arena_s.alloc(4)
        for (wsrc, wdst_d, ssb, nr) in ((w_gate, WQTG, swg, QIC), (w_up, WQTU, swu, QIC)):
            for ic in range(nr):
                bi2 = ic % 2
                kb.wait("sync", S_dv)
                kb.dma("sync", wld[bi2], wsrc[ts(ic, 128), :], S_w2[bi2])
                kb.wait("vector", S_w2[bi2])
                kb.wait("vector", S_ev)
                kb.on("vector", lambda e, o=rowm[:, 0:1], i=wld[bi2]:
                      e.tensor_reduce(o, i, AX.X, ALU.max, apply_absolute_value=True))
                kb.on("vector", lambda e: e.drain())
                kb.on("vector", lambda e, o=ssb[:, ic:ic + 1], i=rowm[:, 0:1]:
                      e.tensor_scalar(o, i, 1.0 / 127.0, 0.0, ALU.mult, ALU.add))
                kb.on("vector", lambda e: e.drain())
                kb.incop("vector", S_dv, lambda e, o=rowm[:, 1:2], i=ssb[:, ic:ic + 1]:
                         e.reciprocal(o, i))
                kb.wait("vector", S_xb)
                quant_chain(kwb, wld[bi2], rowm[:, 1:2], 1024, tmpa, tmpsgn)
                kb.wait("sync", S_dv)
                kb.wait("sync", S_st)
                for hc in range(NCH):
                    kb.dma_tr("sync", trst[:, ts(hc, 128)], kwb[:, ts(hc, 128)], S_xb)
                kb.wait("sync", S_xb)
                for hc in range(NCH):
                    kb.dma("sync", wdst_d[hc, :, ts(ic, 128)], trst[:, ts(hc, 128)], S_st)
        _wldd1 = arena_f.alloc(2816)
        wldd = [_wldd1, _wldd1]
        kwd = arena_bf.alloc(2816)
        for hc in range(NCH):
            bi2 = hc % 2
            kb.wait("sync", S_dv)
            kb.dma("sync", wldd[bi2], w_down[ts(hc, 128), :], S_w2[bi2])
            kb.wait("vector", S_w2[bi2])
            kb.wait("vector", S_ev)
            kb.on("vector", lambda e, o=rowm[:, 0:1], i=wldd[bi2]:
                  e.tensor_reduce(o, i, AX.X, ALU.max, apply_absolute_value=True))
            kb.on("vector", lambda e: e.drain())
            kb.on("vector", lambda e, o=swd[:, hc:hc + 1], i=rowm[:, 0:1]:
                  e.tensor_scalar(o, i, 1.0 / 127.0, 0.0, ALU.mult, ALU.add))
            kb.on("vector", lambda e: e.drain())
            kb.incop("vector", S_dv, lambda e, o=rowm[:, 1:2], i=swd[:, hc:hc + 1]:
                     e.reciprocal(o, i))
            kb.wait("vector", S_xb)
            quant_chain(kwd, wldd[bi2], rowm[:, 1:2], 2816, tmpa, tmpsgn)
            kb.wait("sync", S_dv)
            kb.wait("sync", S_st)
            for ic in range(QIC):
                kb.dma_tr("sync", trsd[:, ts(ic, 128)], kwd[:, ts(ic, 128)], S_xb)
            kb.wait("sync", S_xb)
            for ic in range(QIC):
                kb.dma("sync", WDT[ic, :, ts(hc, 128)], trsd[:, ts(ic, 128)], S_st)
        kb.wait("sync", S_st)
        kb.wait("sync", S_xb)
        arena_bf.pop(); arena_f.pop(); arena_s.pop()

    if KPH >= 9:
        phase_quantw()

    def phase_scale(ccout_t, clip_in, dst_off, sw_pairs):
        ph_serial("scale")
        phase_barrier()
        arena_f.push()
        ld = arena_f.alloc(16)
        kb.wait("sync", S_cc, S_cc.n)
        kb.dma("sync", ld[0:1, 0:8], ccout_t.ap(), S_ldw)
        kb.dma("sync", ld[0:1, 8:9], clip_in.unsqueeze(0), S_ldw)
        kb.wait("scalar", S_ldw)
        kb.incop("scalar", S_ev, lambda e, o=ld[0:1, 10:11], i=ld[0:1, 8:9]:
                 e.activation(o, i, AF.Sigmoid, bias=0.0, scale=1.0))
        kb.wait("vector", S_ev)
        kb.on("vector", lambda e, o=ld[0:1, 11:12], a=ld[0:1, 10:11], b=ld[0:1, 0:1]:
              e.tensor_tensor(o, a, b, ALU.mult))
        kb.on("vector", lambda e: e.drain())
        kb.incop("vector", S_dv, lambda e, o=ld[0:1, 12:13], i=ld[0:1, 11:12]:
                 e.tensor_scalar(o, i, 1.0 / 127.0, 0.0, ALU.mult, ALU.add))
        kb.wait("scalar", S_dv)
        kb.incop("scalar", S_ev, lambda e, o=ld[0:1, 13:14], i=ld[0:1, 12:13]:
                 e.activation(o, i, AF.Ln, bias=0.0, scale=1.0))
        kb.incop("scalar", S_ev, lambda e, o=ld[0:1, 13:14], i=ld[0:1, 13:14]:
                 e.activation(o, i, AF.Exp, bias=0.0, scale=-1.0))
        kb.wait("tensor", S_ev)
        kb.mm(S_mm, lambda e, o=PS[2][:, 0:2], l=ones[0:1, 0:128], r=ld[0:1, 12:14]:
              e.matmul(o, l, r, start=True, stop=True))
        kb.wait("scalar", S_mm)
        kb.incop("scalar", S_ev, lambda e, o=qscal[:, dst_off:dst_off + 2], i=PS[2][:, 0:2]:
                 e.activation(o, i, AF.Identity, bias=0.0, scale=1.0))
        kb.wait("vector", S_ev)
        for (dsts, srcs) in sw_pairs:
            kb.incop("vector", S_dv, lambda e, o=dsts, i=srcs, m=qscal[:, dst_off:dst_off + 1]:
                     e.tensor_scalar(o, i, m, 0.0, ALU.mult, ALU.add))
        arena_f.pop()

    if KPH >= 10:
        phase_scale(CCOUT, clip_x, 0, [(sgx, swg), (sux, swu)])

    def phase_gateup():
        ph_serial("gateup")
        phase_barrier()
        arena_bf.push(); arena_f.push(); arena_s.push()
        xl = arena_f.alloc(1024)
        kxt = arena_bf.alloc(1024)
        kxfm = arena_bf.alloc(4096)
        wstr = [arena_bf.alloc(2048) for _ in range(2)]
        tmpa = arena_f.alloc(1024)
        tmpsgn = arena_bf.alloc(1024)
        sgf = [arena_f.alloc(512) for _ in range(2)]
        hpf = arena_f.alloc(512)
        runm = arena_s.alloc(4)
        sil_done = []
        stt_done = []
        kb.on("gpsimd", lambda e: e.memset(runm[:, :], 0.0))
        kb.on_raw("gpsimd", lambda e: e.drain())
        kb.on_raw("gpsimd", lambda e: e.nop().then_inc(S_gp.h, 1)); S_gp.n += 1
        kb.wait("vector", S_gp)
        for g5 in range(NT512):
            kb.wait("sync", S_mm)
            for tt in range(4):
                g = g5 * 4 + tt
                kb.wait("sync", S_dv)
                kb.dma("sync", xl, XN1[ts(g, 128), :], S_x2[tt % 2])
                kb.wait("scalar", S_x2[tt % 2])
                kb.wait("vector", S_xb)
                quant_chain(kxt, xl, qscal[:, 1:2], 1024, tmpa, tmpsgn)
                kb.wait("sync", S_dv)
                for c in range(NCH):
                    kb.dma_tr("sync", kxfm[:, c * 512 + tt * 128: c * 512 + (tt + 1) * 128],
                              kxt[:, ts(c, 128)], S_xb)
                kb.wait("sync", S_xb)
            kb.wait("tensor", S_xb)
            for ic in range(QIC):
                bi2 = ic % 2
                if len(wstr_done) >= 2:
                    kb.wait("sync", S_mm, wstr_done[-2])
                kb.dma("sync", wstr[bi2][:, 0:1024],
                       WQTG[:, :, ts(ic, 128)].rearrange("c p x -> p c x"), S_w2[bi2])
                kb.dma("sync", wstr[bi2][:, 1024:2048],
                       WQTU[:, :, ts(ic, 128)].rearrange("c p x -> p c x"), S_w2[bi2])
                kb.wait("tensor", S_w2[bi2])
                pgb = PS[2 * (ic % 2)]
                pub = PS[2 * (ic % 2) + 1]
                if len(sil_done) >= 2:
                    kb.wait("tensor", S_ev, sil_done[-2])
                if len(stt_done) >= 2:
                    kb.wait("tensor", S_dv, stt_done[-2])
                for c in range(NCH):
                    kb.mmq(lambda e, o=pgb[:, 0:512], l=wstr[bi2][:, ts(c, 128)],
                           r=kxfm[:, ts(c, 512)], st=(c == 0), sp=(c == NCH - 1):
                           e.matmul(o, l, r, start=st, stop=sp))
                for c in range(NCH):
                    if c == NCH - 1:
                        kb.mm(S_mm, lambda e, o=pub[:, 0:512],
                              l=wstr[bi2][:, 1024 + c * 128: 1024 + (c + 1) * 128],
                              r=kxfm[:, ts(c, 512)], st=(c == 0): e.matmul(o, l, r, start=st, stop=True))
                    else:
                        kb.mmq(lambda e, o=pub[:, 0:512],
                               l=wstr[bi2][:, 1024 + c * 128: 1024 + (c + 1) * 128],
                               r=kxfm[:, ts(c, 512)], st=(c == 0): e.matmul(o, l, r, start=st, stop=False))
                wstr_done.append(S_mm.n)
                kb.wait("scalar", S_mm)
                if len(stt_done) >= 2:
                    kb.wait("scalar", S_dv, stt_done[-2])
                kb.incop("scalar", S_ev, lambda e, o=sgf[ic % 2], i=pgb[:, 0:512], r=sgx[:, ic:ic + 1]:
                         e.activation(o, i, AF.Silu, bias=0.0, scale=r))
                sil_done.append(S_ev.n)
                kb.wait("vector", S_ev)
                kb.wait("vector", S_st)
                kb.on("vector", lambda e, o=hpf, a=pub[:, 0:512], m=sux[:, ic:ic + 1], b=sgf[ic % 2]:
                      e.scalar_tensor_tensor(o, a, m, b, ALU.mult, ALU.mult))
                kb.on("vector", lambda e: e.drain())
                kb.on("vector", lambda e, o=runm[:, 1:2], i=hpf:
                      e.tensor_reduce(o, i, AX.X, ALU.max, apply_absolute_value=True))
                kb.on("vector", lambda e: e.drain())
                kb.incop("vector", S_dv, lambda e, o=runm[:, 0:1], a=runm[:, 0:1], b=runm[:, 1:2]:
                         e.tensor_tensor(o, a, b, ALU.max))
                stt_done.append(S_dv.n)
                kb.wait("sync", S_dv)
                kb.dma("sync", HPRE[g5, :, ts(ic, 512)], hpf, S_st)
        kb.wait("tensor", S_dv)
        kb.mm(S_mm, lambda e, o=PS[2][0:1, 0:128], l=runm[:, 0:1], r=iden_f32_sb:
              e.transpose(o, l, r))
        kb.wait("vector", S_mm)
        kb.on("vector", lambda e, o=runm[0:1, 2:3], i=PS[2][0:1, 0:128]:
              e.tensor_reduce(o, i, AX.X, ALU.max))
        kb.on("vector", lambda e: e.drain())
        kb.on("vector", lambda e, o=ccst[0:1, 0:8]: e.memset(o, 0.0))
        kb.on("vector", lambda e: e.drain())
        kb.incop("vector", S_dv, lambda e: e.tensor_copy(ccst[0:1, 0:1], runm[0:1, 2:3]))
        kb.wait("gpsimd", S_dv)
        kb.dma("gpsimd", CCIN2.ap(), ccst[0:1, 0:8], S_gc)
        kb.wait("gpsimd", S_gc)
        S_cc.n += 1
        if os.environ.get("KNOCC"):
            kb.dma("gpsimd", CCOUT2.ap(), CCIN2.ap(), S_gc)
            kb.on_raw("gpsimd", lambda e: e.nop().then_inc(S_cc.h, 1))
        else:
            kb.on_raw("gpsimd", lambda e: e.collective_compute(
                "AllReduce", ALU.max, replica_groups=[list(range(NCORES))],
                ins=[CCIN2.ap().opt()], outs=[CCOUT2.ap().opt()]).then_inc(S_cc.h, 1))
        arena_bf.pop(); arena_f.pop(); arena_s.pop()

    wstr_done = []
    if KPH >= 11:
        phase_gateup()
    if KPH >= 12:
        phase_scale(CCOUT2, clip_down, 2, [(sdx, swd)])

    def phase_down():
        ph_serial("down")
        phase_barrier()
        arena_bf.push(); arena_f.push(); arena_s.push()
        wdt_sb = arena_bf.alloc(QIC * 1024)
        hql = [arena_f.alloc(512) for _ in range(2)]
        hq = arena_bf.alloc(QIC * 512)
        tmpa = arena_f.alloc(512)
        tmpsgn = arena_bf.alloc(512)
        g1fm = arena_bf.alloc(4096)
        g1tok = arena_bf.alloc(1024)
        x1l = arena_f.alloc(1024)
        s1l = arena_f.alloc(1024)
        outt = arena_f.alloc(1024)
        kb.dma("sync", wdt_sb, WDT[:, :, :].rearrange("i p x -> p i x"), S_ldw)
        kb.wait("tensor", S_ldw)
        dsil = []
        for g5 in range(NT512):
            kb.wait("vector", S_mm)
            for ic in range(QIC):
                bi2 = ic % 2
                kb.wait("sync", S_dv)
                kb.dma("sync", hql[bi2], HPRE[g5, :, ts(ic, 512)], S_h2[bi2])
                kb.wait("scalar", S_h2[bi2])
                quant_chain(hq[:, ts(ic, 512)], hql[bi2], qscal[:, 3:4], 512, tmpa, tmpsgn)
            kb.wait("tensor", S_dv)
            for hc in range(NCH):
                pb2 = PS[hc % 2]
                if len(dsil) >= 2:
                    kb.wait("tensor", S_ev, dsil[-2])
                for ic in range(QIC):
                    if ic == QIC - 1:
                        kb.mm(S_mm, lambda e, o=pb2[:, 0:512],
                              l=wdt_sb[:, ic * 1024 + hc * 128: ic * 1024 + (hc + 1) * 128],
                              r=hq[:, ts(ic, 512)], st=(ic == 0): e.matmul(o, l, r, start=st, stop=True))
                    else:
                        kb.mmq(lambda e, o=pb2[:, 0:512],
                               l=wdt_sb[:, ic * 1024 + hc * 128: ic * 1024 + (hc + 1) * 128],
                               r=hq[:, ts(ic, 512)], st=(ic == 0): e.matmul(o, l, r, start=st, stop=False))
                kb.wait("scalar", S_mm)
                kb.wait("scalar", S_xb)
                kb.incop("scalar", S_ev, lambda e, o=g1fm[:, ts(hc, 512)], i=pb2[:, 0:512],
                         r=sdx[:, hc:hc + 1]:
                         e.activation(o, i, AF.Silu, bias=0.0, scale=r))
                dsil.append(S_ev.n)
            kb.wait("sync", S_ev)
            for tt in range(4):
                g = g5 * 4 + tt
                kb.wait("sync", S_dv)
                for hc in range(NCH):
                    kb.dma_tr("sync", g1tok[:, ts(hc, 128)],
                              g1fm[:, hc * 512 + tt * 128: hc * 512 + (tt + 1) * 128], S_xb)
                kb.dma("sync", x1l, X1[ts(g, 128), :], S_x2[tt % 2])
                kb.dma("sync", s1l, S1[ts(g, 128), :], S_x2[tt % 2])
                kb.wait("vector", S_xb)
                kb.wait("vector", S_x2[tt % 2])
                kb.wait("vector", S_st)
                kb.on("vector", lambda e, o=s1l, a=s1l, b=g1tok: e.tensor_tensor(o, a, b, ALU.mult))
                kb.on("vector", lambda e: e.drain())
                kb.incop("vector", S_dv, lambda e, o=outt, a=x1l, b=s1l:
                         e.tensor_tensor(o, a, b, ALU.add))
                kb.wait("sync", S_dv)
                kb.dma("sync", out[ts(g, 128), :], outt, S_st)
        arena_bf.pop(); arena_f.pop(); arena_s.pop()

    if KPH >= 13:
        phase_down()
    else:
        out_stub()

    if "d_aux" in dbg:
        kb.wait("sync", S_st)
        aux = dbg["d_aux"]
        kb.dma("sync", aux[0:128, 0:8], qscal[:, 0:8], S_st)
        kb.dma("sync", aux[0:128, 32:54], swg[:, 0:22], S_st)
        kb.dma("sync", aux[0:128, 64:86], sgx[:, 0:22], S_st)
        kb.dma("sync", aux[0:128, 96:104], swd[:, 0:8], S_st)
        kb.dma("sync", aux[0:128, 104:112], sdx[:, 0:8], S_st)
        for j in range(8):
            kb.dma("sync", aux[256 + j * 128: 384 + j * 128, :], HPRE[0, :, ts(j, 1024)], S_st)

    kb.wait("gpsimd", S_st)
    kb.wait("sync", S_st)

    with nc.Block() as block:
        @block.sync
        def _(e):
            kb.run("sync", e)

        @block.tensor
        def _(e):
            kb.run("tensor", e)

        @block.scalar
        def _(e):
            kb.run("scalar", e)

        @block.vector
        def _(e):
            kb.run("vector", e)

        @block.gpsimd
        def _(e):
            kb.run("gpsimd", e)

    stack.close()
    return nc


def kernel(**inputs):
    consts = build_consts()
    nc = build_nc(DEBUG)
    x = np.ascontiguousarray(inputs["L0_attn_x"], dtype=np.float32)
    base = {}
    for k, v in inputs.items():
        if k == "L0_attn_x":
            continue
        base[k] = np.ascontiguousarray(v, dtype=np.float32)
    base.update(consts)
    in_maps = []
    for c in range(NCORES):
        m = dict(base)
        m["x0"] = np.ascontiguousarray(x[2 * c:2 * c + 2].reshape(NTOK, D))
        in_maps.append(m)
    res = run_bass_kernel_spmd(nc, in_maps, core_ids=list(range(NCORES)))
    outv = np.concatenate([r["out"].reshape(SEQ_PER_CORE, T, D) for r in res.results], axis=0)
    if DEBUG:
        kernel.dbg = [{k: v for k, v in r.items() if k.startswith("d_")} for r in res.results]
    return outv
